# version 02
# baseline (speedup 1.0000x reference)
import numpy as np
import ml_dtypes
import concourse.bass as bass
import concourse.mybir as mybir
from concourse.bass_utils import run_bass_kernel_spmd

F32 = mybir.dt.float32
BF16 = mybir.dt.bfloat16
AF = mybir.ActivationFunctionType
ALU = mybir.AluOpType
AX = mybir.AxisListType
BF16NP = ml_dtypes.bfloat16

D = 1024
BATCH = 2
S = 8192
NCORE = 8
NRANK = 4
TOK = 2048
NT = 8
BLK = 128
NBLK = 16


ENGINES = ("pe", "act", "dve", "pool", "sp")
PARANOID = [False]


def R(buf, key="whole"):
    name = buf if isinstance(buf, str) else buf.name
    return (name, key)


class _Op:
    __slots__ = ("idx", "engine", "fn", "reads", "writes", "kind", "slot",
                 "hazard", "need_inc", "inc_no", "waits", "slot_idx", "tag")

    def __init__(self, idx, engine, fn, reads, writes, kind, slot, hazard, tag):
        self.idx = idx
        self.engine = engine
        self.fn = fn
        self.reads = tuple(reads)
        self.writes = tuple(writes)
        self.kind = kind
        self.slot = slot
        self.hazard = hazard
        self.need_inc = False
        self.inc_no = None
        self.slot_idx = None
        self.waits = {}
        self.tag = tag


class KB:
    def __init__(self, nc):
        self.nc = nc
        self.ops = []
        self.slot_counts = {}
        self.psum_names = set()

    def op(self, engine, fn, reads=(), writes=(), hazard=False, tag=""):
        o = _Op(len(self.ops), engine, fn, reads, writes, "eng", None, hazard, tag)
        self.ops.append(o)
        return o

    def dma(self, queue, slot, out_ap, in_ap, reads=(), writes=(), tag="",
            serialize=True, **kw):
        def fn(eng, _o=out_ap, _i=in_ap, _kw=kw):
            return eng.dma_start(out=_o, in_=_i, **_kw)
        if serialize:
            reads = tuple(reads) + (R("_slotchain_" + slot),)
            writes = tuple(writes) + (R("_slotchain_" + slot),)
        o = _Op(len(self.ops), queue, fn, reads, writes, "dma", slot, False, tag)
        o.slot_idx = self.slot_counts.get(slot, 0)
        self.slot_counts[slot] = o.slot_idx + 1
        self.ops.append(o)
        return o

    @staticmethod
    def _conflicts(store, region):
        name, key = region
        sub = store.get(name)
        if not sub:
            return []
        if key == "whole":
            return list(sub.items())
        out = []
        if key in sub:
            out.append((key, sub[key]))
        if "whole" in sub:
            out.append(("whole", sub["whole"]))
        return out

    def finalize(self, block, sem_of):
        ops = self.ops
        deps = [None] * len(ops)
        last_write = {}
        readers = {}
        for o in ops:
            mydeps = {}
            pwrites = tuple(rg for rg in o.reads if rg[0] in self.psum_names)
            owrites = tuple(o.writes) + pwrites
            for rg in o.reads:
                for _, w in self._conflicts(last_write, rg):
                    mydeps[w.idx] = w
            for wg in owrites:
                for _, w in self._conflicts(last_write, wg):
                    mydeps[w.idx] = w
                for _, rs in self._conflicts(readers, wg):
                    for rr in rs:
                        mydeps[rr.idx] = rr
            mydeps.pop(o.idx, None)
            kept = []
            for d_ in mydeps.values():
                if (not PARANOID[0] and d_.kind == "eng" and o.kind == "eng"
                        and d_.engine == o.engine == "pe"):
                    continue
                kept.append(d_)
                if d_.kind == "eng":
                    d_.need_inc = True
            deps[o.idx] = kept
            for rg in o.reads:
                readers.setdefault(rg[0], {}).setdefault(rg[1], []).append(o)
            for wg in owrites:
                name, key = wg
                lw = last_write.setdefault(name, {})
                if key == "whole":
                    lw.clear()
                    rd = readers.get(name)
                    if rd:
                        rd.clear()
                else:
                    rd = readers.get(name)
                    if rd:
                        rd.pop(key, None)
                        rd.pop("whole", None)
                lw[key] = o
        inc_counts = {e: 0 for e in ENGINES}
        for o in ops:
            if o.kind == "eng" and o.need_inc:
                inc_counts[o.engine] += 1
                o.inc_no = inc_counts[o.engine]

        def _semval(d_):
            if d_.kind == "dma":
                return ("slot:" + d_.slot, 16 * (d_.slot_idx + 1))
            return (d_.engine, d_.inc_no)

        know = {e: {} for e in ENGINES}
        opknow = {}
        for o in ops:
            k = know[o.engine]
            waits = {}
            for d_ in sorted(deps[o.idx], key=lambda x: x.idx):
                s, v = _semval(d_)
                if k.get(s, 0) >= v:
                    continue
                waits[s] = max(waits.get(s, 0), v)
                k[s] = v
                pk = opknow.get(d_.idx)
                if pk:
                    for s2, v2 in pk.items():
                        if k.get(s2, 0) < v2:
                            k[s2] = v2
            o.waits = waits
            if o.kind == "dma" or o.need_inc:
                pk = dict(k)
                if o.kind == "eng" and o.inc_no is not None:
                    pass
                opknow[o.idx] = pk

        eng_ops = {e: [o for o in ops if o.engine == e] for e in ENGINES}
        attr = {"pe": "tensor", "act": "scalar", "dve": "vector",
                "pool": "gpsimd", "sp": "sync"}

        def runner(e):
            def _run(eng):
                for o in eng_ops[e]:
                    for s in sorted(o.waits):
                        eng.wait_ge(sem_of(s), o.waits[s])
                    ins = o.fn(eng)
                    if o.kind == "dma":
                        ins.then_inc(sem_of("slot:" + o.slot), 16)
                    elif o.inc_no is not None:
                        ins.then_inc(sem_of(e), 1)
            return _run

        for e in ENGINES:
            getattr(block, attr[e])(runner(e))
        return inc_counts, dict(self.slot_counts)


class Prog:

    def __init__(self, name):
        self.nc = bass.Bass("TRN2", target_bir_lowering=False)
        self.name = name
        self.kb = KB(self.nc)
        self._stack = []
        self._sems = {}
        self.bufs = {}

    def inp(self, name, shape, dt=F32):
        return self.nc.dram_tensor(name, shape, dt, kind="ExternalInput").ap()

    def out(self, name, shape, dt=F32):
        return self.nc.dram_tensor(name, shape, dt, kind="ExternalOutput").ap()

    def dram(self, name, shape, dt=F32):
        return self.nc.dram_tensor(name, shape, dt).ap()

    def sbuf(self, name, shape, dt=F32):
        g = self.nc.sbuf_tensor(name, shape, dt)
        h = g.__enter__()
        self._stack.append(g)
        self.bufs[name] = h
        return h

    def psum(self, name, shape, dt=F32):
        g = self.nc.psum_tensor(name, shape, dt)
        h = g.__enter__()
        self._stack.append(g)
        self.bufs[name] = h
        self.kb.psum_names.add(name)
        return h

    def sshare(self, name, shape, dt=F32):
        if name not in self.bufs:
            return self.sbuf(name, list(shape), dt)
        return self.bufs[name]

    def pshare(self, shape, i=0, dt=F32):
        key = f"psS_{'x'.join(map(str, shape))}_{i}"
        if key not in self.bufs:
            return self.psum(key, list(shape), dt)
        return self.bufs[key]

    def _sem(self, name):
        if name not in self._sems:
            g = self.nc.semaphore(name.replace(":", "_"))
            h = g.__enter__()
            self._stack.append(g)
            self._sems[name] = h
        return self._sems[name]

    def finalize(self):
        needed = set(ENGINES)
        for o in self.kb.ops:
            if o.kind == "dma":
                needed.add("slot:" + o.slot)
        for n in sorted(needed):
            self._sem(n)
        bg = self.nc.Block()
        block = bg.__enter__()
        self._stack.append(bg)
        stats = self.kb.finalize(block, self._sem)
        while self._stack:
            self._stack.pop().__exit__(None, None, None)
        return stats



def e_mm(kb, out_ap, lhsT, rhs, reads, writes, start=True, stop=True,
         tile_position=None, tag=""):
    def fn(eng, _o=out_ap, _l=lhsT, _r=rhs, _s=start, _p=stop, _tp=tile_position):
        return eng.matmul(_o, _l, _r, start=_s, stop=_p, tile_position=_tp)
    return kb.op("pe", fn, reads=reads, writes=writes, tag=tag)


def e_act(kb, out_ap, in_ap, func, reads, writes, bias=None, scale=None,
          accum_out=None, hazard=False, tag=""):
    kw = {}
    if bias is not None:
        kw["bias"] = bias
    if scale is not None:
        kw["scale"] = scale
    if accum_out is not None:
        kw["accum_out"] = accum_out
        hazard = True

    def fn(eng, _o=out_ap, _i=in_ap, _f=func, _kw=kw):
        return eng.activation(_o, _i, _f, **_kw)
    return kb.op("act", fn, reads=reads, writes=writes, hazard=hazard, tag=tag)


def e_tt(kb, out_ap, in0, in1, op, reads, writes, engine="dve", tag=""):
    def fn(eng, _o=out_ap, _a=in0, _b=in1, _op=op):
        return eng.tensor_tensor(_o, _a, _b, _op)
    return kb.op(engine, fn, reads=reads, writes=writes, tag=tag)


def e_ts(kb, out_ap, in0, scalar1, op0, reads, writes, scalar2=None, op1=None,
         engine="dve", tag=""):
    def fn(eng, _o=out_ap, _i=in0, _s1=scalar1, _s2=scalar2, _o0=op0, _o1=op1):
        if _o1 is None:
            return eng.tensor_scalar(out=_o, in0=_i, scalar1=_s1, scalar2=None,
                                     op0=_o0)
        return eng.tensor_scalar(out=_o, in0=_i, scalar1=_s1, scalar2=_s2,
                                 op0=_o0, op1=_o1)
    return kb.op(engine, fn, reads=reads, writes=writes, tag=tag)


def e_stt(kb, out_ap, in0, scalar, in1, op0, op1, reads, writes, engine="dve",
          tag=""):
    def fn(eng, _o=out_ap, _i=in0, _s=scalar, _j=in1, _o0=op0, _o1=op1):
        return eng.scalar_tensor_tensor(out=_o, in0=_i, scalar=_s, in1=_j,
                                        op0=_o0, op1=_o1)
    return kb.op(engine, fn, reads=reads, writes=writes, tag=tag)


def e_copy(kb, out_ap, in_ap, reads, writes, engine="dve", tag=""):
    if engine == "act":
        def fn(eng, _o=out_ap, _i=in_ap):
            return eng.copy(_o, _i)
    else:
        def fn(eng, _o=out_ap, _i=in_ap):
            return eng.tensor_copy(_o, _i)
    return kb.op(engine, fn, reads=reads, writes=writes, tag=tag)


def e_red(kb, out_ap, in_ap, op, reads, writes, engine="dve", tag=""):
    def fn(eng, _o=out_ap, _i=in_ap, _op=op):
        return eng.tensor_reduce(_o, _i, axis=AX.X, op=_op)
    return kb.op(engine, fn, reads=reads, writes=writes, hazard=True, tag=tag)


def e_recip(kb, out_ap, in_ap, reads, writes, tag=""):
    def fn(eng, _o=out_ap, _i=in_ap):
        return eng.reciprocal(_o, _i)
    return kb.op("dve", fn, reads=reads, writes=writes, tag=tag)


def e_transpose(kb, out_ap, in_ap, ident, reads, writes, tag=""):
    def fn(eng, _o=out_ap, _i=in_ap, _id=ident):
        return eng.transpose(_o, _i, _id)
    return kb.op("pe", fn, reads=reads, writes=writes, tag=tag)


def e_memset(kb, ap, val, writes, engine="pool", tag=""):
    def fn(eng, _a=ap, _v=val):
        return eng.memset(_a, _v)
    return kb.op(engine, fn, reads=(), writes=writes, tag=tag)


def e_iota(kb, ap, pattern, writes, base=0, channel_multiplier=0, tag=""):
    def fn(eng, _a=ap, _p=pattern, _b=base, _c=channel_multiplier):
        return eng.iota(_a, _p, base=_b, channel_multiplier=_c,
                        allow_small_or_imprecise_dtypes=True)
    return kb.op("pool", fn, reads=(), writes=writes, tag=tag)



def _own_rows(r):
    idx = []
    for j in range(NBLK):
        blk = 4 * j + r
        idx.extend(range(blk * BLK, (blk + 1) * BLK))
    return np.array(idx)


_OWN = [_own_rows(r) for r in range(NRANK)]


def _gmaps():
    gmap = np.zeros((128, 4), np.float32)
    for p in range(128):
        gmap[p, p // 32] = 1.0
    return gmap, np.ascontiguousarray(gmap.T)


def _kron(n):
    k = np.zeros((n, n * 128), np.float32)
    for e in range(n):
        k[e, e * 128:(e + 1) * 128] = 1.0
    return k


def _cmask_for(r):
    tri = np.tril(np.ones((128, 128), np.float32)).T
    m = np.zeros((128, 4 * 128), np.float32)
    for rp in range(4):
        if rp < r:
            m[:, rp * 128:(rp + 1) * 128] = 1.0
        elif rp == r:
            m[:, rp * 128:(rp + 1) * 128] = tri
    return m


def _c(a):
    return np.ascontiguousarray(a)


_DBG = None


def _dbg_note(name, xcur):
    if _DBG is not None:
        _DBG.append((name, [np.array(x) for x in xcur]))


def kernel(**inputs):
    f = {k: np.asarray(v) for k, v in inputs.items()}
    x0 = f["L0_attn_a_x"]
    gmap, gmapT = _gmaps()
    kron8 = _kron(8)
    kron32 = _kron(32)
    cores = list(range(NCORE))

    def run(P, ins_list):
        res = run_bass_kernel_spmd(P.nc, ins_list, core_ids=cores)
        return res.results

    def percore(fn):
        return [fn(c // 4, c % 4) for c in cores]

    P = _get_prog("A0", build_A0)
    r0 = run(P, percore(lambda g, r: dict(xfm=_c(x0[g][_OWN[r]].T))))
    parts = [np.stack([np.asarray(r0[g * 4 + rr]["part"]) for rr in range(4)])
             for g in range(2)]
    P = _get_prog("A1", build_A1)
    r1 = run(P, percore(lambda g, r: dict(
        tfm=_c(np.asarray(r0[g * 4 + r]["tfm"])), parts=parts[g],
        gamma=f["L0_attn_a_gn_gamma"], beta=f["L0_attn_a_gn_beta"],
        wqkv=f["L0_attn_a_W_qkv"], bqkv=f["L0_attn_a_b_qkv"],
        gmap=gmap, gmapT=gmapT)))

    def gather_kvn(rr):
        kf, vf, nf = [], [], []
        for g in range(2):
            ks = [np.asarray(rr[g * 4 + j]["kfm"]) for j in range(4)]
            vs = [np.asarray(rr[g * 4 + j]["vtm"]) for j in range(4)]
            ns = [np.asarray(rr[g * 4 + j]["norms"]) for j in range(4)]
            kf.append(_c(np.concatenate(ks, axis=1)))
            vf.append(_c(np.concatenate(vs, axis=0)))
            nf.append(_c(np.concatenate(ns, axis=1)))
        return kf, vf, nf

    kf, vf, nf = gather_kvn(r1)
    P = _get_prog("ATTA", lambda: build_attn_mh("ATTA", True))
    xcur = [np.asarray(r0[c]["tfm"]) * 0 for c in cores]
    r2 = run(P, percore(lambda g, r: dict(
        xfm=_c(x0[g][_OWN[r]].T), qfm=np.asarray(r1[g * 4 + r]["qfm"]),
        kfull=kf[g], vfull=vf[g], norms4=nf[g],
        wo=f["L0_attn_a_W_o"], bo=f["L0_attn_a_b_o"],
        tres=_c(np.asarray(r0[g * 4 + r]["tfm"])))))
    xcur = [np.asarray(r2[c]["x1fm"]) for c in cores]
    _dbg_note("ATTA", xcur)
    P = _get_prog("FFN0", build_ffn0)
    ffn0_w = dict(gwT=_c(f["L0_ffn_gate_w"].T),
                  ebias=_c(f["L0_ffn_expert_bias"].reshape(1, 8)),
                  w1=f["L0_ffn_w1"], w2=f["L0_ffn_w2"], w3=f["L0_ffn_w3"],
                  kron8=kron8)
    r3 = run(P, [dict(xfm=xcur[c], **ffn0_w) for c in cores])
    xcur = [np.asarray(r3[c]["x2fm"]) for c in cores]
    _dbg_note("FFN0a", xcur)
    wqkv_b = _c(np.concatenate([f["L0_attn_b_Wq"].T, f["L0_attn_b_Wk"].T,
                                f["L0_attn_b_Wv"].T], axis=1))
    P = _get_prog("QKVP", lambda: build_qkv_plain("QKVP", False))
    r4 = run(P, [dict(xfm=xcur[c], wqkv=wqkv_b) for c in cores])
    kf, vf, nf = gather_kvn(r4)
    P = _get_prog("ATTB", lambda: build_attn_mh("ATTB", False))
    r5 = run(P, percore(lambda g, r: dict(
        xfm=xcur[g * 4 + r], qfm=np.asarray(r4[g * 4 + r]["qfm"]),
        kfull=kf[g], vfull=vf[g], norms4=nf[g],
        wo=_c(f["L0_attn_b_Wo"].T))))
    xcur = [np.asarray(r5[c]["x1fm"]) for c in cores]
    _dbg_note("ATTB", xcur)
    P = _get_prog("FFN0", build_ffn0)
    r6 = run(P, [dict(xfm=xcur[c], **ffn0_w) for c in cores])
    xcur = [np.asarray(r6[c]["x2fm"]) for c in cores]
    _dbg_note("FFN0b", xcur)
    wqkv_c = _c(np.concatenate([f["L1_attn_a_Wq"].T, f["L1_attn_a_Wk"].T,
                                f["L1_attn_a_Wv"].T], axis=1))
    P = _get_prog("QKVW", lambda: build_qkv_plain("QKVW", True))
    r7 = run(P, [dict(xfm=xcur[c], wqkv=wqkv_c) for c in cores])
    kf, vf, nf = gather_kvn(r7)
    P = _get_prog("ATTC", build_attn_causal)
    r8 = run(P, percore(lambda g, r: dict(
        xfm=xcur[g * 4 + r], qfm=np.asarray(r7[g * 4 + r]["qfm"]),
        kfull=kf[g], vfull=vf[g], norms4=nf[g], cmask=_cmask_for(r))))
    xcur = [np.asarray(r8[c]["x5fm"]) for c in cores]
    _dbg_note("ATTC", xcur)
    P = _get_prog("FFN1", build_ffn1)
    ffn1_w = dict(gwT=_c(f["L1_ffn_gate_weight"].T),
                  gbias=_c(f["L1_ffn_gate_bias"].reshape(1, 32)),
                  w_gate=f["L1_ffn_w_gate"], w_up=f["L1_ffn_w_up"],
                  w_down=f["L1_ffn_w_down"],
                  sh_gate=f["L1_ffn_shared_w_gate"],
                  sh_up=f["L1_ffn_shared_w_up"],
                  sh_down=f["L1_ffn_shared_w_down"],
                  an_gate=f["L1_ffn_anchor_w_gate"],
                  an_up=f["L1_ffn_anchor_w_up"],
                  an_down=f["L1_ffn_anchor_w_down"], kron32=kron32)
    r9 = run(P, [dict(xfm=xcur[c], **ffn1_w) for c in cores])
    xcur = [np.asarray(r9[c]["xofm"]) for c in cores]
    _dbg_note("FFN1a", xcur)
    P = _get_prog("SPREP", build_scanprep)
    r10 = run(P, [dict(xfm=xcur[c], eta_w=f["L1_attn_b_eta_w"],
                       eta_b=f["L1_attn_b_eta_b"],
                       alpha_w=f["L1_attn_b_alpha_w"],
                       alpha_b=f["L1_attn_b_alpha_b"],
                       gate_w=f["L1_attn_b_gate_w"],
                       gate_b=f["L1_attn_b_gate_b"]) for c in cores])
    NCH = 128
    w1T = _c(f["L1_attn_b_vg_w1"].T)
    w2T = _c(f["L1_attn_b_vg_w2"].T)
    scan_ins = []
    for g in range(2):
        t6g = np.zeros((D, S), dtype=BF16NP)
        kmr = np.zeros((NCH, D), np.float32)
        sca = np.zeros((1, 5 * NCH), np.float32)
        for r in range(4):
            rr = r10[g * 4 + r]
            t6g[:, _OWN[r]] = np.asarray(rr["t6fm"])
            km = np.asarray(rr["kmean"])
            sc = np.asarray(rr["sca"])
            for j in range(32):
                ct = (4 * (j // 2) + r) * 2 + (j % 2)
                kmr[ct] = km[j]
                for row in range(5):
                    sca[0, row * NCH + ct] = sc[0, row * 32 + j]
        scan_ins.append(dict(t6g=_c(t6g), kmr=kmr, kmcT=_c(kmr.T), sca=sca,
                             m0T=_c(f["L1_attn_b_M_init"].T), w1T=w1T,
                             w2T=w2T))
    P = _get_prog("SCAN", build_scan)
    r11 = run(P, [scan_ins[c // 4] for c in cores])
    outsg = [np.asarray(r11[g * 4]["outsg"]) for g in range(2)]
    P = _get_prog("VG", build_vg)
    r12 = run(P, percore(lambda g, r: dict(
        xfm=xcur[g * 4 + r], outso=_c(outsg[g][:, _OWN[r]]),
        w1T=w1T, w2T=w2T)))
    xcur = [np.asarray(r12[c]["x7fm"]) for c in cores]
    _dbg_note("VG", xcur)
    P = _get_prog("FFN1", build_ffn1)
    r13 = run(P, [dict(xfm=xcur[c], **ffn1_w) for c in cores])
    out = np.zeros((BATCH, S, D), np.float32)
    for c in cores:
        g, r = c // 4, c % 4
        out[g][_OWN[r]] = np.asarray(r13[c]["xofm"]).T
    return out



def emit_ln(P, X, outs, ntok, cst, pref, eps=1e-5, xkey=None, chunk_cb=None,
            first_rot=False):
    kb = P.kb
    nc2 = 512
    nchunk = ntok // nc2
    sq = [P.sshare(f"lnS_sq{i}", [128, nc2]) for i in range(2)]
    mrow = P.sshare(f"lnS_m{ntok}", [1, ntok])
    ssrow = P.sshare(f"lnS_ss{ntok}", [1, ntok])
    rr = P.sshare(f"lnS_rr{ntok}", [1, ntok])
    mr2 = P.sshare(f"lnS_mr{ntok}", [1, ntok])
    ps_red = P.pshare([1, nc2], 0)
    ps_bc = P.pshare([128, nc2], 0)
    ps_bc2 = P.pshare([128, nc2], 1)
    ones = cst["ones_col_f32"]
    ones_row = cst["ones_row_f32"]
    xr = (lambda k: R(X[k])) if xkey is None else (lambda k: R(X[k], xkey))

    for c in range(nchunk):
        sl = slice(c * nc2, (c + 1) * nc2)
        for k in range(NT):
            e_mm(kb, ps_red[:, :], ones[:, :], X[k][:, sl],
                 reads=[R(ones), xr(k)], writes=[R(ps_red)],
                 start=(k == 0), stop=(k == NT - 1), tag=f"{pref}mean{c}")
        e_copy(kb, mrow[:, sl], ps_red[:, :], reads=[R(ps_red)],
               writes=[R(mrow, c)], engine="act")
        for k in range(NT):
            b = sq[k % 2]
            e_act(kb, b[:, :], X[k][:, sl], AF.Square,
                  reads=[xr(k)], writes=[R(b)], tag=f"{pref}sq")
            e_mm(kb, ps_red[:, :], ones[:, :], b[:, :],
                 reads=[R(ones), R(b)], writes=[R(ps_red)],
                 start=(k == 0), stop=(k == NT - 1), tag=f"{pref}ssmm{c}")
        e_copy(kb, ssrow[:, sl], ps_red[:, :], reads=[R(ps_red)],
               writes=[R(ssrow, c)], engine="act")
    e_ts(kb, mrow[:, :], mrow[:, :], 1.0 / D, ALU.mult,
         reads=[R(mrow, c) for c in range(nchunk)], writes=[R(mrow)])
    e_ts(kb, ssrow[:, :], ssrow[:, :], 1.0 / D, ALU.mult,
         reads=[R(ssrow, c) for c in range(nchunk)], writes=[R(ssrow)])
    e_tt(kb, rr[:, :], mrow[:, :], mrow[:, :], ALU.mult,
         reads=[R(mrow)], writes=[R(rr)])
    e_tt(kb, rr[:, :], ssrow[:, :], rr[:, :], ALU.subtract,
         reads=[R(ssrow), R(rr)], writes=[R(rr)])
    e_ts(kb, rr[:, :], rr[:, :], float(eps), ALU.add,
         reads=[R(rr)], writes=[R(rr)])
    e_act(kb, rr[:, :], rr[:, :], AF.Sqrt,
          reads=[R(rr)], writes=[R(rr)], tag=f"{pref}sqrt")
    e_recip(kb, rr[:, :], rr[:, :], reads=[R(rr)], writes=[R(rr)])
    e_tt(kb, mr2[:, :], mrow[:, :], rr[:, :], ALU.mult,
         reads=[R(mrow), R(rr)], writes=[R(mr2)])
    for c in range(nchunk):
        sl = slice(c * nc2, (c + 1) * nc2)
        e_mm(kb, ps_bc[:, :], ones_row[:, :], rr[:, sl],
             reads=[R(ones_row), R(rr)], writes=[R(ps_bc)], tag=f"{pref}bc")
        e_mm(kb, ps_bc2[:, :], ones_row[:, :], mr2[:, sl],
             reads=[R(ones_row), R(mr2)], writes=[R(ps_bc2)], tag=f"{pref}bc2")
        for k in range(NT):
            first = outs[0][k]
            fsl = slice(0, nc2) if first_rot else sl
            fkey = "whole" if first_rot else c
            e_tt(kb, first[:, fsl], X[k][:, sl], ps_bc[:, :], ALU.mult,
                 reads=[xr(k), R(ps_bc)], writes=[R(first, fkey)])
            e_tt(kb, first[:, fsl], first[:, fsl], ps_bc2[:, :], ALU.subtract,
                 reads=[R(first, fkey), R(ps_bc2)], writes=[R(first, fkey)])
            for extra in outs[1:]:
                e_copy(kb, extra[k][:, sl], first[:, fsl],
                       reads=[R(first, fkey)], writes=[R(extra[k], c)])
        if chunk_cb is not None:
            chunk_cb(c, outs[0])


def emit_consts(P):
    cst = {}
    cst["ones_col_f32"] = P.sbuf("c_ones_col", [128, 1], F32)
    cst["ones_row_f32"] = P.sbuf("c_ones_row", [1, 128], F32)
    cst["ones_col_bf16"] = P.sbuf("c_ones_colb", [128, 1], BF16)
    e_memset(P.kb, cst["ones_col_f32"][:, :], 1.0, writes=[R(cst["ones_col_f32"])])
    e_memset(P.kb, cst["ones_row_f32"][:, :], 1.0, writes=[R(cst["ones_row_f32"])])
    e_memset(P.kb, cst["ones_col_bf16"][:, :], 1.0, writes=[R(cst["ones_col_bf16"])])
    return cst



def build_A0():
    P = Prog("A0")
    xin = P.inp("xfm", [D, TOK], F32)
    tout = P.out("tfm", [D, TOK], F32)
    pout = P.out("part", [128, 16], F32)
    kb = P.kb
    X = [P.sbuf(f"x{k}", [128, TOK], F32) for k in range(NT)]
    T = [P.sbuf(f"t{k}", [128, TOK], F32) for k in range(NT)]
    part = P.sbuf("partb", [128, 16], F32)
    sqs = P.sbuf("sqs", [128, TOK], F32)
    cst = emit_consts(P)
    for k in range(NT):
        kb.dma("sp", "ldx", X[k][:, :], xin[k * 128:(k + 1) * 128, :],
               reads=[R("dram_xfm")], writes=[R(X[k])])
    emit_ln(P, X, [T], TOK, cst, "ln0")
    for k in range(NT):
        e_red(kb, part[:, 2 * k:2 * k + 1], T[k][:, :], ALU.add,
              reads=[R(T[k])], writes=[R(part, 2 * k)])
        e_act(kb, sqs[:, :], T[k][:, :], AF.Square,
              reads=[R(T[k])], writes=[R(sqs)], tag="partsq")
        e_red(kb, part[:, 2 * k + 1:2 * k + 2], sqs[:, :], ALU.add,
              reads=[R(sqs)], writes=[R(part, 2 * k + 1)])
    for k in range(NT):
        kb.dma("sp", "stt", tout[k * 128:(k + 1) * 128, :], T[k][:, :],
               reads=[R(T[k])], writes=[R("dram_tfm", k)])
    kb.dma("sp", "stp", pout, part[:, :],
           reads=[R(part, 2 * k) for k in range(NT)] +
                 [R(part, 2 * k + 1) for k in range(NT)],
           writes=[R("dram_part")])
    P.finalize()
    return P


_PROG_CACHE = {}


def _get_prog(name, builder):
    if name not in _PROG_CACHE:
        _PROG_CACHE[name] = builder()
    return _PROG_CACHE[name]



def load_weight_fm(P, name, dram_ap, din, dout, queue="pool", row0=0):
    nk = din // 128
    w = P.sbuf("s_" + name, [128, nk * dout], BF16)
    src3 = dram_ap[row0:row0 + din, :].rearrange("(k p) h -> p k h", p=128)
    dst3 = w[:, :].rearrange("p (k h) -> p k h", h=dout)
    P.kb.dma(queue, name, dst3, src3, reads=[R("dram_" + name)],
             writes=[R(w)], tag=name, serialize=False)
    return w, (lambda k, _w=w, _d=dout: _w[:, k * _d:(k + 1) * _d])


def load_cols(P, name, dram_1d, n, queue="sp"):
    v = P.sbuf("s_" + name, [128, n // 128], F32)
    for k in range(n // 128):
        P.kb.dma(queue, name, v[:, k:k + 1], dram_1d[k * 128:(k + 1) * 128],
                 reads=[R("dram_" + name)], writes=[R(v, k)], tag=name)
    return v



def emit_linear_fm(P, pref, Xb, wacc, wname, OUT, n_out, ntok, bias_cols=None,
                   bias_off=0, w_off=0, nk=NT, out_key=None, x_key=None,
                   accum_into=None):
    kb = P.kb
    nc2 = 512
    nchunk = ntok // nc2
    no = n_out // 128
    ps = [P.pshare([128, nc2], i) for i in range(2)]
    for o in range(no):
        for c in range(nchunk):
            p = ps[(o * nchunk + c) % 2]
            sl = slice(c * nc2, (c + 1) * nc2)
            for k in range(nk):
                e_mm(kb, p[:, :], wacc(k)[:, w_off + o * 128:w_off + (o + 1) * 128],
                     Xb[k][:, sl],
                     reads=[R(wname, k), R(Xb[k]) if x_key is None else R(Xb[k], x_key)],
                     writes=[R(p)],
                     start=(k == 0), stop=(k == nk - 1), tag=f"{pref}mm")
            dst = OUT[o]
            okey = (o, c) if out_key is None else out_key
            if accum_into is not None:
                if bias_cols is not None:
                    e_stt(kb, dst[:, sl], p[:, :], bias_cols[:, bias_off + o:bias_off + o + 1],
                          dst[:, sl], ALU.add, ALU.add,
                          reads=[R(p), R(bias_cols), R(dst, okey)],
                          writes=[R(dst, okey)], tag=f"{pref}acc")
                else:
                    e_tt(kb, dst[:, sl], dst[:, sl], p[:, :], ALU.add,
                         reads=[R(p), R(dst, okey)], writes=[R(dst, okey)],
                         tag=f"{pref}acc")
            elif bias_cols is not None:
                e_act(kb, dst[:, sl], p[:, :], AF.Identity,
                      bias=bias_cols[:, bias_off + o:bias_off + o + 1],
                      reads=[R(p), R(bias_cols)], writes=[R(dst, okey)],
                      tag=f"{pref}ev")
            else:
                e_copy(kb, dst[:, sl], p[:, :], reads=[R(p)],
                       writes=[R(dst, okey)], engine="act", tag=f"{pref}ev")


def emit_linear_tm(P, pref, Xb, wacc, wname, OUT_tm, n_out, ntok, w_off=0,
                   nk=NT, bias_row_bc=None, out_dram=None, dram_name=None,
                   store_queue="sp"):
    kb = P.kb
    nts = ntok // 128
    no2 = n_out // 512
    ps = [P.pshare([128, 512], i) for i in range(2)]
    for ts in range(nts):
        for oc in range(no2):
            p = ps[(ts * no2 + oc) % 2]
            for k in range(nk):
                e_mm(kb, p[:, :], Xb[k][:, ts * 128:(ts + 1) * 128],
                     wacc(k)[:, w_off + oc * 512:w_off + (oc + 1) * 512],
                     reads=[R(Xb[k]), R(wname, k)], writes=[R(p)],
                     start=(k == 0), stop=(k == nk - 1), tag=f"{pref}mm")
            dst = OUT_tm[ts]
            sl = slice(oc * 512, (oc + 1) * 512)
            if bias_row_bc is not None:
                e_tt(kb, dst[:, sl], p[:, :], bias_row_bc[:, sl], ALU.add,
                     reads=[R(p), R(bias_row_bc)], writes=[R(dst, oc)],
                     tag=f"{pref}ev")
            else:
                e_copy(kb, dst[:, sl], p[:, :], reads=[R(p)],
                       writes=[R(dst, oc)], engine="act", tag=f"{pref}ev")
        if out_dram is not None:
            kb.dma(store_queue, f"{pref}_st", out_dram[ts * 128:(ts + 1) * 128, :],
                   dst[:, :], reads=[R(dst, oc2) for oc2 in range(no2)],
                   writes=[R(dram_name, ts)], tag=f"{pref}st")


def emit_headnorm_max(P, pref, Qb, n_heads, ntok, cst, norms_out, norms_row,
                      sqs):
    kb = P.kb
    nc2 = 512
    nchunk = ntok // nc2
    ps_n = P.pshare([1, nc2], 0)
    nr = P.sbuf(f"{pref}_nr", [1, ntok], F32)
    ones = cst["ones_col_f32"]
    for h in range(n_heads):
        kt = h // 2
        base = (h % 2) * 64
        e_act(kb, sqs[:, :], Qb[kt][:, :], AF.Square,
              reads=[R(Qb[kt])], writes=[R(sqs)], tag=f"{pref}sq")
        for c in range(nchunk):
            e_mm(kb, ps_n[:, :], ones[base:base + 64, :],
                 sqs[base:base + 64, c * nc2:(c + 1) * nc2],
                 reads=[R(ones), R(sqs)], writes=[R(ps_n)], tag=f"{pref}nmm")
            e_copy(kb, nr[:, c * nc2:(c + 1) * nc2], ps_n[:, :],
                   reads=[R(ps_n)], writes=[R(nr, c)], engine="act")
        col = norms_row * n_heads + h
        e_red(kb, norms_out[0:1, col:col + 1], nr[:, :], ALU.max,
              reads=[R(nr, c) for c in range(nchunk)],
              writes=[R(norms_out, (norms_row, h))], tag=f"{pref}max")


def build_A1(dbg=False):
    P = Prog("A1")
    kb = P.kb
    tin = P.inp("tfm", [D, TOK], F32)
    parts = P.inp("parts", [NRANK, 128, 16], F32)
    gma = P.inp("gamma", [D], F32)
    bta = P.inp("beta", [D], F32)
    wqkv = P.inp("wqkv", [D, 3 * D], F32)
    bqkv = P.inp("bqkv", [3 * D], F32)
    gmap = P.inp("gmap", [128, 4], F32)
    gmapT = P.inp("gmapT", [4, 128], F32)
    qout = P.out("qfm", [D, TOK], BF16)
    kout = P.out("kfm", [D, TOK], BF16)
    vout = P.out("vtm", [TOK, D], BF16)
    nout = P.out("norms", [1, 32], F32)

    cst = emit_consts(P)
    T = [P.sbuf(f"t{k}", [128, TOK], F32) for k in range(2)]
    H = [P.sbuf(f"h{k}", [128, TOK], BF16) for k in range(NT)]
    def Tk(k):
        return T[k % 2]

    def load_T(k):
        kb.dma("sp", f"ldt{k % 2}", Tk(k)[:, :], tin[k * 128:(k + 1) * 128, :],
               reads=[R("dram_tfm")], writes=[R(Tk(k))])
    pt = [P.sbuf(f"pt{i}", [128, 16], F32) for i in range(NRANK)]
    sgm = P.sbuf("sgmap", [128, 4], F32)
    sgmT = P.sbuf("sgmapT", [4, 128], F32)
    kb.dma("sp", "ldgm", sgm[:, :], gmap, reads=[R("dram_gmap")], writes=[R(sgm)])
    kb.dma("sp", "ldgmT", sgmT[:, :], gmapT, reads=[R("dram_gmapT")], writes=[R(sgmT)])
    for i in range(NRANK):
        kb.dma("sp", f"ldpt{i}", pt[i][:, :], parts[i],
               reads=[R("dram_parts")], writes=[R(pt[i])])
    for i in range(1, NRANK):
        e_tt(kb, pt[0][:, :], pt[0][:, :], pt[i][:, :], ALU.add,
             reads=[R(pt[0]), R(pt[i])], writes=[R(pt[0])])
    gam = load_cols(P, "gam", gma, D)
    bet = load_cols(P, "bet", bta, D)
    ps_g = P.psum("ps_g", [4, 16], F32)
    ps_h = P.psum("ps_h", [128, 16], F32)
    gv = P.sbuf("gv", [4, 16], F32)
    mr = P.sbuf("mrst", [4, 16], F32)
    gcol = P.sbuf("gcol", [128, 16], F32)
    NTOT = float(32 * S)
    for k in range(NT):
        e_mm(kb, ps_g[:, 2 * k:2 * k + 2], sgm[:, :], pt[0][:, 2 * k:2 * k + 2],
             reads=[R(sgm), R(pt[0])], writes=[R(ps_g)])
    e_copy(kb, gv[:, :], ps_g[:, :], reads=[R(ps_g)],
           writes=[R(gv)], engine="act")
    e_ts(kb, gv[:, :], gv[:, :], 1.0 / NTOT, ALU.mult, reads=[R(gv)],
         writes=[R(gv)])
    e_copy(kb, mr[:, 0:8], gv[:, 0:16:2], reads=[R(gv)], writes=[R(mr, "m")])
    e_tt(kb, mr[:, 8:16], mr[:, 0:8], mr[:, 0:8], ALU.mult,
         reads=[R(mr, "m")], writes=[R(mr, "v")])
    e_tt(kb, mr[:, 8:16], gv[:, 1:16:2], mr[:, 8:16], ALU.subtract,
         reads=[R(gv), R(mr, "v")], writes=[R(mr, "v")])
    e_ts(kb, mr[:, 8:16], mr[:, 8:16], 1e-6, ALU.add,
         reads=[R(mr, "v")], writes=[R(mr, "v")])
    e_act(kb, mr[:, 8:16], mr[:, 8:16], AF.Sqrt, reads=[R(mr, "v")],
          writes=[R(mr, "v")], tag="gnsqrt")
    e_recip(kb, mr[:, 8:16], mr[:, 8:16], reads=[R(mr, "v")], writes=[R(mr, "v")])
    e_mm(kb, ps_h[:, :], sgmT[:, :], mr[:, :],
         reads=[R(sgmT), R(mr, "m"), R(mr, "v")], writes=[R(ps_h)])
    e_copy(kb, gcol[:, :], ps_h[:, :], reads=[R(ps_h)], writes=[R(gcol)],
           engine="act")
    AB = P.sbuf("ABc", [128, 16], F32)
    for k in range(NT):
        load_T(k)
        e_tt(kb, AB[:, 2 * k:2 * k + 1], gam[:, k:k + 1], gcol[:, 8 + k:9 + k],
             ALU.mult, reads=[R(gam), R(gcol)], writes=[R(AB, 2 * k)])
        e_tt(kb, AB[:, 2 * k + 1:2 * k + 2], gcol[:, k:k + 1],
             AB[:, 2 * k:2 * k + 1], ALU.mult,
             reads=[R(gcol), R(AB, 2 * k)], writes=[R(AB, 2 * k + 1)])
        e_tt(kb, AB[:, 2 * k + 1:2 * k + 2], bet[:, k:k + 1],
             AB[:, 2 * k + 1:2 * k + 2], ALU.subtract,
             reads=[R(bet), R(AB, 2 * k + 1)], writes=[R(AB, 2 * k + 1)])
        e_ts(kb, H[k][:, :], Tk(k)[:, :], AB[:, 2 * k:2 * k + 1], ALU.mult,
             scalar2=AB[:, 2 * k + 1:2 * k + 2], op1=ALU.add,
             reads=[R(Tk(k)), R(AB, 2 * k), R(AB, 2 * k + 1)],
             writes=[R(H[k])], tag="hmk")
    wq, wqa = load_weight_fm(P, "wqkv", wqkv, D, 3 * D)
    bq = load_cols(P, "bq", bqkv, 3 * D)
    Q = [P.sbuf(f"q{k}", [128, TOK], BF16) for k in range(NT)]
    Kf = [P.sbuf(f"kf{k}", [128, TOK], BF16) for k in range(NT)]
    emit_linear_fm(P, "q", H, wqa, "wqkv", Q, D, TOK, bias_cols=bq, bias_off=0,
                   w_off=0)
    emit_linear_fm(P, "k", H, wqa, "wqkv", Kf, D, TOK, bias_cols=bq, bias_off=8,
                   w_off=D)
    bvb = P.sbuf("bvb", [128, D], F32)
    ps_bv = P.pshare([128, 512], 0)
    brow = P.sbuf("brow", [1, D], F32)
    kb.dma("sp", "ldbv", brow[:, :], bqkv[2 * D:3 * D],
           reads=[R("dram_bqkv")], writes=[R(brow)])
    for c in range(2):
        e_mm(kb, ps_bv[:, :], cst["ones_row_f32"][:, :],
             brow[:, c * 512:(c + 1) * 512],
             reads=[R(cst["ones_row_f32"]), R(brow)], writes=[R(ps_bv)])
        e_copy(kb, bvb[:, c * 512:(c + 1) * 512], ps_bv[:, :],
               reads=[R(ps_bv)], writes=[R(bvb, c)], engine="act")
    Vt2 = [P.sbuf(f"vt{t}", [128, D], BF16) for t in range(2)]
    Vt = [Vt2[t % 2] for t in range(TOK // 128)]
    emit_linear_tm(P, "v", H, wqa, "wqkv", Vt, D, TOK, w_off=2 * D,
                   bias_row_bc=bvb, out_dram=vout, dram_name="dram_vtm")
    norms = P.sbuf("normsb", [1, 32], F32)
    sqs = P.sbuf("sqs", [128, TOK], F32)
    emit_headnorm_max(P, "nq", Q, 16, TOK, cst, norms, 0, sqs)
    emit_headnorm_max(P, "nk", Kf, 16, TOK, cst, norms, 1, sqs)
    kb.dma("sp", "stn", nout, norms[:, :],
           reads=[R(norms, (r_, h)) for r_ in range(2) for h in range(16)],
           writes=[R("dram_norms")])
    if dbg:
        ptdb = P.out("ptdb", [128, 16], F32)
        gvdb = P.out("gvdb", [4, 16], F32)
        mrdb = P.out("mrdb", [4, 16], F32)
        kb.dma("sp", "stpt", ptdb, pt[0][:, :], reads=[R(pt[0])], writes=[R("d_pt")])
        kb.dma("sp", "stgv", gvdb, gv[:, :], reads=[R(gv)], writes=[R("d_gv")])
        kb.dma("sp", "stmr", mrdb, mr[:, :], reads=[R(mr, "m"), R(mr, "v")], writes=[R("d_mr")])
        hdb = P.out("hdb", [D, TOK], BF16)
        gdb = P.out("gdb", [128, 16], F32)
        adb = P.out("adb", [128, 16], F32)
        for k in range(NT):
            kb.dma("sp", "sthd", hdb[k * 128:(k + 1) * 128, :], H[k][:, :],
                   reads=[R(H[k])], writes=[R("dram_hdb", k)])
        kb.dma("sp", "stgd", gdb, gcol[:, :], reads=[R(gcol)], writes=[R("d_gdb")])
        kb.dma("sp", "stad", adb, AB[:, :],
               reads=[R(AB, j) for j in range(16)], writes=[R("d_adb")])
    for k in range(NT):
        kb.dma("sp", "stq", qout[k * 128:(k + 1) * 128, :], Q[k][:, :],
               reads=[R(Q[k], (o, c)) for o in [k] for c in range(4)],
               writes=[R("dram_qfm", k)])
        kb.dma("sp", "stk", kout[k * 128:(k + 1) * 128, :], Kf[k][:, :],
               reads=[R(Kf[k], (o, c)) for o in [k] for c in range(4)],
               writes=[R("dram_kfm", k)])
    P.finalize()
    return P



def build_attn_mh(name, with_bias):
    QK_SCALE = 1.0 / 8.0
    P = Prog(name)
    kb = P.kb
    xin = P.inp("xfm", [D, TOK], F32)
    trin = P.inp("tres", [D, TOK], F32) if with_bias else None
    qin = P.inp("qfm", [D, TOK], BF16)
    kin = P.inp("kfull", [D, S], BF16)
    vin = P.inp("vfull", [S, D], BF16)
    nin = P.inp("norms4", [1, 4 * 32], F32)
    woin = P.inp("wo", [D, D], F32)
    boin = P.inp("bo", [D], F32) if with_bias else None
    xout = P.out("x1fm", [D, TOK], F32)

    cst = emit_consts(P)
    X = [P.sbuf(f"x{k}", [128, TOK], F32) for k in range(NT)]
    Q = [P.sbuf(f"q{k}", [128, TOK], BF16) for k in range(NT)]
    O = [P.sbuf(f"o{k}", [128, TOK], BF16) for k in range(NT)]
    for k in range(NT):
        kb.dma("sp", "ldx", X[k][:, :], xin[k * 128:(k + 1) * 128, :],
               reads=[R("dram_xfm")], writes=[R(X[k])])
        kb.dma("sp", "ldq", Q[k][:, :], qin[k * 128:(k + 1) * 128, :],
               reads=[R("dram_qfm")], writes=[R(Q[k])])
    nrow = P.sbuf("nrow", [1, 128], F32)
    nmax = P.sbuf("nmax", [1, 32], F32)
    crow = P.sbuf("crow", [1, 16], F32)
    ccol = P.sbuf("ccol", [128, 16], F32)
    kb.dma("sp", "ldn", nrow[:, :], nin, reads=[R("dram_n")], writes=[R(nrow)])
    def fmax(out, a, b, reads, writes):
        def fn(eng, _o=out, _a=a, _b=b):
            return eng.tensor_max(_o, _a, _b)
        return kb.op("dve", fn, reads=reads, writes=writes)
    fmax(nmax[:, :], nrow[:, 0:32], nrow[:, 32:64], [R(nrow)], [R(nmax)])
    fmax(nmax[:, :], nmax[:, :], nrow[:, 64:96], [R(nmax), R(nrow)], [R(nmax)])
    fmax(nmax[:, :], nmax[:, :], nrow[:, 96:128], [R(nmax), R(nrow)], [R(nmax)])
    e_act(kb, nmax[:, :], nmax[:, :], AF.Sqrt, reads=[R(nmax)], writes=[R(nmax)],
          tag="csqrt")
    e_tt(kb, crow[:, :], nmax[:, 0:16], nmax[:, 16:32], ALU.mult,
         reads=[R(nmax)], writes=[R(crow)])
    e_ts(kb, crow[:, :], crow[:, :], -QK_SCALE, ALU.mult,
         reads=[R(crow)], writes=[R(crow)])
    ps_cc = P.pshare([128, 512], 0)
    for h in range(16):
        e_mm(kb, ps_cc[:, 0:1], cst["ones_row_f32"][:, :], crow[:, h:h + 1],
             reads=[R(cst["ones_row_f32"]), R(crow)], writes=[R(ps_cc)],
             tag="ccbc")
        e_copy(kb, ccol[:, h:h + 1], ps_cc[:, 0:1], reads=[R(ps_cc)],
               writes=[R(ccol, h)], engine="act")
    NKT = S // 128
    kpair = [P.sbuf(f"kp{i}", [128, S], BF16) for i in range(1)]
    vpair = [P.sbuf(f"vp{i}", [128, NKT * 130], BF16) for i in range(2)]
    pt_ = [P.sbuf(f"pt{i}", [128, 512], BF16) for i in range(4)]
    ps_st = [P.pshare([128, 512], i) for i in range(2)]
    ps_av = [P.psum(f"psav{i}", [65, 512], F32) for i in range(4)]
    rrow = P.sbuf("rrow", [1, 512], F32)
    rbc = P.sbuf("rbc", [64, 512], F32)
    ps_rb = P.psum("ps_rb", [64, 512], F32)
    v4 = vin.rearrange("(kt p) d -> p kt d", p=128)
    for i in range(2):
        e_memset(kb, vpair[i][:, :].rearrange("p (kt c) -> p kt c", c=130)[:, :, 64:65],
                 1.0, writes=[R(vpair[i], "ones")])
        e_memset(kb, vpair[i][:, :].rearrange("p (kt c) -> p kt c", c=130)[:, :, 129:130],
                 1.0, writes=[R(vpair[i], "ones2")])
    for hp in range(8):
        kbuf = kpair[0]
        vbuf = vpair[hp % 2]
        kb.dma("act", f"ldk{hp % 2}", kbuf[:, :], kin[hp * 128:(hp + 1) * 128, :],
               reads=[R("dram_k")], writes=[R(kbuf)], tag="ldk", serialize=False)
        vdst = vbuf[:, :].rearrange("p (kt c) -> p kt c", c=130)
        kb.dma("act", f"ldv{hp % 2}", vdst[:, :, 0:64],
               v4[:, :, hp * 128:hp * 128 + 64],
               reads=[R("dram_v")], writes=[R(vbuf, "h0")], tag="ldv", serialize=False)
        kb.dma("act", f"ldv{hp % 2}b", vdst[:, :, 65:129],
               v4[:, :, hp * 128 + 64:hp * 128 + 128],
               reads=[R("dram_v")], writes=[R(vbuf, "h1")], tag="ldv", serialize=False)
        for qg in range(2):
            for kt in range(NKT):
                for h in range(2):
                    for qb in range(2):
                        pi = h * 2 + qb
                        pst = ps_st[(kt * 4 + pi) % 2]
                        qsl = slice(qg * 1024 + qb * 512, qg * 1024 + (qb + 1) * 512)
                        e_mm(kb, pst[:, :],
                             kbuf[h * 64:(h + 1) * 64, kt * 128:(kt + 1) * 128],
                             Q[hp][h * 64:(h + 1) * 64, qsl],
                             reads=[R(kbuf), R(Q[hp])], writes=[R(pst)],
                             tile_position=(64 * h, 0), tag="st")
                        e_act(kb, pt_[pi][:, :], pst[:, :], AF.Exp,
                              bias=ccol[:, 2 * hp + h:2 * hp + h + 1],
                              scale=QK_SCALE,
                              reads=[R(pst), R(ccol, 2 * hp + h)],
                              writes=[R(pt_[pi])], tag="exp")
                        e_mm(kb, ps_av[pi][:, :],
                             vbuf[:, kt * 130 + 65 * h:kt * 130 + 65 * h + 65],
                             pt_[pi][:, :],
                             reads=[R(vbuf), R(vbuf, "h0" if h == 0 else "h1"),
                                    R(vbuf, "ones" if h == 0 else "ones2"),
                                    R(pt_[pi])],
                             writes=[R(ps_av[pi])],
                             start=(kt == 0), stop=(kt == NKT - 1), tag="av")
            for h in range(2):
                for qb in range(2):
                    pi = h * 2 + qb
                    qsl = slice(qg * 1024 + qb * 512, qg * 1024 + (qb + 1) * 512)
                    e_recip(kb, rrow[:, :], ps_av[pi][64:65, :],
                            reads=[R(ps_av[pi])], writes=[R(rrow)])
                    e_mm(kb, ps_rb[:, :], cst["ones_row_f32"][:, 0:64], rrow[:, :],
                         reads=[R(cst["ones_row_f32"]), R(rrow)],
                         writes=[R(ps_rb)], tag="rbc")
                    e_copy(kb, rbc[:, :], ps_rb[:, :], reads=[R(ps_rb)],
                           writes=[R(rbc)], engine="act")
                    e_tt(kb, O[hp][h * 64:(h + 1) * 64, qsl], ps_av[pi][0:64, :],
                         rbc[:, :], ALU.mult,
                         reads=[R(ps_av[pi]), R(rbc)],
                         writes=[R(O[hp], (h, qsl.start))], tag="oev")
    wo, woa = load_weight_fm(P, "wo", woin, D, D)
    if with_bias:
        bo = load_cols(P, "bo", boin, D)
    else:
        bo = None
    emit_linear_fm(P, "woP", O, woa, "wo", X, D, TOK, bias_cols=bo, bias_off=0,
                   w_off=0, accum_into=True,
                   x_key=None)
    if with_bias:
        TR = P.sbuf("tres_t", [128, 512], F32)
        for k in range(NT):
            for hh in range(4):
                kb.dma("sp", "ldtr", TR[:, :],
                       trin[k * 128:(k + 1) * 128, hh * 512:(hh + 1) * 512],
                       reads=[R("dram_tr")], writes=[R(TR)])
                e_tt(kb, X[k][:, hh * 512:(hh + 1) * 512],
                     X[k][:, hh * 512:(hh + 1) * 512], TR[:, :], ALU.add,
                     reads=[R(X[k]), R(TR)], writes=[R(X[k])], tag="innerres")
    for k in range(NT):
        kb.dma("sp", "stx", xout[k * 128:(k + 1) * 128, :], X[k][:, :],
               reads=[R(X[k])], writes=[R("dram_x1", k)])
    P.finalize()
    return P



def build_ffn0():
    NE, HID = 8, 2816
    NHK = HID // 128
    HCS = [768, 768, 768, 512]
    HCMAX = 768
    P = Prog("FFN0")
    kb = P.kb
    xin = P.inp("xfm", [D, TOK], F32)
    gwTin = P.inp("gwT", [D, NE], F32)
    ebin = P.inp("ebias", [1, NE], F32)
    w1in = P.inp("w1", [NE, D, HID], F32)
    w2in = P.inp("w2", [NE, HID, D], F32)
    w3in = P.inp("w3", [NE, D, HID], F32)
    kronin = P.inp("kron8", [NE, NE * 128], F32)
    xout = P.out("x2fm", [D, TOK], F32)

    cst = emit_consts(P)
    HTOK = 1024
    X = [P.sbuf(f"x{k}", [128, HTOK], F32) for k in range(NT)]
    T = [P.sbuf(f"t{k}", [128, HTOK], BF16) for k in range(NT)]
    Tfc = [P.sbuf(f"tfc{k}", [128, 512], F32) for k in range(NT)]
    lgT = P.sbuf("lgT", [NE, HTOK], F32)
    Ts = [P.sbuf(f"ts{k}", [128, HTOK], BF16) for k in range(NT)]
    w13 = P.sbuf("w13", [128, NT * HCMAX], BF16)
    sg = P.sbuf("sgb", [128, (HCMAX // 128) * HTOK], F32)
    hb = P.sbuf("hb", [128, (HCMAX // 128) * HTOK], BF16)
    w2c = P.sbuf("w2c", [128, (HCMAX // 128) * D], BF16)
    gwT = P.sbuf("s_gwT", [128, NT * NE], F32)
    ebrow = P.sbuf("ebrow", [1, NE], F32)
    ebbc = P.sbuf("ebbc", [128, NE], F32)
    cw = P.sbuf("cw", [NE, HTOK], F32)
    kron = P.sbuf("s_kron", [NE, NE * 128], F32)
    ident = P.sbuf("ident", [128, 128], F32)
    siota = P.sbuf("siota", [128, 128], F32)
    lg = P.sbuf("lg", [128, NE], F32)
    sc2 = P.sbuf("sc2", [128, NE], F32)
    eq = P.sbuf("eqb", [128, NE], F32)
    mcol = P.sbuf("mcol", [128, 1], F32)
    srow = P.sbuf("srow", [128, 1], F32)
    ps_l = P.pshare([128, 512], 0)
    ps_l2 = P.pshare([128, 512], 1)
    ps_t = P.psum("ps_t", [128, 128], F32)

    e_iota(kb, siota[:, :], [[1, 128]], writes=[R(siota)], channel_multiplier=-1)
    e_ts(kb, ident[:, :], siota[:, :], 0.0, ALU.is_equal, reads=[R(siota)],
         writes=[R(ident)], engine="pool")
    kb.dma("sp", "ldeb", ebrow[:, :], ebin, reads=[R("d_eb")], writes=[R(ebrow)])
    kb.dma("sp", "ldkron", kron[:, :], kronin, reads=[R("d_kron")], writes=[R(kron)])
    for k in range(NT):
        kb.dma("pool", "ldgw", gwT[:, k * NE:(k + 1) * NE],
               gwTin[k * 128:(k + 1) * 128, :], reads=[R("d_gwT")],
               writes=[R(gwT, k)])
    e_mm(kb, ps_l[:, 0:NE], cst["ones_row_f32"][:, :], ebrow[:, :],
         reads=[R(cst["ones_row_f32"]), R(ebrow)], writes=[R(ps_l)], tag="ebbc")
    e_copy(kb, ebbc[:, :], ps_l[:, 0:NE], reads=[R(ps_l)], writes=[R(ebbc)],
           engine="act")

    for half in range(2):
        hsl = slice(half * HTOK, (half + 1) * HTOK)
        for k in range(NT):
            kb.dma("sp", f"ldx{k % 2}", X[k][:, :],
                   xin[k * 128:(k + 1) * 128, hsl],
                   reads=[R("dram_xfm")], writes=[R(X[k])])
        def rt_cb(c, rot):
            psr = ps_l2
            for k in range(NT):
                e_mm(kb, psr[0:NE, 0:512], gwT[:, k * NE:(k + 1) * NE],
                     rot[k][:, 0:512],
                     reads=[R(gwT, k), R(rot[k])], writes=[R(psr)],
                     start=(k == 0), stop=(k == NT - 1), tag="rtmm")
            e_copy(kb, lgT[:, c * 512:(c + 1) * 512], psr[0:NE, 0:512],
                   reads=[R(psr)], writes=[R(lgT, c)], engine="act")
        emit_ln(P, X, [Tfc, T], HTOK, cst, f"ln{half}", chunk_cb=rt_cb,
                first_rot=True)
        for ts_ in range(HTOK // 128):
            e_transpose(kb, ps_t[:, 0:NE], lgT[:, ts_ * 128:(ts_ + 1) * 128],
                        ident[0:NE, 0:NE],
                        reads=[R(lgT, ts_ // 4), R(ident)], writes=[R(ps_t)])
            e_copy(kb, lg[:, :], ps_t[:, 0:NE],
                   reads=[R(ps_t)], writes=[R(lg)], engine="act")
            e_red(kb, mcol[:, :], lg[:, :], ALU.max, reads=[R(lg)],
                  writes=[R(mcol)])
            e_ts(kb, mcol[:, :], mcol[:, :], -1.0, ALU.mult, reads=[R(mcol)],
                 writes=[R(mcol)])
            e_act(kb, lg[:, :], lg[:, :], AF.Exp, bias=mcol[:, :],
                  accum_out=srow[:, :], reads=[R(lg), R(mcol)],
                  writes=[R(lg), R(srow)], tag="rtexp")
            e_recip(kb, srow[:, :], srow[:, :], reads=[R(srow)], writes=[R(srow)])
            e_ts(kb, lg[:, :], lg[:, :], srow[:, :], ALU.mult,
                 reads=[R(lg), R(srow)], writes=[R(lg)])
            e_tt(kb, sc2[:, :], lg[:, :], ebbc[:, :], ALU.add,
                 reads=[R(lg), R(ebbc)], writes=[R(sc2)])
            e_red(kb, mcol[:, :], sc2[:, :], ALU.max, reads=[R(sc2)],
                  writes=[R(mcol)])
            e_ts(kb, eq[:, :], sc2[:, :], mcol[:, :], ALU.is_equal,
                 reads=[R(sc2), R(mcol)], writes=[R(eq)])
            e_stt(kb, sc2[:, :], eq[:, :], -1e30, sc2[:, :], ALU.mult, ALU.add,
                  reads=[R(eq), R(sc2)], writes=[R(sc2)])
            e_red(kb, mcol[:, :], sc2[:, :], ALU.max, reads=[R(sc2)],
                  writes=[R(mcol)])
            e_ts(kb, sc2[:, :], sc2[:, :], mcol[:, :], ALU.is_equal,
                 reads=[R(sc2), R(mcol)], writes=[R(sc2)])
            e_tt(kb, eq[:, :], eq[:, :], sc2[:, :], ALU.add,
                 reads=[R(eq), R(sc2)], writes=[R(eq)])
            e_tt(kb, eq[:, :], eq[:, :], lg[:, :], ALU.mult,
                 reads=[R(eq), R(lg)], writes=[R(eq)])
            e_transpose(kb, ps_t[0:NE, 0:128], eq[:, :], ident[:, :],
                        reads=[R(eq), R(ident)], writes=[R(ps_t)])
            e_copy(kb, cw[:, ts_ * 128:(ts_ + 1) * 128], ps_t[0:NE, 0:128],
                   reads=[R(ps_t)], writes=[R(cw, ts_)], engine="act")
        for e in range(NE):
            for c2 in range(2):
                e_mm(kb, ps_l2[:, :], kron[:, e * 128:(e + 1) * 128],
                     cw[:, c2 * 512:(c2 + 1) * 512],
                     reads=[R(kron)] + [R(cw, t2) for t2 in range(4 * c2, 4 * c2 + 4)],
                     writes=[R(ps_l2)], tag="cwbc")
                for k in range(NT):
                    e_tt(kb, Ts[k][:, c2 * 512:(c2 + 1) * 512], T[k][:, c2 * 512:(c2 + 1) * 512],
                         ps_l2[:, :], ALU.mult,
                         reads=[R(T[k]), R(ps_l2)], writes=[R(Ts[k], c2)],
                         tag="tscale")
            hc_off = 0
            for hc, HC in enumerate(HCS):
                kb.dma("pool", "w13",
                       w13[:, :].rearrange("p (k h) -> p k h", h=HCMAX)[:, :, 0:HC],
                       w1in[e, :, hc_off:hc_off + HC].rearrange("(k p) h -> p k h", p=128),
                       reads=[R("d_w1")], writes=[R("w13")], tag="ldw1",
                       serialize=False)
                for ho in range(HC // 128):
                    for c2 in range(2):
                        p = ps_l if (ho + c2) % 2 == 0 else ps_l2
                        for k in range(NT):
                            e_mm(kb, p[:, :], w13[:, k * HCMAX + ho * 128:k * HCMAX + (ho + 1) * 128],
                                 Ts[k][:, c2 * 512:(c2 + 1) * 512],
                                 reads=[R("w13", k), R(Ts[k], c2)], writes=[R(p)],
                                 start=(k == 0), stop=(k == NT - 1), tag="gmm")
                        e_act(kb, sg[:, ho * HTOK + c2 * 512:ho * HTOK + (c2 + 1) * 512],
                              p[:, :], AF.Silu, reads=[R(p)],
                              writes=[R("sgb", (ho, c2))], tag="silu")
                kb.dma("pool", "w13",
                       w13[:, :].rearrange("p (k h) -> p k h", h=HCMAX)[:, :, 0:HC],
                       w3in[e, :, hc_off:hc_off + HC].rearrange("(k p) h -> p k h", p=128),
                       reads=[R("d_w3")], writes=[R("w13")], tag="ldw3",
                       serialize=False)
                for ho in range(HC // 128):
                    for c2 in range(2):
                        p = ps_l if (ho + c2) % 2 == 0 else ps_l2
                        for k in range(NT):
                            e_mm(kb, p[:, :], w13[:, k * HCMAX + ho * 128:k * HCMAX + (ho + 1) * 128],
                                 Ts[k][:, c2 * 512:(c2 + 1) * 512],
                                 reads=[R("w13", k), R(Ts[k], c2)], writes=[R(p)],
                                 start=(k == 0), stop=(k == NT - 1), tag="umm")
                        e_tt(kb, hb[:, ho * HTOK + c2 * 512:ho * HTOK + (c2 + 1) * 512],
                             sg[:, ho * HTOK + c2 * 512:ho * HTOK + (c2 + 1) * 512],
                             p[:, :], ALU.mult,
                             reads=[R(p), R("sgb", (ho, c2))],
                             writes=[R("hb", (ho, c2))], tag="hmul")
                kb.dma("pool", "w2c",
                       w2c[:, :].rearrange("p (k h) -> p k h", h=D)[:, 0:HC // 128, :],
                       w2in[e, hc_off:hc_off + HC, :].rearrange("(k p) h -> p k h", p=128),
                       reads=[R("d_w2")], writes=[R("w2c")], tag="ldw2",
                       serialize=False)
                for o in range(NT):
                    for c2 in range(2):
                        p = ps_l if (o + c2) % 2 == 0 else ps_l2
                        for hk in range(HC // 128):
                            e_mm(kb, p[:, :], w2c[:, hk * D + o * 128:hk * D + (o + 1) * 128],
                                 hb[:, hk * HTOK + c2 * 512:hk * HTOK + (c2 + 1) * 512],
                                 reads=[R("w2c", hk), R("hb", (hk, c2))],
                                 writes=[R(p)],
                                 start=(hk == 0), stop=(hk == HC // 128 - 1),
                                 tag="dmm")
                        e_tt(kb, X[o][:, c2 * 512:(c2 + 1) * 512],
                             X[o][:, c2 * 512:(c2 + 1) * 512], p[:, :], ALU.add,
                             reads=[R(X[o]), R(p)], writes=[R(X[o])], tag="xacc")
                hc_off += HC
        for k in range(NT):
            kb.dma("sp", f"stx{k % 2}", xout[k * 128:(k + 1) * 128, hsl],
                   X[k][:, :], reads=[R(X[k])], writes=[R("dram_xo", (half, k))])
    P.finalize()
    return P



def build_qkv_plain(name, wide):
    P = Prog(name)
    kb = P.kb
    xin = P.inp("xfm", [D, TOK], F32)
    wqkv = P.inp("wqkv", [D, 3 * D], F32)
    qout = P.out("qfm", [D, TOK], BF16)
    kout = P.out("kfm", [D, TOK], BF16)
    vout = P.out("vtm", [TOK, D], BF16)
    nout = P.out("norms", [1, 32], F32)
    cst = emit_consts(P)
    H = [P.sbuf(f"h{k}", [128, TOK], BF16) for k in range(NT)]
    emit_ln_dram(P, xin, [H], TOK, cst, "ln")
    wq, wqa = load_weight_fm(P, "wqkv", wqkv, D, 3 * D)
    Q = [P.sbuf(f"q{k}", [128, TOK], BF16) for k in range(NT)]
    Kf = [P.sbuf(f"kf{k}", [128, TOK], BF16) for k in range(NT)]
    emit_linear_fm(P, "q", H, wqa, "wqkv", Q, D, TOK, w_off=0)
    emit_linear_fm(P, "k", H, wqa, "wqkv", Kf, D, TOK, w_off=D)
    Vt2 = [P.sbuf(f"vt{t}", [128, D], BF16) for t in range(2)]
    Vt = [Vt2[t % 2] for t in range(TOK // 128)]
    emit_linear_tm(P, "v", H, wqa, "wqkv", Vt, D, TOK, w_off=2 * D,
                   out_dram=vout, dram_name="dram_vtm")
    norms = P.sbuf("normsb", [1, 32], F32)
    sqs = P.sbuf("sqs", [128, TOK], F32)
    if not wide:
        emit_headnorm_max(P, "nq", Q, 16, TOK, cst, norms, 0, sqs)
        emit_headnorm_max(P, "nk", Kf, 16, TOK, cst, norms, 1, sqs)
        nreads = [R(norms, (r_, h)) for r_ in range(2) for h in range(16)]
    else:
        ps_n = P.pshare([1, 512], 0)
        nr = P.sbuf("nr_w", [1, TOK], F32)
        ones = cst["ones_col_f32"]
        for row, QQ in ((0, Q), (1, Kf)):
            for c in range(TOK // 512):
                for k in range(NT):
                    e_act(kb, sqs[:, 0:512], QQ[k][:, c * 512:(c + 1) * 512],
                          AF.Square, reads=[R(QQ[k])], writes=[R(sqs)],
                          tag="wsq")
                    e_mm(kb, ps_n[:, :], ones[:, :], sqs[:, 0:512],
                         reads=[R(ones), R(sqs)], writes=[R(ps_n)],
                         start=(k == 0), stop=(k == NT - 1), tag="wnmm")
                e_copy(kb, nr[:, c * 512:(c + 1) * 512], ps_n[:, :],
                       reads=[R(ps_n)], writes=[R(nr, c)], engine="act")
            col = row * 16
            e_red(kb, norms[0:1, col:col + 1], nr[:, :], ALU.max,
                  reads=[R(nr, c) for c in range(TOK // 512)],
                  writes=[R(norms, (row, 0))], tag="wmax")
        nreads = [R(norms, (r_, 0)) for r_ in range(2)]
    kb.dma("sp", "stn", nout, norms[:, :], reads=nreads, writes=[R("d_no")])
    for k in range(NT):
        kb.dma("sp", "stq", qout[k * 128:(k + 1) * 128, :], Q[k][:, :],
               reads=[R(Q[k], (kk2, c)) for kk2 in [k] for c in range(4)],
               writes=[R("dram_qfm", k)])
        kb.dma("sp", "stk", kout[k * 128:(k + 1) * 128, :], Kf[k][:, :],
               reads=[R(Kf[k], (kk2, c)) for kk2 in [k] for c in range(4)],
               writes=[R("dram_kfm", k)])
    P.finalize()
    return P



def build_attn_causal():
    QK_SCALE = 1.0 / 32.0
    P = Prog("ATTC")
    kb = P.kb
    xin = P.inp("xfm", [D, TOK], F32)
    qin = P.inp("qfm", [D, TOK], BF16)
    kin = P.inp("kfull", [D, S], BF16)
    vin = P.inp("vfull", [S, D], BF16)
    nin = P.inp("norms4", [1, 4 * 32], F32)
    mkin = P.inp("cmask", [128, 4 * 128], F32)
    xout = P.out("x5fm", [D, TOK], F32)
    cst = emit_consts(P)
    X = [P.sbuf(f"x{k}", [128, TOK], F32) for k in range(NT)]
    Q = [P.sbuf(f"q{k}", [128, TOK], BF16) for k in range(NT)]
    for k in range(NT):
        kb.dma("sp", f"ldx{k % 2}", X[k][:, :], xin[k * 128:(k + 1) * 128, :],
               reads=[R("dram_xfm")], writes=[R(X[k])])
        kb.dma("sp", "ldq", Q[k][:, :], qin[k * 128:(k + 1) * 128, :],
               reads=[R("dram_qfm")], writes=[R(Q[k])])
    cmask = P.sbuf("s_cmask", [128, 4 * 128], F32)
    kb.dma("sp", "ldmk", cmask[:, :], mkin, reads=[R("d_mk")], writes=[R(cmask)])
    nrow = P.sbuf("nrow", [1, 128], F32)
    nmax = P.sbuf("nmax", [1, 32], F32)
    ccol = P.sbuf("ccol", [128, 1], F32)
    kb.dma("sp", "ldn", nrow[:, :], nin, reads=[R("d_n")], writes=[R(nrow)])
    def fmax(out, a, b, reads, writes):
        def fn(eng, _o=out, _a=a, _b=b):
            return eng.tensor_max(_o, _a, _b)
        return kb.op("dve", fn, reads=reads, writes=writes)
    fmax(nmax[:, :], nrow[:, 0:32], nrow[:, 32:64], [R(nrow)], [R(nmax)])
    fmax(nmax[:, :], nmax[:, :], nrow[:, 64:96], [R(nmax), R(nrow)], [R(nmax)])
    fmax(nmax[:, :], nmax[:, :], nrow[:, 96:128], [R(nmax), R(nrow)], [R(nmax)])
    e_act(kb, nmax[:, :], nmax[:, :], AF.Sqrt, reads=[R(nmax)], writes=[R(nmax)],
          tag="csq")
    e_tt(kb, nmax[:, 0:1], nmax[:, 0:1], nmax[:, 16:17], ALU.mult,
         reads=[R(nmax)], writes=[R(nmax, "c")])
    e_ts(kb, nmax[:, 0:1], nmax[:, 0:1], -QK_SCALE, ALU.mult,
         reads=[R(nmax, "c")], writes=[R(nmax, "c")])
    ps_cc = P.pshare([128, 512], 0)
    e_mm(kb, ps_cc[:, 0:1], cst["ones_row_f32"][:, :], nmax[:, 0:1],
         reads=[R(cst["ones_row_f32"]), R(nmax, "c")], writes=[R(ps_cc)],
         tag="cbc")
    e_copy(kb, ccol[:, :], ps_cc[:, 0:1], reads=[R(ps_cc)], writes=[R(ccol)],
           engine="act")
    kblk = [P.sbuf(f"kb{i}", [128, NT * 128], BF16) for i in range(2)]
    vblk = [P.sbuf(f"vb{i}", [128, D], BF16) for i in range(2)]
    pt_ = [P.sbuf(f"ptc{i}", [128, 128], BF16) for i in range(2)]
    ps_st = [P.pshare([128, 512], i) for i in range(2)]
    ps_av = [P.psum(f"psavc{i}", [128, 512], F32) for i in range(2)]
    ps_sum = P.psum("ps_sumc", [1, 128], F32)
    sumr = P.sbuf("sumr", [1, 128], F32)
    rbc = P.sbuf("rbcc", [128, 128], F32)
    O = [P.sbuf(f"oc{k}", [128, TOK], BF16) for k in range(NT)]
    k4 = kin.rearrange("(dt p) s -> p dt s", p=128)
    r_ = None
    for qi in range(NBLK):
        nkb = 4 * (qi + 1)
        kjs = [rr * NBLK + j for rr in range(4) for j in range(qi + 1)]
        for ji, kj in enumerate(kjs):
            kbuf = kblk[ji % 2]
            vbuf = vblk[ji % 2]
            kb.dma("act", f"ldkb{ji % 2}", kbuf[:, :],
                   k4[:, :, kj * 128:(kj + 1) * 128],
                   reads=[R("dram_k")], writes=[R(kbuf)], tag="ldkb", serialize=False)
            kb.dma("act", f"ldvb{ji % 2}", vbuf[:, :],
                   vin[kj * 128:(kj + 1) * 128, :],
                   reads=[R("dram_v")], writes=[R(vbuf)], tag="ldvb", serialize=False)
            pst = ps_st[ji % 2]
            for dt in range(NT):
                e_mm(kb, pst[:, 0:128], kbuf[:, dt * 128:(dt + 1) * 128],
                     Q[dt][:, qi * 128:(qi + 1) * 128],
                     reads=[R(kbuf), R(Q[dt])], writes=[R(pst)],
                     start=(dt == 0), stop=(dt == NT - 1), tag="stc")
            pt = pt_[ji % 2]
            e_act(kb, pt[:, :], pst[:, 0:128], AF.Exp, bias=ccol[:, :],
                  scale=QK_SCALE, reads=[R(pst), R(ccol)], writes=[R(pt)],
                  tag="expc")
            if kj % NBLK == qi:
                rr = kj // NBLK
                e_tt(kb, pt[:, :], pt[:, :], cmask[:, rr * 128:(rr + 1) * 128],
                     ALU.mult, reads=[R(pt), R(cmask)], writes=[R(pt)],
                     tag="maskc")
            e_mm(kb, ps_sum[:, :], cst["ones_col_bf16"][:, :], pt[:, :],
                 reads=[R(cst["ones_col_bf16"]), R(pt)], writes=[R(ps_sum)],
                 start=(ji == 0), stop=(ji == nkb - 1), tag="sumc")
            for dh in range(2):
                for j2 in range(4):
                    e_mm(kb, ps_av[dh][:, j2 * 128:(j2 + 1) * 128],
                         vbuf[:, dh * 512 + j2 * 128:dh * 512 + (j2 + 1) * 128],
                         pt[:, :], reads=[R(vbuf), R(pt)],
                         writes=[R(ps_av[dh])],
                         start=(ji == 0 and j2 == 0),
                         stop=False, tag="avc")
        e_recip(kb, sumr[:, :], ps_sum[:, :], reads=[R(ps_sum)], writes=[R(sumr)])
        e_mm(kb, ps_cc[:, 0:128], cst["ones_row_f32"][:, :], sumr[:, :],
             reads=[R(cst["ones_row_f32"]), R(sumr)], writes=[R(ps_cc)],
             tag="rbcc")
        e_copy(kb, rbc[:, :], ps_cc[:, 0:128], reads=[R(ps_cc)], writes=[R(rbc)],
               engine="act")
        for dh in range(2):
            for j2 in range(4):
                o = dh * 4 + j2
                e_tt(kb, O[o][:, qi * 128:(qi + 1) * 128],
                     ps_av[dh][:, j2 * 128:(j2 + 1) * 128], rbc[:, :], ALU.mult,
                     reads=[R(ps_av[dh]), R(rbc)], writes=[R(O[o], qi)],
                     tag="oevc")
    for k in range(NT):
        e_tt(kb, X[k][:, :], X[k][:, :], O[k][:, :], ALU.add,
             reads=[R(X[k])] + [R(O[k], qi) for qi in range(NBLK)],
             writes=[R(X[k])], tag="resc")
        kb.dma("sp", f"stx{k % 2}", xout[k * 128:(k + 1) * 128, :], X[k][:, :],
               reads=[R(X[k])], writes=[R("dram_xo", k)])
    P.finalize()
    return P


def emit_ln_dram(P, xdram, outs, ntok, cst, pref, eps=1e-5):
    kb = P.kb
    nc2 = 512
    nchunk = ntok // nc2
    XC = [P.sbuf(f"{pref}_xc{k}", [128, nc2], F32) for k in range(NT)]
    sq = P.sbuf(f"{pref}_sq", [128, nc2], F32)
    mrow = P.sbuf(f"{pref}_m", [1, nc2], F32)
    ssrow = P.sbuf(f"{pref}_ss", [1, nc2], F32)
    rr = P.sbuf(f"{pref}_rr", [1, nc2], F32)
    mr2 = P.sbuf(f"{pref}_mr", [1, nc2], F32)
    ps_red = P.pshare([1, nc2], 0)
    ps_bc = P.pshare([128, nc2], 0)
    ps_bc2 = P.pshare([128, nc2], 1)
    ones = cst["ones_col_f32"]
    ones_row = cst["ones_row_f32"]
    for c in range(nchunk):
        sl = slice(c * nc2, (c + 1) * nc2)
        for k in range(NT):
            kb.dma("sp", f"{pref}ldx{k % 4}", XC[k][:, :],
                   xdram[k * 128:(k + 1) * 128, sl],
                   reads=[R("dram_x")], writes=[R(XC[k])])
        for k in range(NT):
            e_mm(kb, ps_red[:, :], ones[:, :], XC[k][:, :],
                 reads=[R(ones), R(XC[k])], writes=[R(ps_red)],
                 start=(k == 0), stop=(k == NT - 1), tag=f"{pref}mean")
        e_copy(kb, mrow[:, :], ps_red[:, :], reads=[R(ps_red)],
               writes=[R(mrow)], engine="act")
        for k in range(NT):
            e_act(kb, sq[:, :], XC[k][:, :], AF.Square,
                  reads=[R(XC[k])], writes=[R(sq)], tag=f"{pref}sq")
            e_mm(kb, ps_red[:, :], ones[:, :], sq[:, :],
                 reads=[R(ones), R(sq)], writes=[R(ps_red)],
                 start=(k == 0), stop=(k == NT - 1), tag=f"{pref}ss")
        e_copy(kb, ssrow[:, :], ps_red[:, :], reads=[R(ps_red)],
               writes=[R(ssrow)], engine="act")
        e_ts(kb, mrow[:, :], mrow[:, :], 1.0 / D, ALU.mult, reads=[R(mrow)],
             writes=[R(mrow)])
        e_ts(kb, ssrow[:, :], ssrow[:, :], 1.0 / D, ALU.mult, reads=[R(ssrow)],
             writes=[R(ssrow)])
        e_tt(kb, rr[:, :], mrow[:, :], mrow[:, :], ALU.mult,
             reads=[R(mrow)], writes=[R(rr)])
        e_tt(kb, rr[:, :], ssrow[:, :], rr[:, :], ALU.subtract,
             reads=[R(ssrow), R(rr)], writes=[R(rr)])
        e_ts(kb, rr[:, :], rr[:, :], float(eps), ALU.add, reads=[R(rr)],
             writes=[R(rr)])
        e_act(kb, rr[:, :], rr[:, :], AF.Sqrt, reads=[R(rr)], writes=[R(rr)],
              tag=f"{pref}sqrt")
        e_recip(kb, rr[:, :], rr[:, :], reads=[R(rr)], writes=[R(rr)])
        e_tt(kb, mr2[:, :], mrow[:, :], rr[:, :], ALU.mult,
             reads=[R(mrow), R(rr)], writes=[R(mr2)])
        e_mm(kb, ps_bc[:, :], ones_row[:, :], rr[:, :],
             reads=[R(ones_row), R(rr)], writes=[R(ps_bc)], tag=f"{pref}bc")
        e_mm(kb, ps_bc2[:, :], ones_row[:, :], mr2[:, :],
             reads=[R(ones_row), R(mr2)], writes=[R(ps_bc2)], tag=f"{pref}bc2")
        for k in range(NT):
            first = outs[0][k]
            e_tt(kb, first[:, sl], XC[k][:, :], ps_bc[:, :], ALU.mult,
                 reads=[R(XC[k]), R(ps_bc)], writes=[R(first, c)])
            e_tt(kb, first[:, sl], first[:, sl], ps_bc2[:, :], ALU.subtract,
                 reads=[R(first, c), R(ps_bc2)], writes=[R(first, c)])
            for extra in outs[1:]:
                e_copy(kb, extra[k][:, sl], first[:, sl],
                       reads=[R(first, c)], writes=[R(extra[k], c)])



def build_ffn1(dbg=False):
    NE, MOE_I, NGRP, NLOC, ANC_I, SH_I = 32, 512, 4, 16, 256, 512
    P = Prog("FFN1")
    kb = P.kb
    xin = P.inp("xfm", [D, TOK], F32)
    gwTin = P.inp("gwT", [D, NE], F32)
    gbin = P.inp("gbias", [1, NE], F32)
    wgin = P.inp("w_gate", [NE, D, MOE_I], F32)
    wuin = P.inp("w_up", [NE, D, MOE_I], F32)
    wdin = P.inp("w_down", [NE, MOE_I, D], F32)
    swgin = P.inp("sh_gate", [D, SH_I], F32)
    swuin = P.inp("sh_up", [D, SH_I], F32)
    swdin = P.inp("sh_down", [SH_I, D], F32)
    awgin = P.inp("an_gate", [NLOC, D, ANC_I], F32)
    awuin = P.inp("an_up", [NLOC, D, ANC_I], F32)
    awdin = P.inp("an_down", [NLOC, ANC_I, D], F32)
    kronin = P.inp("kron32", [NE, NE * 128], F32)
    xout = P.out("xofm", [D, TOK], F32)

    cst = emit_consts(P)
    HTOK = 1024
    X = [P.sbuf(f"x{k}", [128, HTOK], F32) for k in range(NT)]
    T = [P.sbuf(f"t{k}", [128, HTOK], BF16) for k in range(NT)]
    Tfc = [P.sbuf(f"tfc{k}", [128, 512], F32) for k in range(NT)]
    lgT = P.sbuf("lgT", [NE, HTOK], F32)
    wgb = P.sbuf("wgb", [128, NT * MOE_I], BF16)
    hb = P.sbuf("hb", [128, (MOE_I // 128) * HTOK], BF16)
    sg = P.sbuf("sgb", [128, (MOE_I // 128) * HTOK], F32)
    wdb = P.sbuf("wdb", [128, (MOE_I // 128) * D], BF16)
    gwT = P.sbuf("s_gwT", [128, NT * NE], F32)
    gbrow = P.sbuf("gbrow", [1, NE], F32)
    gbbc = P.sbuf("gbbc", [128, NE], F32)
    cw = P.sbuf("cw", [NE, HTOK], F32)
    acw = P.sbuf("acw", [16, HTOK], F32)
    kron = P.sbuf("s_kron", [NE, NE * 128], F32)
    ident = P.sbuf("ident", [128, 128], F32)
    siota = P.sbuf("siota", [128, 128], F32)
    cbc = P.sbuf("cbc", [128, HTOK], F32)
    lg = P.sbuf("lg", [128, NE], F32)
    sc = P.sbuf("scb", [128, NE], F32)
    scm = P.sbuf("scm", [128, NE], F32)
    eq = P.sbuf("eqb", [128, NE], F32)
    eq0 = P.sbuf("eq0b", [128, NE], F32)
    sel = P.sbuf("selb", [128, NE], F32)
    gsc = P.sbuf("gsc", [128, NGRP], F32)
    gtmp = P.sbuf("gtmp", [128, NGRP], F32)
    mcol = P.sbuf("mcol", [128, 1], F32)
    m2col = P.sbuf("m2col", [128, 1], F32)
    ps_l = P.pshare([128, 512], 0)
    ps_l2 = P.pshare([128, 512], 1)
    ps_t = P.psum("ps_t", [NE, 128], F32)
    ps_t2 = P.psum("ps_t2", [128, 128], F32)

    e_iota(kb, siota[:, :], [[1, 128]], writes=[R(siota)], channel_multiplier=-1)
    e_ts(kb, ident[:, :], siota[:, :], 0.0, ALU.is_equal, reads=[R(siota)],
         writes=[R(ident)], engine="pool")
    kb.dma("sp", "ldgb", gbrow[:, :], gbin, reads=[R("d_gb")], writes=[R(gbrow)])
    kb.dma("sp", "ldkron", kron[:, :], kronin, reads=[R("d_kron")], writes=[R(kron)])
    for k in range(NT):
        kb.dma("pool", "ldgw", gwT[:, k * NE:(k + 1) * NE],
               gwTin[k * 128:(k + 1) * 128, :], reads=[R("d_gwT")],
               writes=[R(gwT, k)])
    e_mm(kb, ps_l[:, 0:NE], cst["ones_row_f32"][:, :], gbrow[:, :],
         reads=[R(cst["ones_row_f32"]), R(gbrow)], writes=[R(ps_l)], tag="gbbc")
    e_copy(kb, gbbc[:, :], ps_l[:, 0:NE], reads=[R(ps_l)], writes=[R(gbbc)],
           engine="act")

    def expert_block(pref, e, wg_ap, wu_ap, wd_ap, moei, coef_rows, coef_n,
                     kron_base, alpha):
        nmo = moei // 128
        kb.dma("pool", "wgb",
               wgb[:, 0:NT * moei].rearrange("p (k h) -> p k h", h=moei),
               wg_ap[:, :].rearrange("(k p) h -> p k h", p=128),
               reads=[R("d_wg")], writes=[R(wgb)], tag=f"{pref}lg",
               serialize=False)
        for ho in range(nmo):
            for c2 in range(2):
                p = ps_l if (ho + c2) % 2 == 0 else ps_l2
                for k in range(NT):
                    e_mm(kb, p[:, :], wgb[:, k * moei + ho * 128:k * moei + (ho + 1) * 128],
                         T[k][:, c2 * 512:(c2 + 1) * 512],
                         reads=[R(wgb, k), R(T[k])], writes=[R(p)],
                         start=(k == 0), stop=(k == NT - 1), tag=f"{pref}g")
                e_act(kb, sg[:, ho * HTOK + c2 * 512:ho * HTOK + (c2 + 1) * 512],
                      p[:, :], AF.Silu, reads=[R(p)],
                      writes=[R("sgb", (ho, c2))], tag=f"{pref}si")
        kb.dma("pool", "wgb",
               wgb[:, 0:NT * moei].rearrange("p (k h) -> p k h", h=moei),
               wu_ap[:, :].rearrange("(k p) h -> p k h", p=128),
               reads=[R("d_wu")], writes=[R(wgb)], tag=f"{pref}lu",
               serialize=False)
        if coef_rows is not None:
            for c2 in range(2):
                e_mm(kb, ps_l2[:, :] if c2 == 0 else ps_l[:, :],
                     kron[0:coef_n, kron_base * 128:(kron_base + 1) * 128],
                     coef_rows[:, c2 * 512:(c2 + 1) * 512],
                     reads=[R(kron), R(coef_rows)],
                     writes=[R(ps_l2) if c2 == 0 else R(ps_l)], tag=f"{pref}cb")
                e_copy(kb, cbc[:, c2 * 512:(c2 + 1) * 512],
                       ps_l2[:, :] if c2 == 0 else ps_l[:, :],
                       reads=[R(ps_l2) if c2 == 0 else R(ps_l)],
                       writes=[R(cbc, c2)], engine="act")
        for ho in range(nmo):
            for c2 in range(2):
                p = ps_l if (ho + c2) % 2 == 0 else ps_l2
                for k in range(NT):
                    e_mm(kb, p[:, :], wgb[:, k * moei + ho * 128:k * moei + (ho + 1) * 128],
                         T[k][:, c2 * 512:(c2 + 1) * 512],
                         reads=[R(wgb, k), R(T[k])], writes=[R(p)],
                         start=(k == 0), stop=(k == NT - 1), tag=f"{pref}u")
                sl = slice(ho * HTOK + c2 * 512, ho * HTOK + (c2 + 1) * 512)
                if coef_rows is not None:
                    e_tt(kb, sg[:, sl], sg[:, sl], p[:, :], ALU.mult,
                         reads=[R(p), R("sgb", (ho, c2))],
                         writes=[R("sgb", (ho, c2))], tag=f"{pref}h")
                    e_tt(kb, hb[:, sl], sg[:, sl], cbc[:, c2 * 512:(c2 + 1) * 512],
                         ALU.mult,
                         reads=[R("sgb", (ho, c2)), R(cbc, c2)],
                         writes=[R("hb", (ho, c2))], tag=f"{pref}hc")
                else:
                    e_tt(kb, hb[:, sl], sg[:, sl], p[:, :], ALU.mult,
                         reads=[R(p), R("sgb", (ho, c2))], writes=[R("hb", (ho, c2))],
                         tag=f"{pref}h")
        kb.dma("pool", "wdb",
               wdb[:, :].rearrange("p (k h) -> p k h", h=D)[:, 0:nmo, :],
               wd_ap[:, :].rearrange("(k p) h -> p k h", p=128),
               reads=[R("d_wd")], writes=[R(wdb)], tag=f"{pref}ld",
               serialize=False)
        for o in range(NT):
            for c2 in range(2):
                p = ps_l if (o + c2) % 2 == 0 else ps_l2
                for hk in range(nmo):
                    e_mm(kb, p[:, :], wdb[:, hk * D + o * 128:hk * D + (o + 1) * 128],
                         hb[:, hk * HTOK + c2 * 512:hk * HTOK + (c2 + 1) * 512],
                         reads=[R(wdb, hk), R("hb", (hk, c2))], writes=[R(p)],
                         start=(hk == 0), stop=(hk == nmo - 1), tag=f"{pref}d")
                if alpha == 1.0:
                    e_tt(kb, X[o][:, c2 * 512:(c2 + 1) * 512],
                         X[o][:, c2 * 512:(c2 + 1) * 512], p[:, :], ALU.add,
                         reads=[R(X[o]), R(p)], writes=[R(X[o])], tag=f"{pref}x")
                else:
                    e_stt(kb, X[o][:, c2 * 512:(c2 + 1) * 512], p[:, :],
                          float(alpha), X[o][:, c2 * 512:(c2 + 1) * 512],
                          ALU.mult, ALU.add,
                          reads=[R(X[o]), R(p)], writes=[R(X[o])], tag=f"{pref}xa")

    for half in range(2):
        hsl = slice(half * HTOK, (half + 1) * HTOK)
        for k in range(NT):
            kb.dma("sp", f"ldx{k % 2}", X[k][:, :],
                   xin[k * 128:(k + 1) * 128, hsl],
                   reads=[R("dram_xfm")], writes=[R(X[k])])
        def rt_cb(c, rot):
            psr = ps_l2
            for k in range(NT):
                e_mm(kb, psr[0:NE, 0:512], gwT[:, k * NE:(k + 1) * NE],
                     rot[k][:, 0:512],
                     reads=[R(gwT, k), R(rot[k])], writes=[R(psr)],
                     start=(k == 0), stop=(k == NT - 1), tag="rt1")
            e_copy(kb, lgT[:, c * 512:(c + 1) * 512], psr[0:NE, 0:512],
                   reads=[R(psr)], writes=[R(lgT, c)], engine="act")
        emit_ln(P, X, [Tfc, T], HTOK, cst, f"ln{half}", chunk_cb=rt_cb,
                first_rot=True)
        for ts_ in range(HTOK // 128):
            tsl = slice(ts_ * 128, (ts_ + 1) * 128)
            e_transpose(kb, ps_t2[:, 0:NE], lgT[:, ts_ * 128:(ts_ + 1) * 128],
                        ident[0:NE, 0:NE],
                        reads=[R(lgT, ts_ // 4), R(ident)], writes=[R(ps_t2)])
            e_act(kb, lg[:, :], ps_t2[:, 0:NE], AF.Sigmoid, reads=[R(ps_t2)],
                  writes=[R(lg)], tag="rtsig")
            e_tt(kb, sc[:, :], lg[:, :], gbbc[:, :], ALU.add,
                 reads=[R(lg), R(gbbc)], writes=[R(sc)])
            for g in range(NGRP):
                gs = slice(g * 8, (g + 1) * 8)
                e_red(kb, mcol[:, :], sc[:, gs], ALU.max, reads=[R(sc)],
                      writes=[R(mcol)])
                e_ts(kb, eq[:, gs], sc[:, gs], mcol[:, :], ALU.is_equal,
                     reads=[R(sc), R(mcol)], writes=[R(eq, g)])
                e_stt(kb, scm[:, gs], eq[:, gs], -1e30, sc[:, gs],
                      ALU.mult, ALU.add, reads=[R(eq, g), R(sc)],
                      writes=[R(scm, g)])
                e_red(kb, m2col[:, :], scm[:, gs], ALU.max, reads=[R(scm, g)],
                      writes=[R(m2col)])
                e_tt(kb, gsc[:, g:g + 1], mcol[:, :], m2col[:, :], ALU.add,
                     reads=[R(mcol), R(m2col)], writes=[R(gsc, g)])
            e_red(kb, mcol[:, :], gsc[:, :], ALU.max,
                  reads=[R(gsc, g) for g in range(NGRP)], writes=[R(mcol)])
            e_ts(kb, gtmp[:, :], gsc[:, :], mcol[:, :], ALU.is_equal,
                 reads=[R(gsc, g) for g in range(NGRP)] + [R(mcol)],
                 writes=[R(gtmp)])
            e_stt(kb, gtmp[:, :], gtmp[:, :], -1e30,
                  gsc[:, :], ALU.mult, ALU.add,
                  reads=[R(gtmp)] + [R(gsc, g) for g in range(NGRP)],
                  writes=[R(gtmp)])
            e_red(kb, m2col[:, :], gtmp[:, :], ALU.max, reads=[R(gtmp)],
                  writes=[R(m2col)])
            e_ts(kb, gtmp[:, :], gsc[:, :], m2col[:, :], ALU.is_ge,
                 reads=[R(gsc, g) for g in range(NGRP)] + [R(m2col)],
                 writes=[R(gtmp)])
            e_ts(kb, gtmp[:, :], gtmp[:, :], 1.0, ALU.subtract, scalar2=1e30,
                 op1=ALU.mult, reads=[R(gtmp)], writes=[R(gtmp)])
            for g in range(NGRP):
                gs = slice(g * 8, (g + 1) * 8)
                e_ts(kb, scm[:, gs], sc[:, gs], gtmp[:, g:g + 1], ALU.add,
                     reads=[R(sc), R(gtmp)], writes=[R(scm, f"m{g}")])
            e_memset(kb, sel[:, :], 0.0, writes=[R(sel)], engine="dve")
            scm_r = [R(scm, f"m{g}") for g in range(NGRP)]
            for it in range(6):
                e_red(kb, mcol[:, :], scm[:, :], ALU.max, reads=scm_r,
                      writes=[R(mcol)])
                dst = eq0 if it == 0 else eq
                e_ts(kb, dst[:, :], scm[:, :], mcol[:, :], ALU.is_equal,
                     reads=scm_r + [R(mcol)], writes=[R(dst)])
                e_tt(kb, sel[:, :], sel[:, :], dst[:, :], ALU.add,
                     reads=[R(sel), R(dst)], writes=[R(sel)])
                e_stt(kb, scm[:, :], dst[:, :], -1e30, scm[:, :],
                      ALU.mult, ALU.add, reads=[R(dst)] + scm_r, writes=scm_r)
            e_tt(kb, sel[:, :], sel[:, :], lg[:, :], ALU.mult,
                 reads=[R(sel), R(lg)], writes=[R(sel)])
            e_red(kb, mcol[:, :], sel[:, :], ALU.add, reads=[R(sel)],
                  writes=[R(mcol)])
            e_ts(kb, mcol[:, :], mcol[:, :], 1e-20, ALU.add, reads=[R(mcol)],
                 writes=[R(mcol)])
            e_recip(kb, mcol[:, :], mcol[:, :], reads=[R(mcol)], writes=[R(mcol)])
            e_ts(kb, sel[:, :], sel[:, :], mcol[:, :], ALU.mult, scalar2=0.2,
                 op1=ALU.mult, reads=[R(sel), R(mcol)], writes=[R(sel)])
            e_transpose(kb, ps_t[:, :], sel[:, :], ident[:, :],
                        reads=[R(sel), R(ident)], writes=[R(ps_t)])
            e_copy(kb, cw[:, tsl], ps_t[:, :], reads=[R(ps_t)],
                   writes=[R(cw, ts_)], engine="act")
            e_tt(kb, eq[:, 0:16], eq0[:, 0:NE:2], eq0[:, 1:NE:2], ALU.add,
                 reads=[R(eq0)], writes=[R(eq, "a")])
            e_transpose(kb, ps_t[0:16, :], eq[:, 0:16], ident[:, :],
                        reads=[R(eq, "a"), R(ident)], writes=[R(ps_t)])
            e_copy(kb, acw[:, tsl], ps_t[0:16, :], reads=[R(ps_t)],
                   writes=[R(acw, ts_)], engine="act")
        cwreads = [R(cw, t2) for t2 in range(8)]
        acreads = [R(acw, t2) for t2 in range(8)]
        if dbg and half == 0:
            cwdb = P.out("cwdb", [NE, HTOK], F32)
            acdb = P.out("acdb", [16, HTOK], F32)
            kb.dma("sp", "stcw", cwdb, cw[:, :], reads=cwreads, writes=[R("d_cw")])
            kb.dma("sp", "stac", acdb, acw[:, :], reads=acreads, writes=[R("d_ac")])
        for e in range(NE):
            expert_block(f"r{e}", e, wgin[e], wuin[e], wdin[e], MOE_I,
                         cw, NE, e, 1.0)
        expert_block("sh", 0, swgin, swuin, swdin, SH_I, None, 0, 0, 1.0)
        for e in range(NLOC):
            expert_block(f"a{e}", e, awgin[e], awuin[e], awdin[e], ANC_I,
                         acw, 16, e, 0.1)
        for k in range(NT):
            kb.dma("sp", f"stx{k % 2}", xout[k * 128:(k + 1) * 128, hsl],
                   X[k][:, :], reads=[R(X[k])], writes=[R("dram_xo", (half, k))])
    P.finalize()
    return P



def build_scanprep():
    P = Prog("SPREP")
    kb = P.kb
    xin = P.inp("xfm", [D, TOK], F32)
    ewin = P.inp("eta_w", [D], F32)
    ebin = P.inp("eta_b", [1], F32)
    awin = P.inp("alpha_w", [D], F32)
    abin = P.inp("alpha_b", [1], F32)
    gwin = P.inp("gate_w", [D], F32)
    gbin = P.inp("gate_b", [1], F32)
    tout = P.out("t6fm", [D, TOK], BF16)
    kmout = P.out("kmean", [32, D], F32)
    scout = P.out("sca", [1, 5 * 32], F32)

    cst = emit_consts(P)
    Tf = [P.sbuf(f"tf{k}", [128, TOK], F32) for k in range(NT)]
    Tb = [P.sbuf(f"tb{k}", [128, TOK], BF16) for k in range(NT)]
    emit_ln_dram(P, xin, [Tf, Tb], TOK, cst, "ln")
    for k in range(NT):
        kb.dma("sp", "stt", tout[k * 128:(k + 1) * 128, :], Tb[k][:, :],
               reads=[R(Tb[k], c) for c in range(4)], writes=[R("d_t", k)])
    sqb = P.sbuf("sqb", [128, 512], F32)
    nr = P.sbuf("nr", [1, TOK], F32)
    ps_r = P.pshare([1, 512], 0)
    ps_b = P.pshare([128, 512], 0)
    ones = cst["ones_col_f32"]
    ones_row = cst["ones_row_f32"]
    for c in range(4):
        sl = slice(c * 512, (c + 1) * 512)
        for k in range(NT):
            e_act(kb, sqb[:, :], Tf[k][:, sl], AF.Square,
                  reads=[R(Tf[k], c)], writes=[R(sqb)], tag="nsq")
            e_mm(kb, ps_r[:, :], ones[:, :], sqb[:, :],
                 reads=[R(ones), R(sqb)], writes=[R(ps_r)],
                 start=(k == 0), stop=(k == NT - 1), tag="nmm")
        e_copy(kb, nr[:, sl], ps_r[:, :], reads=[R(ps_r)], writes=[R(nr, c)],
               engine="act")
    e_act(kb, nr[:, :], nr[:, :], AF.Sqrt,
          reads=[R(nr, c) for c in range(4)], writes=[R(nr)], tag="nsqrt")
    e_ts(kb, nr[:, :], nr[:, :], 1e-5, ALU.max, reads=[R(nr)], writes=[R(nr)])
    e_recip(kb, nr[:, :], nr[:, :], reads=[R(nr)], writes=[R(nr)])
    kmT = [P.sbuf(f"kmT{k}", [128, 32], F32) for k in range(NT)]
    s6b = P.sbuf("s6b", [128, 512], F32)
    for c in range(4):
        sl = slice(c * 512, (c + 1) * 512)
        e_mm(kb, ps_b[:, :], ones_row[:, :], nr[:, sl],
             reads=[R(ones_row), R(nr)], writes=[R(ps_b)], tag="rnbc")
        for k in range(NT):
            e_tt(kb, s6b[:, :], Tf[k][:, sl], ps_b[:, :], ALU.mult,
                 reads=[R(Tf[k], c), R(ps_b)], writes=[R(s6b)], tag="s6")
            for j in range(8):
                e_red(kb, kmT[k][:, c * 8 + j:c * 8 + j + 1],
                      s6b[:, j * 64:(j + 1) * 64], ALU.add,
                      reads=[R(s6b)], writes=[R(kmT[k], c * 8 + j)], tag="kmred")
    for k in range(NT):
        e_ts(kb, kmT[k][:, :], kmT[k][:, :], 1.0 / 64, ALU.mult,
             reads=[R(kmT[k], j) for j in range(32)], writes=[R(kmT[k])])
    ident = P.sbuf("identS", [128, 128], F32)
    siota = P.sbuf("siotaS", [128, 128], F32)
    e_iota(kb, siota[:, :], [[1, 128]], writes=[R(siota)], channel_multiplier=-1)
    e_ts(kb, ident[:, :], siota[:, :], 0.0, ALU.is_equal, reads=[R(siota)],
         writes=[R(ident)], engine="pool")
    ps_t = P.psum("ps_tS", [32, 128], F32)
    kmr = P.sbuf("kmr", [32, D], F32)
    for k in range(NT):
        e_transpose(kb, ps_t[:, :], kmT[k][:, :], ident[:, :],
                    reads=[R(kmT[k]), R(ident)], writes=[R(ps_t)])
        e_copy(kb, kmr[:, k * 128:(k + 1) * 128], ps_t[:, :],
               reads=[R(ps_t)], writes=[R(kmr, k)], engine="act")
    kb.dma("sp", "stkm", kmout, kmr[:, :],
           reads=[R(kmr, k) for k in range(NT)], writes=[R("d_km")])
    sqk = P.sbuf("sqk", [128, 32], F32)
    ps_k = P.psum("ps_kS", [1, 32], F32)
    kkr = P.sbuf("kkr", [1, 32], F32)
    for k in range(NT):
        e_act(kb, sqk[:, :], kmT[k][:, :], AF.Square,
              reads=[R(kmT[k])], writes=[R(sqk)], tag="kksq")
        e_mm(kb, ps_k[:, :], ones[:, :], sqk[:, :],
             reads=[R(ones), R(sqk)], writes=[R(ps_k)],
             start=(k == 0), stop=(k == NT - 1), tag="kkmm")
    e_copy(kb, kkr[:, :], ps_k[:, :], reads=[R(ps_k)], writes=[R(kkr)],
           engine="act")
    ecol = load_cols(P, "etaw", ewin, D)
    acol = load_cols(P, "alphaw", awin, D)
    gcol = load_cols(P, "gatew", gwin, D)
    bias3 = P.sbuf("bias3", [1, 3], F32)
    kb.dma("sp", "ldb3", bias3[:, 0:1], ebin, reads=[R("d_b3")], writes=[R(bias3, 0)])
    kb.dma("sp", "ldb3b", bias3[:, 1:2], abin, reads=[R("d_b3")], writes=[R(bias3, 1)])
    kb.dma("sp", "ldb3c", bias3[:, 2:3], gbin, reads=[R("d_b3")], writes=[R(bias3, 2)])
    sca = P.sbuf("scab", [1, 5 * 32], F32)
    tmp32 = P.sbuf("tmp32", [1, 32], F32)
    g32 = P.sbuf("g32", [1, 32], F32)
    srow = P.sbuf("srowS", [1, TOK], F32)
    for name, col, brow, dst in (("eta", ecol, 0, 0), ("alpha", acol, 1, 1)):
        for c in range(4):
            sl = slice(c * 512, (c + 1) * 512)
            for k in range(NT):
                e_mm(kb, ps_r[:, :], col[:, k:k + 1], Tf[k][:, sl],
                     reads=[R(col), R(Tf[k], c)], writes=[R(ps_r)],
                     start=(k == 0), stop=(k == NT - 1), tag=f"{name}mm")
            e_act(kb, srow[:, sl], ps_r[:, :], AF.Sigmoid,
                  bias=bias3[:, brow:brow + 1],
                  reads=[R(ps_r), R(bias3, brow)], writes=[R(srow, c)],
                  tag=f"{name}sig")
        for ch in range(32):
            e_red(kb, sca[:, dst * 32 + ch:dst * 32 + ch + 1],
                  srow[:, ch * 64:(ch + 1) * 64], ALU.add,
                  reads=[R(srow, ch // 8)], writes=[R(sca, (dst, ch))],
                  tag=f"{name}red")
    for k in range(NT):
        e_mm(kb, ps_k[:, :], gcol[:, k:k + 1], kmT[k][:, :],
             reads=[R(gcol), R(kmT[k])], writes=[R(ps_k)],
             start=(k == 0), stop=(k == NT - 1), tag="gmm")
    e_act(kb, g32[:, :], ps_k[:, :], AF.Sigmoid, bias=bias3[:, 2:3],
          reads=[R(ps_k), R(bias3, 2)], writes=[R(g32)], tag="gsig")
    sca_r0 = [R(sca, (0, ch)) for ch in range(32)]
    sca_r1 = [R(sca, (1, ch)) for ch in range(32)]
    e_ts(kb, sca[:, 0:32], sca[:, 0:32], 0.2 / 64, ALU.mult,
         reads=sca_r0, writes=[R(sca, "e")])
    e_ts(kb, sca[:, 32:64], sca[:, 32:64], 0.5 / 64, ALU.mult, scalar2=0.5,
         op1=ALU.add, reads=sca_r1, writes=[R(sca, "al")])
    e_ts(kb, tmp32[:, :], sca[:, 32:64], 1.0, ALU.subtract,
         reads=[R(sca, "al")], writes=[R(tmp32)])
    e_tt(kb, tmp32[:, :], g32[:, :], tmp32[:, :], ALU.mult,
         reads=[R(g32), R(tmp32)], writes=[R(tmp32)])
    e_ts(kb, tmp32[:, :], tmp32[:, :], 1.0, ALU.add,
         reads=[R(tmp32)], writes=[R(tmp32)])
    e_tt(kb, sca[:, 32:64], g32[:, :], sca[:, 0:32], ALU.mult,
         reads=[R(g32), R(sca, "e")], writes=[R(sca, "u")])
    e_copy(kb, sca[:, 0:32], tmp32[:, :], reads=[R(tmp32)], writes=[R(sca, "a")])
    e_tt(kb, sca[:, 64:96], sca[:, 0:32], sca[:, 0:32], ALU.mult,
         reads=[R(sca, "a")], writes=[R(sca, "a2")])
    e_tt(kb, sca[:, 96:128], sca[:, 0:32], sca[:, 32:64], ALU.mult,
         reads=[R(sca, "a"), R(sca, "u")], writes=[R(sca, "au")])
    e_ts(kb, sca[:, 96:128], sca[:, 96:128], 2.0, ALU.mult,
         reads=[R(sca, "au")], writes=[R(sca, "au")])
    e_tt(kb, sca[:, 128:160], sca[:, 32:64], sca[:, 32:64], ALU.mult,
         reads=[R(sca, "u")], writes=[R(sca, "uu")])
    e_tt(kb, sca[:, 128:160], sca[:, 128:160], kkr[:, :], ALU.mult,
         reads=[R(sca, "uu"), R(kkr)], writes=[R(sca, "uu")])
    kb.dma("sp", "stsc", scout, sca[:, :],
           reads=[R(sca, "a"), R(sca, "u"), R(sca, "a2"), R(sca, "au"),
                  R(sca, "uu")], writes=[R("d_sc")])
    P.finalize()
    return P



def build_scan(NCH=128, stage=9):
    P = Prog("SCAN")
    kb = P.kb
    NCHF = 128
    t6in = P.inp("t6g", [D, S], BF16)
    kmrin = P.inp("kmr", [NCHF, D], F32)
    kmcin = P.inp("kmcT", [D, NCHF], F32)
    scin = P.inp("sca", [1, 5 * NCHF], F32)
    m0in = P.inp("m0T", [D, D], F32)
    w1in = P.inp("w1T", [D, D], F32)
    w2in = P.inp("w2T", [D, D], F32)
    oout = P.out("outsg", [D, S], BF16)

    cst = emit_consts(P)
    kb_ = kb
    Mst = [P.sbuf(f"mst{k}", [128, D], F32) for k in range(NT)]
    Mbf = [P.sbuf(f"mbf{k}", [128, D], BF16) for k in range(NT)]
    for k in range(NT):
        kb.dma("sp", "ldm0", Mst[k][:, :], m0in[k * 128:(k + 1) * 128, :],
               reads=[R("d_m0")], writes=[R(Mst[k])])
        e_copy(kb, Mbf[k][:, :], Mst[k][:, :], reads=[R(Mst[k])],
               writes=[R(Mbf[k])], engine="act")
    w1b, w1a = load_weight_fm(P, "w1T", w1in, D, D)
    w2b, w2a = load_weight_fm(P, "w2T", w2in, D, D)
    kmc = P.sbuf("kmc", [128, NT * NCHF], F32)
    kb.dma("sp", "ldkmc",
           kmc[:, :].rearrange("p (k c) -> p k c", c=NCHF),
           kmcin.rearrange("(k p) c -> p k c", p=128),
           reads=[R("d_kmc")], writes=[R(kmc)], serialize=False)
    scar = P.sbuf("scar", [1, 5 * NCHF], F32)
    kb.dma("sp", "ldsc", scar[:, :], scin, reads=[R("d_sc")], writes=[R(scar)])
    sqm = P.sbuf("sqmS", [128, 512], F32)
    rcol = P.sbuf("rcolS", [128, 8], F32)
    ps_f = P.psum("pdS", [1, 8], F32)
    Ft = P.sbuf("Ft", [1, 8], F32)
    for k in range(NT):
        for c in range(2):
            e_act(kb, sqm[:, :], Mst[k][:, c * 512:(c + 1) * 512], AF.Square,
                  reads=[R(Mst[k])], writes=[R(sqm)], tag="f0sq")
            e_red(kb, rcol[:, k:k + 1] if c == 0 else rcol[:, k:k + 1],
                  sqm[:, :], ALU.add, reads=[R(sqm)],
                  writes=[R(rcol, (k, c))], tag="f0red")
            if c == 0:
                e_copy(kb, sqm[:, 0:1], rcol[:, k:k + 1],
                       reads=[R(rcol, (k, 0))], writes=[R(sqm, "t")])
            else:
                e_tt(kb, rcol[:, k:k + 1], rcol[:, k:k + 1], sqm[:, 0:1],
                     ALU.add, reads=[R(rcol, (k, 1)), R(sqm, "t")],
                     writes=[R(rcol, (k, "s"))])
    e_mm(kb, ps_f[:, :], cst["ones_col_f32"][:, :], rcol[:, :],
         reads=[R(cst["ones_col_f32"])] + [R(rcol, (k, "s")) for k in range(NT)],
         writes=[R(ps_f)], tag="f0mm")
    e_copy(kb, Ft[:, :], ps_f[:, :], reads=[R(ps_f)], writes=[R(Ft, "i")],
           engine="act")
    e_red(kb, Ft[:, 0:1], Ft[:, :], ALU.add, reads=[R(Ft, "i")],
          writes=[R(Ft, 0)], tag="f0sum")

    errd = P.dram("errscratch", [8, 128], F32)
    cb = [P.sbuf(f"cbS{i}", [128, NT * 64], BF16) for i in range(2)]
    krow = [P.sbuf(f"krowS{i}", [1, D], F32) for i in range(2)]
    po = P.psum("poS", [128, 512], F32)
    ph = P.psum("phS", [128, 512], F32)
    pvk = P.psum("pvkS", [128, 16], F32)
    pv = pvk[:, 8:16]
    pk = pvk[:, 0:8]
    pd = P.bufs["pdS"][0:1, 0:2]
    pout = [P.psum(f"poutS{i}", [128, 512], F32) for i in range(2)]
    ps_bc1 = P.pshare([128, 512], 0)
    ob = P.sbuf("obS", [128, 512], BF16)
    hb = P.sbuf("hbS", [128, 512], BF16)
    omc = P.sbuf("omcS", [128, 8], F32)
    hmcf = P.sbuf("hmcfS", [128, 8], F32)
    hmc = P.sbuf("hmcS", [128, 8], BF16)
    kcb = P.sbuf("kcbS", [128, 8], BF16)
    Mkc = P.sbuf("MkcS", [128, 16], F32)
    se8 = P.sbuf("se8S", [8, 128], F32)
    erow = P.sbuf("erowS", [1, D], F32)
    erb = P.sbuf("erbS", [1, D], BF16)
    krb = P.sbuf("krbS", [1, D], BF16)
    sac = P.sbuf("sacS", [128, 1], F32)
    w9 = P.sbuf("w9S", [1, 12], F32)
    identS = P.sbuf("identS", [128, 128], F32)
    siotaS = P.sbuf("siotaS", [128, 128], F32)
    pse = P.psum("pseS", [8, 128], F32)
    e_iota(kb, siotaS[:, :], [[1, 128]], writes=[R(siotaS)], channel_multiplier=-1)
    e_ts(kb, identS[:, :], siotaS[:, :], 0.0, ALU.is_equal, reads=[R(siotaS)],
         writes=[R(identS)], engine="pool")

    t6r = t6in.rearrange("(k p) s -> p k s", p=128)
    oor = oout.rearrange("(k p) s -> p k s", p=128)

    def loadstep(t):
        b = cb[t % 2]
        kb.dma("act", f"ldcb{t % 2}",
               b[:, :].rearrange("p (k c) -> p k c", c=64),
               t6r[:, :, t * 64:(t + 1) * 64],
               reads=[R("d_t6")], writes=[R(b)], serialize=False, tag="ldcb")
        kb.dma("act", f"ldkr{t % 2}", krow[t % 2][:, :],
               kmrin[t:t + 1, :], reads=[R("d_kmr")],
               writes=[R(krow[t % 2])], serialize=False, tag="ldkr")

    loadstep(0)
    for t in range(NCH):
        b = cb[t % 2]
        kr = krow[t % 2]
        if t + 1 < NCH:
            loadstep(t + 1)
        for o in range(NT):
            for k in range(NT):
                e_mm(kb, po[:, o * 64:(o + 1) * 64],
                     Mbf[k][:, o * 128:(o + 1) * 128],
                     b[:, k * 64:(k + 1) * 64],
                     reads=[R(Mbf[k]), R(b)], writes=[R(po)],
                     start=(o == 0 and k == 0), stop=False, tag="outmm")
        e_copy(kb, ob[:, :], po[:, :], reads=[R(po)], writes=[R(ob)],
               engine="act")
        kb.dma("sp", "stout", oor[:, :, t * 64:(t + 1) * 64],
               ob[:, :].rearrange("p (k c) -> p k c", c=64),
               reads=[R(ob)], writes=[R("d_og", t)], serialize=False,
               tag="stout")
        if stage < 2:
            continue
        for o in range(NT):
            e_red(kb, omc[:, o:o + 1], po[:, o * 64:(o + 1) * 64], ALU.add,
                  reads=[R(po)], writes=[R(omc, o)], tag="omred")
        e_ts(kb, omc[:, :], omc[:, :], 1.0 / 64, ALU.mult,
             reads=[R(omc, o) for o in range(NT)], writes=[R(omc)])
        if stage < 2.7:
            continue
        for o2 in range(NT):
            for k in range(NT):
                e_mm(kb, ph[:, o2 * 64:(o2 + 1) * 64],
                     w1a(k)[:, o2 * 128:(o2 + 1) * 128],
                     ob[:, k * 64:(k + 1) * 64],
                     reads=[R("s_w1T", k), R(ob)], writes=[R(ph)],
                     start=(o2 == 0 and k == 0), stop=False, tag="vgmm")
        e_act(kb, hb[:, :], ph[:, :], AF.Silu, reads=[R(ph)], writes=[R(hb)],
              tag="vgsilu")
        for o2 in range(NT):
            e_red(kb, hmcf[:, o2:o2 + 1], hb[:, o2 * 64:(o2 + 1) * 64],
                  ALU.add, reads=[R(hb)], writes=[R(hmcf, o2)], tag="hmred")
        e_ts(kb, hmc[:, :], hmcf[:, :], 1.0 / 64, ALU.mult,
             reads=[R(hmcf, o2) for o2 in range(NT)], writes=[R(hmc)])
        if stage < 3:
            continue
        for o2 in range(NT):
            for j in range(NT):
                e_mm(kb, pvk[:, 8 + o2:9 + o2],
                     w2a(j)[:, o2 * 128:(o2 + 1) * 128], hmc[:, j:j + 1],
                     reads=[R("s_w2T", j), R(hmc)], writes=[R("pvkS")],
                     start=(o2 == 0 and j == 0), stop=False, tag="v2mm")
        e_copy(kb, kcb[:, :], kmc[:, :].rearrange("p (k c) -> p k c", c=NCHF)[:, :, t],
               reads=[R(kmc)], writes=[R(kcb)], tag="kcb")
        for o in range(NT):
            for i in range(NT):
                e_mm(kb, pvk[:, o:o + 1],
                     Mbf[i][:, o * 128:(o + 1) * 128], kcb[:, i:i + 1],
                     reads=[R(Mbf[i]), R(kcb)], writes=[R("pvkS")],
                     start=(o == 0 and i == 0), stop=False, tag="mkmm")
        e_copy(kb, Mkc[:, 0:8], pk, reads=[R("pvkS")], writes=[R(Mkc, "mk")],
               engine="act")
        e_tt(kb, Mkc[:, 8:16], pv, omc[:, :], ALU.add,
             reads=[R("pvkS"), R(omc)], writes=[R(Mkc, "er")])
        e_tt(kb, Mkc[:, 8:16], Mkc[:, 8:16], Mkc[:, 0:8], ALU.subtract,
             reads=[R(Mkc, "er"), R(Mkc, "mk")], writes=[R(Mkc, "er")])
        for i in range(NT):
            e_mm(kb, pd, Mkc[:, 8 + i:9 + i],
                 Mkc[:, i:17 + i:8][:, 0:2] if False else Mkc[:, i:i + 9:8],
                 reads=[R(Mkc, "mk"), R(Mkc, "er")], writes=[R("pdS")],
                 start=(i == 0), stop=(i == NT - 1), tag="dotmm")
        e_copy(kb, Ft[:, 1:3], pd, reads=[R("pdS")], writes=[R(Ft, "d")],
               engine="act")
        if stage < 4:
            continue
        e_tt(kb, w9[:, 0:1], scar[:, 2 * NCHF + t:2 * NCHF + t + 1], Ft[:, 0:1],
             ALU.mult, reads=[R(scar), R(Ft, 0)], writes=[R(w9, 0)])
        e_tt(kb, w9[:, 1:2], scar[:, 3 * NCHF + t:3 * NCHF + t + 1], Ft[:, 1:2],
             ALU.mult, reads=[R(scar), R(Ft, "d")], writes=[R(w9, 1)])
        e_tt(kb, w9[:, 2:3], scar[:, 4 * NCHF + t:4 * NCHF + t + 1], Ft[:, 2:3],
             ALU.mult, reads=[R(scar), R(Ft, "d")], writes=[R(w9, 2)])
        e_tt(kb, w9[:, 0:1], w9[:, 0:1], w9[:, 1:2], ALU.add,
             reads=[R(w9, 0), R(w9, 1)], writes=[R(w9, 0)])
        e_tt(kb, w9[:, 0:1], w9[:, 0:1], w9[:, 2:3], ALU.add,
             reads=[R(w9, 0), R(w9, 2)], writes=[R(w9, 0)])
        e_act(kb, w9[:, 3:4], w9[:, 0:1], AF.Sqrt, reads=[R(w9, 0)],
              writes=[R(w9, 3)], tag="ssqrt")
        e_ts(kb, w9[:, 3:4], w9[:, 3:4], 1e-6, ALU.add, reads=[R(w9, 3)],
             writes=[R(w9, 3)])
        e_recip(kb, w9[:, 3:4], w9[:, 3:4], reads=[R(w9, 3)], writes=[R(w9, 3)])
        e_ts(kb, w9[:, 3:4], w9[:, 3:4], 30.0, ALU.mult, scalar2=1.0,
             op1=ALU.min, reads=[R(w9, 3)], writes=[R(w9, 3)])
        e_tt(kb, w9[:, 4:5], w9[:, 3:4], w9[:, 3:4], ALU.mult,
             reads=[R(w9, 3)], writes=[R(w9, 4)])
        e_tt(kb, Ft[:, 0:1], w9[:, 4:5], w9[:, 0:1], ALU.mult,
             reads=[R(w9, 4), R(w9, 0)], writes=[R(Ft, 0)])
        e_tt(kb, w9[:, 5:6], w9[:, 3:4], scar[:, t:t + 1], ALU.mult,
             reads=[R(w9, 3), R(scar)], writes=[R(w9, 5)])
        e_tt(kb, w9[:, 6:7], w9[:, 3:4], scar[:, NCHF + t:NCHF + t + 1],
             ALU.mult, reads=[R(w9, 3), R(scar)], writes=[R(w9, 6)])
        e_mm(kb, ps_bc1[:, 0:1], cst["ones_row_f32"][:, :], w9[:, 5:6],
             reads=[R(cst["ones_row_f32"]), R(w9, 5)], writes=[R(ps_bc1)],
             tag="sabc")
        e_copy(kb, sac[:, :], ps_bc1[:, 0:1], reads=[R(ps_bc1)],
               writes=[R(sac)], engine="act")
        if stage < 5:
            continue
        e_transpose(kb, pse[:, :], Mkc[:, 8:16], identS[:, :],
                    reads=[R(Mkc, "er"), R(identS)], writes=[R(pse)])
        e_copy(kb, se8[:, :], pse[:, :], reads=[R(pse)], writes=[R(se8)],
               engine="dve")
        kb.dma("sp", "errio", errd[:, :], se8[:, :], reads=[R(se8)],
               writes=[R("d_err")], tag="errw")
        kb.dma("sp", "errio", erow[:, :],
               errd[:, :].rearrange("a b -> (a b)"),
               reads=[R("d_err")], writes=[R(erow)], tag="errr")
        e_ts(kb, erb[:, :], erow[:, :], w9[:, 6:7], ALU.mult,
             reads=[R(erow), R(w9, 6)], writes=[R(erb)])
        e_copy(kb, krb[:, :], kr[:, :], reads=[R(kr)], writes=[R(krb)])
        if stage < 6:
            continue
        for i in range(NT):
            for ch in range(2):
                pp = pout[(i * 2 + ch) % 2]
                e_mm(kb, pp[:, :], krb[0:1, i * 128:(i + 1) * 128],
                     erb[0:1, ch * 512:(ch + 1) * 512],
                     reads=[R(krb), R(erb)], writes=[R(pp)], tag="outer")
                e_stt(kb, Mst[i][:, ch * 512:(ch + 1) * 512],
                      Mst[i][:, ch * 512:(ch + 1) * 512], sac[:, :], pp[:, :],
                      ALU.mult, ALU.add,
                      reads=[R(Mst[i]), R(sac), R(pp)], writes=[R(Mst[i])],
                      tag="mupd")
                e_copy(kb, Mbf[i][:, ch * 512:(ch + 1) * 512],
                       Mst[i][:, ch * 512:(ch + 1) * 512],
                       reads=[R(Mst[i])], writes=[R(Mbf[i])],
                       engine="pool" if ch == 0 else "dve", tag="mcast")
    P.finalize()
    return P



def build_vg():
    P = Prog("VG")
    kb = P.kb
    xin = P.inp("xfm", [D, TOK], F32)
    oin = P.inp("outso", [D, TOK], BF16)
    w1in = P.inp("w1T", [D, D], F32)
    w2in = P.inp("w2T", [D, D], F32)
    xout = P.out("x7fm", [D, TOK], F32)
    cst = emit_consts(P)
    X = [P.sbuf(f"x{k}", [128, TOK], F32) for k in range(NT)]
    O = [P.sbuf(f"ob{k}", [128, TOK], BF16) for k in range(NT)]
    Hh = [P.sbuf(f"hh{k}", [128, TOK], BF16) for k in range(NT)]
    for k in range(NT):
        kb.dma("sp", f"ldx{k % 2}", X[k][:, :], xin[k * 128:(k + 1) * 128, :],
               reads=[R("dram_x")], writes=[R(X[k])])
        kb.dma("sp", f"ldo{k % 2}", O[k][:, :], oin[k * 128:(k + 1) * 128, :],
               reads=[R("dram_o")], writes=[R(O[k])])
    w1b, w1a = load_weight_fm(P, "w1T", w1in, D, D)
    w2b, w2a = load_weight_fm(P, "w2T", w2in, D, D)
    ps = [P.pshare([128, 512], i) for i in range(2)]
    for o in range(NT):
        for c in range(4):
            p = ps[(o * 4 + c) % 2]
            sl = slice(c * 512, (c + 1) * 512)
            for k in range(NT):
                e_mm(kb, p[:, :], w1a(k)[:, o * 128:(o + 1) * 128], O[k][:, sl],
                     reads=[R("s_w1T", k), R(O[k])], writes=[R(p)],
                     start=(k == 0), stop=(k == NT - 1), tag="vg1")
            e_act(kb, Hh[o][:, sl], p[:, :], AF.Silu, reads=[R(p)],
                  writes=[R(Hh[o], c)], tag="vgsi")
    for o in range(NT):
        for c in range(4):
            p = ps[(o * 4 + c) % 2]
            sl = slice(c * 512, (c + 1) * 512)
            for k in range(NT):
                e_mm(kb, p[:, :], w2a(k)[:, o * 128:(o + 1) * 128],
                     Hh[k][:, sl],
                     reads=[R("s_w2T", k), R(Hh[k], c)], writes=[R(p)],
                     start=(k == 0), stop=(k == NT - 1), tag="vg2")
            e_tt(kb, X[o][:, sl], X[o][:, sl], p[:, :], ALU.add,
                 reads=[R(X[o]), R(p)], writes=[R(X[o])], tag="vgacc")
        e_tt(kb, X[o][:, :], X[o][:, :], O[o][:, :], ALU.add,
             reads=[R(X[o]), R(O[o])], writes=[R(X[o])], tag="vgres")
        kb.dma("sp", f"stx{o % 2}", xout[o * 128:(o + 1) * 128, :], X[o][:, :],
               reads=[R(X[o])], writes=[R("d_xo", o)])
    P.finalize()
    return P


# version 04 (final)
# speedup vs baseline: 1.1516x; 1.1516x over previous
import numpy as np
import ml_dtypes
import concourse.bass as bass
import concourse.mybir as mybir
from concourse.bass_utils import run_bass_kernel_spmd

F32 = mybir.dt.float32
BF16 = mybir.dt.bfloat16
AF = mybir.ActivationFunctionType
ALU = mybir.AluOpType
AX = mybir.AxisListType
BF16NP = ml_dtypes.bfloat16

D = 1024
BATCH = 2
S = 8192
NCORE = 8
NRANK = 4
TOK = 2048
NT = 8
BLK = 128
NBLK = 16


ENGINES = ("pe", "act", "dve", "pool", "sp")
PARANOID = [False]


def R(buf, key="whole"):
    name = buf if isinstance(buf, str) else buf.name
    return (name, key)


class _Op:
    __slots__ = ("idx", "engine", "fn", "reads", "writes", "kind", "slot",
                 "hazard", "need_inc", "inc_no", "waits", "slot_idx", "tag")

    def __init__(self, idx, engine, fn, reads, writes, kind, slot, hazard, tag):
        self.idx = idx
        self.engine = engine
        self.fn = fn
        self.reads = tuple(reads)
        self.writes = tuple(writes)
        self.kind = kind
        self.slot = slot
        self.hazard = hazard
        self.need_inc = False
        self.inc_no = None
        self.slot_idx = None
        self.waits = {}
        self.tag = tag


class KB:
    def __init__(self, nc):
        self.nc = nc
        self.ops = []
        self.slot_counts = {}
        self.psum_names = set()

    def op(self, engine, fn, reads=(), writes=(), hazard=False, tag=""):
        o = _Op(len(self.ops), engine, fn, reads, writes, "eng", None, hazard, tag)
        self.ops.append(o)
        return o

    def dma(self, queue, slot, out_ap, in_ap, reads=(), writes=(), tag="",
            serialize=True, **kw):
        def fn(eng, _o=out_ap, _i=in_ap, _kw=kw):
            return eng.dma_start(out=_o, in_=_i, **_kw)
        if serialize:
            reads = tuple(reads) + (R("_slotchain_" + slot),)
            writes = tuple(writes) + (R("_slotchain_" + slot),)
        o = _Op(len(self.ops), queue, fn, reads, writes, "dma", slot, False, tag)
        o.slot_idx = self.slot_counts.get(slot, 0)
        self.slot_counts[slot] = o.slot_idx + 1
        self.ops.append(o)
        return o

    @staticmethod
    def _conflicts(store, region):
        name, key = region
        sub = store.get(name)
        if not sub:
            return []
        if key == "whole":
            return list(sub.items())
        out = []
        if key in sub:
            out.append((key, sub[key]))
        if "whole" in sub:
            out.append(("whole", sub["whole"]))
        return out

    def finalize(self, block, sem_of):
        ops = self.ops
        deps = [None] * len(ops)
        last_write = {}
        readers = {}
        for o in ops:
            mydeps = {}
            pwrites = tuple(rg for rg in o.reads if rg[0] in self.psum_names)
            owrites = tuple(o.writes) + pwrites
            for rg in o.reads:
                for _, w in self._conflicts(last_write, rg):
                    mydeps[w.idx] = w
            for wg in owrites:
                for _, w in self._conflicts(last_write, wg):
                    mydeps[w.idx] = w
                for _, rs in self._conflicts(readers, wg):
                    for rr in rs:
                        mydeps[rr.idx] = rr
            mydeps.pop(o.idx, None)
            kept = []
            for d_ in mydeps.values():
                if (not PARANOID[0] and d_.kind == "eng" and o.kind == "eng"
                        and d_.engine == o.engine == "pe"):
                    continue
                kept.append(d_)
                if d_.kind == "eng":
                    d_.need_inc = True
            deps[o.idx] = kept
            for rg in o.reads:
                readers.setdefault(rg[0], {}).setdefault(rg[1], []).append(o)
            for wg in owrites:
                name, key = wg
                lw = last_write.setdefault(name, {})
                if key == "whole":
                    lw.clear()
                    rd = readers.get(name)
                    if rd:
                        rd.clear()
                else:
                    rd = readers.get(name)
                    if rd:
                        rd.pop(key, None)
                        rd.pop("whole", None)
                lw[key] = o
        inc_counts = {e: 0 for e in ENGINES}
        for o in ops:
            if o.kind == "eng" and o.need_inc:
                inc_counts[o.engine] += 1
                o.inc_no = inc_counts[o.engine]

        def _semval(d_):
            if d_.kind == "dma":
                return ("slot:" + d_.slot, 16 * (d_.slot_idx + 1))
            return (d_.engine, d_.inc_no)

        know = {e: {} for e in ENGINES}
        opknow = {}
        for o in ops:
            k = know[o.engine]
            waits = {}
            for d_ in sorted(deps[o.idx], key=lambda x: x.idx):
                s, v = _semval(d_)
                if k.get(s, 0) >= v:
                    continue
                waits[s] = max(waits.get(s, 0), v)
                k[s] = v
                pk = opknow.get(d_.idx)
                if pk:
                    for s2, v2 in pk.items():
                        if k.get(s2, 0) < v2:
                            k[s2] = v2
            o.waits = waits
            if o.kind == "dma" or o.need_inc:
                pk = dict(k)
                if o.kind == "eng" and o.inc_no is not None:
                    pass
                opknow[o.idx] = pk

        eng_ops = {e: [o for o in ops if o.engine == e] for e in ENGINES}
        attr = {"pe": "tensor", "act": "scalar", "dve": "vector",
                "pool": "gpsimd", "sp": "sync"}

        def runner(e):
            def _run(eng):
                for o in eng_ops[e]:
                    for s in sorted(o.waits):
                        eng.wait_ge(sem_of(s), o.waits[s])
                    ins = o.fn(eng)
                    if o.kind == "dma":
                        ins.then_inc(sem_of("slot:" + o.slot), 16)
                    elif o.inc_no is not None:
                        ins.then_inc(sem_of(e), 1)
            return _run

        for e in ENGINES:
            getattr(block, attr[e])(runner(e))
        return inc_counts, dict(self.slot_counts)


class Prog:

    def __init__(self, name):
        self.nc = bass.Bass("TRN2", target_bir_lowering=False)
        self.name = name
        self.kb = KB(self.nc)
        self._stack = []
        self._sems = {}
        self.bufs = {}

    def inp(self, name, shape, dt=F32):
        return self.nc.dram_tensor(name, shape, dt, kind="ExternalInput").ap()

    def out(self, name, shape, dt=F32):
        return self.nc.dram_tensor(name, shape, dt, kind="ExternalOutput").ap()

    def dram(self, name, shape, dt=F32):
        return self.nc.dram_tensor(name, shape, dt).ap()

    def sbuf(self, name, shape, dt=F32):
        g = self.nc.sbuf_tensor(name, shape, dt)
        h = g.__enter__()
        self._stack.append(g)
        self.bufs[name] = h
        return h

    def psum(self, name, shape, dt=F32):
        g = self.nc.psum_tensor(name, shape, dt)
        h = g.__enter__()
        self._stack.append(g)
        self.bufs[name] = h
        self.kb.psum_names.add(name)
        return h

    def sshare(self, name, shape, dt=F32):
        if name not in self.bufs:
            return self.sbuf(name, list(shape), dt)
        return self.bufs[name]

    def pshare(self, shape, i=0, dt=F32):
        key = f"psS_{'x'.join(map(str, shape))}_{i}"
        if key not in self.bufs:
            return self.psum(key, list(shape), dt)
        return self.bufs[key]

    def _sem(self, name):
        if name not in self._sems:
            g = self.nc.semaphore(name.replace(":", "_"))
            h = g.__enter__()
            self._stack.append(g)
            self._sems[name] = h
        return self._sems[name]

    def finalize(self):
        needed = set(ENGINES)
        for o in self.kb.ops:
            if o.kind == "dma":
                needed.add("slot:" + o.slot)
        for n in sorted(needed):
            self._sem(n)
        bg = self.nc.Block()
        block = bg.__enter__()
        self._stack.append(bg)
        stats = self.kb.finalize(block, self._sem)
        while self._stack:
            self._stack.pop().__exit__(None, None, None)
        return stats



def e_mm(kb, out_ap, lhsT, rhs, reads, writes, start=True, stop=True,
         tile_position=None, tag=""):
    def fn(eng, _o=out_ap, _l=lhsT, _r=rhs, _s=start, _p=stop, _tp=tile_position):
        return eng.matmul(_o, _l, _r, start=_s, stop=_p, tile_position=_tp)
    return kb.op("pe", fn, reads=reads, writes=writes, tag=tag)


def e_act(kb, out_ap, in_ap, func, reads, writes, bias=None, scale=None,
          accum_out=None, hazard=False, tag=""):
    kw = {}
    if bias is not None:
        kw["bias"] = bias
    if scale is not None:
        kw["scale"] = scale
    if accum_out is not None:
        kw["accum_out"] = accum_out
        hazard = True

    def fn(eng, _o=out_ap, _i=in_ap, _f=func, _kw=kw):
        return eng.activation(_o, _i, _f, **_kw)
    return kb.op("act", fn, reads=reads, writes=writes, hazard=hazard, tag=tag)


def e_tt(kb, out_ap, in0, in1, op, reads, writes, engine="dve", tag=""):
    def fn(eng, _o=out_ap, _a=in0, _b=in1, _op=op):
        return eng.tensor_tensor(_o, _a, _b, _op)
    return kb.op(engine, fn, reads=reads, writes=writes, tag=tag)


def e_ts(kb, out_ap, in0, scalar1, op0, reads, writes, scalar2=None, op1=None,
         engine="dve", tag=""):
    def fn(eng, _o=out_ap, _i=in0, _s1=scalar1, _s2=scalar2, _o0=op0, _o1=op1):
        if _o1 is None:
            return eng.tensor_scalar(out=_o, in0=_i, scalar1=_s1, scalar2=None,
                                     op0=_o0)
        return eng.tensor_scalar(out=_o, in0=_i, scalar1=_s1, scalar2=_s2,
                                 op0=_o0, op1=_o1)
    return kb.op(engine, fn, reads=reads, writes=writes, tag=tag)


def e_stt(kb, out_ap, in0, scalar, in1, op0, op1, reads, writes, engine="dve",
          tag=""):
    def fn(eng, _o=out_ap, _i=in0, _s=scalar, _j=in1, _o0=op0, _o1=op1):
        return eng.scalar_tensor_tensor(out=_o, in0=_i, scalar=_s, in1=_j,
                                        op0=_o0, op1=_o1)
    return kb.op(engine, fn, reads=reads, writes=writes, tag=tag)


def e_copy(kb, out_ap, in_ap, reads, writes, engine="dve", tag=""):
    if engine == "act":
        def fn(eng, _o=out_ap, _i=in_ap):
            return eng.copy(_o, _i)
    else:
        def fn(eng, _o=out_ap, _i=in_ap):
            return eng.tensor_copy(_o, _i)
    return kb.op(engine, fn, reads=reads, writes=writes, tag=tag)


def e_red(kb, out_ap, in_ap, op, reads, writes, engine="dve", tag=""):
    def fn(eng, _o=out_ap, _i=in_ap, _op=op):
        return eng.tensor_reduce(_o, _i, axis=AX.X, op=_op)
    return kb.op(engine, fn, reads=reads, writes=writes, hazard=True, tag=tag)


def e_recip(kb, out_ap, in_ap, reads, writes, tag=""):
    def fn(eng, _o=out_ap, _i=in_ap):
        return eng.reciprocal(_o, _i)
    return kb.op("dve", fn, reads=reads, writes=writes, tag=tag)


def e_transpose(kb, out_ap, in_ap, ident, reads, writes, tag=""):
    def fn(eng, _o=out_ap, _i=in_ap, _id=ident):
        return eng.transpose(_o, _i, _id)
    return kb.op("pe", fn, reads=reads, writes=writes, tag=tag)


def e_memset(kb, ap, val, writes, engine="pool", tag=""):
    def fn(eng, _a=ap, _v=val):
        return eng.memset(_a, _v)
    return kb.op(engine, fn, reads=(), writes=writes, tag=tag)


def e_iota(kb, ap, pattern, writes, base=0, channel_multiplier=0, tag=""):
    def fn(eng, _a=ap, _p=pattern, _b=base, _c=channel_multiplier):
        return eng.iota(_a, _p, base=_b, channel_multiplier=_c,
                        allow_small_or_imprecise_dtypes=True)
    return kb.op("pool", fn, reads=(), writes=writes, tag=tag)



def _own_rows(r):
    idx = []
    for j in range(NBLK):
        blk = 4 * j + r
        idx.extend(range(blk * BLK, (blk + 1) * BLK))
    return np.array(idx)


_OWN = [_own_rows(r) for r in range(NRANK)]


def _gmaps():
    gmap = np.zeros((128, 4), np.float32)
    for p in range(128):
        gmap[p, p // 32] = 1.0
    return gmap, np.ascontiguousarray(gmap.T)


def _kron(n):
    k = np.zeros((n, n * 128), np.float32)
    for e in range(n):
        k[e, e * 128:(e + 1) * 128] = 1.0
    return k


def _cmask_for(r):
    tri = np.tril(np.ones((128, 128), np.float32)).T
    m = np.zeros((128, 4 * 128), np.float32)
    for rp in range(4):
        if rp < r:
            m[:, rp * 128:(rp + 1) * 128] = 1.0
        elif rp == r:
            m[:, rp * 128:(rp + 1) * 128] = tri
    return m


def _c(a):
    return np.ascontiguousarray(a)


_DBG = None


def _dbg_note(name, xcur):
    if _DBG is not None:
        _DBG.append((name, [np.array(x) for x in xcur]))


def kernel(**inputs):
    f = {k: np.asarray(v) for k, v in inputs.items()}
    x0 = f["L0_attn_a_x"]
    gmap, gmapT = _gmaps()
    kron8 = _kron(8)
    kron32 = _kron(32)
    cores = list(range(NCORE))

    def run(P, ins_list):
        res = run_bass_kernel_spmd(P.nc, ins_list, core_ids=cores)
        return res.results

    def percore(fn):
        return [fn(c // 4, c % 4) for c in cores]

    P = _get_prog("A0", build_A0)
    r0 = run(P, percore(lambda g, r: dict(xfm=_c(x0[g][_OWN[r]].T))))
    parts = [np.stack([np.asarray(r0[g * 4 + rr]["part"]) for rr in range(4)])
             for g in range(2)]
    P = _get_prog("A1", build_A1)
    r1 = run(P, percore(lambda g, r: dict(
        tfm=_c(np.asarray(r0[g * 4 + r]["tfm"])), parts=parts[g],
        gamma=f["L0_attn_a_gn_gamma"], beta=f["L0_attn_a_gn_beta"],
        wqkv=f["L0_attn_a_W_qkv"], bqkv=f["L0_attn_a_b_qkv"],
        gmap=gmap, gmapT=gmapT)))

    def gather_kvn(rr):
        kf, vf, nf = [], [], []
        for g in range(2):
            ks = [np.asarray(rr[g * 4 + j]["kfm"]) for j in range(4)]
            vs = [np.asarray(rr[g * 4 + j]["vtm"]) for j in range(4)]
            ns = [np.asarray(rr[g * 4 + j]["norms"]) for j in range(4)]
            kf.append(_c(np.concatenate(ks, axis=1)))
            vf.append(_c(np.concatenate(vs, axis=0)))
            nf.append(_c(np.concatenate(ns, axis=1)))
        return kf, vf, nf

    kf, vf, nf = gather_kvn(r1)
    P = _get_prog("ATTA", lambda: build_attn_mh("ATTA", True))
    xcur = [np.asarray(r0[c]["tfm"]) * 0 for c in cores]
    r2 = run(P, percore(lambda g, r: dict(
        xfm=_c(x0[g][_OWN[r]].T), qfm=np.asarray(r1[g * 4 + r]["qfm"]),
        kfull=kf[g], vfull=vf[g], norms4=nf[g],
        wo=f["L0_attn_a_W_o"], bo=f["L0_attn_a_b_o"],
        tres=_c(np.asarray(r0[g * 4 + r]["tfm"])))))
    xcur = [np.asarray(r2[c]["x1fm"]) for c in cores]
    _dbg_note("ATTA", xcur)
    P = _get_prog("FFN0", build_ffn0)
    ffn0_w = dict(gwT=_c(f["L0_ffn_gate_w"].T),
                  ebias=_c(f["L0_ffn_expert_bias"].reshape(1, 8)),
                  w1=f["L0_ffn_w1"], w2=f["L0_ffn_w2"], w3=f["L0_ffn_w3"],
                  kron8=kron8)
    r3 = run(P, [dict(xfm=xcur[c], **ffn0_w) for c in cores])
    xcur = [np.asarray(r3[c]["x2fm"]) for c in cores]
    _dbg_note("FFN0a", xcur)
    wqkv_b = _c(np.concatenate([f["L0_attn_b_Wq"].T, f["L0_attn_b_Wk"].T,
                                f["L0_attn_b_Wv"].T], axis=1))
    P = _get_prog("QKVP", lambda: build_qkv_plain("QKVP", False))
    r4 = run(P, [dict(xfm=xcur[c], wqkv=wqkv_b) for c in cores])
    kf, vf, nf = gather_kvn(r4)
    P = _get_prog("ATTB", lambda: build_attn_mh("ATTB", False))
    r5 = run(P, percore(lambda g, r: dict(
        xfm=xcur[g * 4 + r], qfm=np.asarray(r4[g * 4 + r]["qfm"]),
        kfull=kf[g], vfull=vf[g], norms4=nf[g],
        wo=_c(f["L0_attn_b_Wo"].T))))
    xcur = [np.asarray(r5[c]["x1fm"]) for c in cores]
    _dbg_note("ATTB", xcur)
    P = _get_prog("FFN0", build_ffn0)
    r6 = run(P, [dict(xfm=xcur[c], **ffn0_w) for c in cores])
    xcur = [np.asarray(r6[c]["x2fm"]) for c in cores]
    _dbg_note("FFN0b", xcur)
    wqkv_c = _c(np.concatenate([f["L1_attn_a_Wq"].T, f["L1_attn_a_Wk"].T,
                                f["L1_attn_a_Wv"].T], axis=1))
    P = _get_prog("QKVW", lambda: build_qkv_plain("QKVW", True))
    r7 = run(P, [dict(xfm=xcur[c], wqkv=wqkv_c) for c in cores])
    kf, vf, nf = gather_kvn(r7)
    P = _get_prog("ATTC", build_attn_causal)
    r8 = run(P, percore(lambda g, r: dict(
        xfm=xcur[g * 4 + r], qfm=np.asarray(r7[g * 4 + r]["qfm"]),
        kfull=kf[g], vfull=vf[g], norms4=nf[g], cmask=_cmask_for(r))))
    xcur = [np.asarray(r8[c]["x5fm"]) for c in cores]
    _dbg_note("ATTC", xcur)
    P = _get_prog("FFN1", build_ffn1)
    ffn1_w = dict(gwT=_c(f["L1_ffn_gate_weight"].T),
                  gbias=_c(f["L1_ffn_gate_bias"].reshape(1, 32)),
                  w_gate=f["L1_ffn_w_gate"], w_up=f["L1_ffn_w_up"],
                  w_down=f["L1_ffn_w_down"],
                  sh_gate=f["L1_ffn_shared_w_gate"],
                  sh_up=f["L1_ffn_shared_w_up"],
                  sh_down=f["L1_ffn_shared_w_down"],
                  an_gate=f["L1_ffn_anchor_w_gate"],
                  an_up=f["L1_ffn_anchor_w_up"],
                  an_down=f["L1_ffn_anchor_w_down"], kron32=kron32)
    r9 = run(P, [dict(xfm=xcur[c], **ffn1_w) for c in cores])
    xcur = [np.asarray(r9[c]["xofm"]) for c in cores]
    _dbg_note("FFN1a", xcur)
    P = _get_prog("SPREP", build_scanprep)
    r10 = run(P, [dict(xfm=xcur[c], eta_w=f["L1_attn_b_eta_w"],
                       eta_b=f["L1_attn_b_eta_b"],
                       alpha_w=f["L1_attn_b_alpha_w"],
                       alpha_b=f["L1_attn_b_alpha_b"],
                       gate_w=f["L1_attn_b_gate_w"],
                       gate_b=f["L1_attn_b_gate_b"]) for c in cores])
    NCH = 128
    w1T = _c(f["L1_attn_b_vg_w1"].T)
    w2T = _c(f["L1_attn_b_vg_w2"].T)
    scan_ins = []
    for g in range(2):
        t6g = np.zeros((D, S), dtype=BF16NP)
        kmr = np.zeros((NCH, D), np.float32)
        sca = np.zeros((1, 5 * NCH), np.float32)
        for r in range(4):
            rr = r10[g * 4 + r]
            t6g[:, _OWN[r]] = np.asarray(rr["t6fm"])
            km = np.asarray(rr["kmean"])
            sc = np.asarray(rr["sca"])
            for j in range(32):
                ct = (4 * (j // 2) + r) * 2 + (j % 2)
                kmr[ct] = km[j]
                for row in range(5):
                    sca[0, row * NCH + ct] = sc[0, row * 32 + j]
        scan_ins.append(dict(t6g=_c(t6g), kmr=kmr, kmcT=_c(kmr.T), sca=sca,
                             m0T=_c(f["L1_attn_b_M_init"].T), w1T=w1T,
                             w2T=w2T))
    P = _get_prog("SCAN", build_scan)
    r11 = run(P, [scan_ins[c // 4] for c in cores])
    outsg = [np.asarray(r11[g * 4]["outsg"]) for g in range(2)]
    P = _get_prog("VG", build_vg)
    r12 = run(P, percore(lambda g, r: dict(
        xfm=xcur[g * 4 + r], outso=_c(outsg[g][:, _OWN[r]]),
        w1T=w1T, w2T=w2T)))
    xcur = [np.asarray(r12[c]["x7fm"]) for c in cores]
    _dbg_note("VG", xcur)
    P = _get_prog("FFN1", build_ffn1)
    r13 = run(P, [dict(xfm=xcur[c], **ffn1_w) for c in cores])
    out = np.zeros((BATCH, S, D), np.float32)
    for c in cores:
        g, r = c // 4, c % 4
        out[g][_OWN[r]] = np.asarray(r13[c]["xofm"]).T
    return out



def emit_ln(P, X, outs, ntok, cst, pref, eps=1e-5, xkey=None, chunk_cb=None,
            first_rot=False):
    kb = P.kb
    nc2 = 512
    nchunk = ntok // nc2
    sq = [P.sshare(f"lnS_sq{i}", [128, nc2]) for i in range(2)]
    mrow = P.sshare(f"lnS_m{ntok}", [1, ntok])
    ssrow = P.sshare(f"lnS_ss{ntok}", [1, ntok])
    rr = P.sshare(f"lnS_rr{ntok}", [1, ntok])
    mr2 = P.sshare(f"lnS_mr{ntok}", [1, ntok])
    ps_red = P.pshare([1, nc2], 0)
    ps_bc = P.pshare([128, nc2], 0)
    ps_bc2 = P.pshare([128, nc2], 1)
    ones = cst["ones_col_f32"]
    ones_row = cst["ones_row_f32"]
    xr = (lambda k: R(X[k])) if xkey is None else (lambda k: R(X[k], xkey))

    for c in range(nchunk):
        sl = slice(c * nc2, (c + 1) * nc2)
        for k in range(NT):
            e_mm(kb, ps_red[:, :], ones[:, :], X[k][:, sl],
                 reads=[R(ones), xr(k)], writes=[R(ps_red)],
                 start=(k == 0), stop=(k == NT - 1), tag=f"{pref}mean{c}")
        e_copy(kb, mrow[:, sl], ps_red[:, :], reads=[R(ps_red)],
               writes=[R(mrow, c)], engine="act")
        for k in range(NT):
            b = sq[k % 2]
            e_act(kb, b[:, :], X[k][:, sl], AF.Square,
                  reads=[xr(k)], writes=[R(b)], tag=f"{pref}sq")
            e_mm(kb, ps_red[:, :], ones[:, :], b[:, :],
                 reads=[R(ones), R(b)], writes=[R(ps_red)],
                 start=(k == 0), stop=(k == NT - 1), tag=f"{pref}ssmm{c}")
        e_copy(kb, ssrow[:, sl], ps_red[:, :], reads=[R(ps_red)],
               writes=[R(ssrow, c)], engine="act")
    e_ts(kb, mrow[:, :], mrow[:, :], 1.0 / D, ALU.mult,
         reads=[R(mrow, c) for c in range(nchunk)], writes=[R(mrow)])
    e_ts(kb, ssrow[:, :], ssrow[:, :], 1.0 / D, ALU.mult,
         reads=[R(ssrow, c) for c in range(nchunk)], writes=[R(ssrow)])
    e_tt(kb, rr[:, :], mrow[:, :], mrow[:, :], ALU.mult,
         reads=[R(mrow)], writes=[R(rr)])
    e_tt(kb, rr[:, :], ssrow[:, :], rr[:, :], ALU.subtract,
         reads=[R(ssrow), R(rr)], writes=[R(rr)])
    e_ts(kb, rr[:, :], rr[:, :], float(eps), ALU.add,
         reads=[R(rr)], writes=[R(rr)])
    e_act(kb, rr[:, :], rr[:, :], AF.Sqrt,
          reads=[R(rr)], writes=[R(rr)], tag=f"{pref}sqrt")
    e_recip(kb, rr[:, :], rr[:, :], reads=[R(rr)], writes=[R(rr)])
    e_tt(kb, mr2[:, :], mrow[:, :], rr[:, :], ALU.mult,
         reads=[R(mrow), R(rr)], writes=[R(mr2)])
    for c in range(nchunk):
        sl = slice(c * nc2, (c + 1) * nc2)
        e_mm(kb, ps_bc[:, :], ones_row[:, :], rr[:, sl],
             reads=[R(ones_row), R(rr)], writes=[R(ps_bc)], tag=f"{pref}bc")
        e_mm(kb, ps_bc2[:, :], ones_row[:, :], mr2[:, sl],
             reads=[R(ones_row), R(mr2)], writes=[R(ps_bc2)], tag=f"{pref}bc2")
        for k in range(NT):
            first = outs[0][k]
            fsl = slice(0, nc2) if first_rot else sl
            fkey = "whole" if first_rot else c
            e_tt(kb, first[:, fsl], X[k][:, sl], ps_bc[:, :], ALU.mult,
                 reads=[xr(k), R(ps_bc)], writes=[R(first, fkey)])
            e_tt(kb, first[:, fsl], first[:, fsl], ps_bc2[:, :], ALU.subtract,
                 reads=[R(first, fkey), R(ps_bc2)], writes=[R(first, fkey)])
            for extra in outs[1:]:
                e_copy(kb, extra[k][:, sl], first[:, fsl],
                       reads=[R(first, fkey)], writes=[R(extra[k], c)])
        if chunk_cb is not None:
            chunk_cb(c, outs[0])


def emit_consts(P):
    cst = {}
    cst["ones_col_f32"] = P.sbuf("c_ones_col", [128, 1], F32)
    cst["ones_row_f32"] = P.sbuf("c_ones_row", [1, 128], F32)
    cst["ones_col_bf16"] = P.sbuf("c_ones_colb", [128, 1], BF16)
    e_memset(P.kb, cst["ones_col_f32"][:, :], 1.0, writes=[R(cst["ones_col_f32"])])
    e_memset(P.kb, cst["ones_row_f32"][:, :], 1.0, writes=[R(cst["ones_row_f32"])])
    e_memset(P.kb, cst["ones_col_bf16"][:, :], 1.0, writes=[R(cst["ones_col_bf16"])])
    return cst



def build_A0():
    P = Prog("A0")
    xin = P.inp("xfm", [D, TOK], F32)
    tout = P.out("tfm", [D, TOK], F32)
    pout = P.out("part", [128, 16], F32)
    kb = P.kb
    X = [P.sbuf(f"x{k}", [128, TOK], F32) for k in range(NT)]
    T = [P.sbuf(f"t{k}", [128, TOK], F32) for k in range(NT)]
    part = P.sbuf("partb", [128, 16], F32)
    sqs = P.sbuf("sqs", [128, TOK], F32)
    cst = emit_consts(P)
    for k in range(NT):
        kb.dma("sp", "ldx", X[k][:, :], xin[k * 128:(k + 1) * 128, :],
               reads=[R("dram_xfm")], writes=[R(X[k])])
    emit_ln(P, X, [T], TOK, cst, "ln0")
    for k in range(NT):
        e_red(kb, part[:, 2 * k:2 * k + 1], T[k][:, :], ALU.add,
              reads=[R(T[k])], writes=[R(part, 2 * k)])
        e_act(kb, sqs[:, :], T[k][:, :], AF.Square,
              reads=[R(T[k])], writes=[R(sqs)], tag="partsq")
        e_red(kb, part[:, 2 * k + 1:2 * k + 2], sqs[:, :], ALU.add,
              reads=[R(sqs)], writes=[R(part, 2 * k + 1)])
    for k in range(NT):
        kb.dma("sp", "stt", tout[k * 128:(k + 1) * 128, :], T[k][:, :],
               reads=[R(T[k])], writes=[R("dram_tfm", k)])
    kb.dma("sp", "stp", pout, part[:, :],
           reads=[R(part, 2 * k) for k in range(NT)] +
                 [R(part, 2 * k + 1) for k in range(NT)],
           writes=[R("dram_part")])
    P.finalize()
    return P


_PROG_CACHE = {}


def _get_prog(name, builder):
    if name not in _PROG_CACHE:
        _PROG_CACHE[name] = builder()
    return _PROG_CACHE[name]



def load_weight_fm(P, name, dram_ap, din, dout, queue="pool", row0=0):
    nk = din // 128
    w = P.sbuf("s_" + name, [128, nk * dout], BF16)
    src3 = dram_ap[row0:row0 + din, :].rearrange("(k p) h -> p k h", p=128)
    dst3 = w[:, :].rearrange("p (k h) -> p k h", h=dout)
    P.kb.dma(queue, name, dst3, src3, reads=[R("dram_" + name)],
             writes=[R(w)], tag=name, serialize=False)
    return w, (lambda k, _w=w, _d=dout: _w[:, k * _d:(k + 1) * _d])


def load_cols(P, name, dram_1d, n, queue="sp"):
    v = P.sbuf("s_" + name, [128, n // 128], F32)
    for k in range(n // 128):
        P.kb.dma(queue, name, v[:, k:k + 1], dram_1d[k * 128:(k + 1) * 128],
                 reads=[R("dram_" + name)], writes=[R(v, k)], tag=name)
    return v



def emit_linear_fm(P, pref, Xb, wacc, wname, OUT, n_out, ntok, bias_cols=None,
                   bias_off=0, w_off=0, nk=NT, out_key=None, x_key=None,
                   accum_into=None):
    kb = P.kb
    nc2 = 512
    nchunk = ntok // nc2
    no = n_out // 128
    ps = [P.pshare([128, nc2], i) for i in range(2)]
    for o in range(no):
        for c in range(nchunk):
            p = ps[(o * nchunk + c) % 2]
            sl = slice(c * nc2, (c + 1) * nc2)
            for k in range(nk):
                e_mm(kb, p[:, :], wacc(k)[:, w_off + o * 128:w_off + (o + 1) * 128],
                     Xb[k][:, sl],
                     reads=[R(wname, k), R(Xb[k]) if x_key is None else R(Xb[k], x_key)],
                     writes=[R(p)],
                     start=(k == 0), stop=(k == nk - 1), tag=f"{pref}mm")
            dst = OUT[o]
            okey = (o, c) if out_key is None else out_key
            if accum_into is not None:
                if bias_cols is not None:
                    e_stt(kb, dst[:, sl], p[:, :], bias_cols[:, bias_off + o:bias_off + o + 1],
                          dst[:, sl], ALU.add, ALU.add,
                          reads=[R(p), R(bias_cols), R(dst, okey)],
                          writes=[R(dst, okey)], tag=f"{pref}acc")
                else:
                    e_tt(kb, dst[:, sl], dst[:, sl], p[:, :], ALU.add,
                         reads=[R(p), R(dst, okey)], writes=[R(dst, okey)],
                         tag=f"{pref}acc")
            elif bias_cols is not None:
                e_act(kb, dst[:, sl], p[:, :], AF.Identity,
                      bias=bias_cols[:, bias_off + o:bias_off + o + 1],
                      reads=[R(p), R(bias_cols)], writes=[R(dst, okey)],
                      tag=f"{pref}ev")
            else:
                e_copy(kb, dst[:, sl], p[:, :], reads=[R(p)],
                       writes=[R(dst, okey)], engine="act", tag=f"{pref}ev")


def emit_linear_tm(P, pref, Xb, wacc, wname, OUT_tm, n_out, ntok, w_off=0,
                   nk=NT, bias_row_bc=None, out_dram=None, dram_name=None,
                   store_queue="sp"):
    kb = P.kb
    nts = ntok // 128
    no2 = n_out // 512
    ps = [P.pshare([128, 512], i) for i in range(2)]
    for ts in range(nts):
        for oc in range(no2):
            p = ps[(ts * no2 + oc) % 2]
            for k in range(nk):
                e_mm(kb, p[:, :], Xb[k][:, ts * 128:(ts + 1) * 128],
                     wacc(k)[:, w_off + oc * 512:w_off + (oc + 1) * 512],
                     reads=[R(Xb[k]), R(wname, k)], writes=[R(p)],
                     start=(k == 0), stop=(k == nk - 1), tag=f"{pref}mm")
            dst = OUT_tm[ts]
            sl = slice(oc * 512, (oc + 1) * 512)
            if bias_row_bc is not None:
                e_tt(kb, dst[:, sl], p[:, :], bias_row_bc[:, sl], ALU.add,
                     reads=[R(p), R(bias_row_bc)], writes=[R(dst, oc)],
                     tag=f"{pref}ev")
            else:
                e_copy(kb, dst[:, sl], p[:, :], reads=[R(p)],
                       writes=[R(dst, oc)], engine="act", tag=f"{pref}ev")
        if out_dram is not None:
            kb.dma(store_queue, f"{pref}_st", out_dram[ts * 128:(ts + 1) * 128, :],
                   dst[:, :], reads=[R(dst, oc2) for oc2 in range(no2)],
                   writes=[R(dram_name, ts)], tag=f"{pref}st")


def emit_headnorm_max(P, pref, Qb, n_heads, ntok, cst, norms_out, norms_row,
                      sqs):
    kb = P.kb
    nc2 = 512
    nchunk = ntok // nc2
    ps_n = P.pshare([1, nc2], 0)
    nr = P.sbuf(f"{pref}_nr", [1, ntok], F32)
    ones = cst["ones_col_f32"]
    for h in range(n_heads):
        kt = h // 2
        base = (h % 2) * 64
        e_act(kb, sqs[:, :], Qb[kt][:, :], AF.Square,
              reads=[R(Qb[kt])], writes=[R(sqs)], tag=f"{pref}sq")
        for c in range(nchunk):
            e_mm(kb, ps_n[:, :], ones[base:base + 64, :],
                 sqs[base:base + 64, c * nc2:(c + 1) * nc2],
                 reads=[R(ones), R(sqs)], writes=[R(ps_n)], tag=f"{pref}nmm")
            e_copy(kb, nr[:, c * nc2:(c + 1) * nc2], ps_n[:, :],
                   reads=[R(ps_n)], writes=[R(nr, c)], engine="act")
        col = norms_row * n_heads + h
        e_red(kb, norms_out[0:1, col:col + 1], nr[:, :], ALU.max,
              reads=[R(nr, c) for c in range(nchunk)],
              writes=[R(norms_out, (norms_row, h))], tag=f"{pref}max")


def build_A1(dbg=False):
    P = Prog("A1")
    kb = P.kb
    tin = P.inp("tfm", [D, TOK], F32)
    parts = P.inp("parts", [NRANK, 128, 16], F32)
    gma = P.inp("gamma", [D], F32)
    bta = P.inp("beta", [D], F32)
    wqkv = P.inp("wqkv", [D, 3 * D], F32)
    bqkv = P.inp("bqkv", [3 * D], F32)
    gmap = P.inp("gmap", [128, 4], F32)
    gmapT = P.inp("gmapT", [4, 128], F32)
    qout = P.out("qfm", [D, TOK], BF16)
    kout = P.out("kfm", [D, TOK], BF16)
    vout = P.out("vtm", [TOK, D], BF16)
    nout = P.out("norms", [1, 32], F32)

    cst = emit_consts(P)
    T = [P.sbuf(f"t{k}", [128, TOK], F32) for k in range(2)]
    H = [P.sbuf(f"h{k}", [128, TOK], BF16) for k in range(NT)]
    def Tk(k):
        return T[k % 2]

    def load_T(k):
        kb.dma("sp", f"ldt{k % 2}", Tk(k)[:, :], tin[k * 128:(k + 1) * 128, :],
               reads=[R("dram_tfm")], writes=[R(Tk(k))])
    pt = [P.sbuf(f"pt{i}", [128, 16], F32) for i in range(NRANK)]
    sgm = P.sbuf("sgmap", [128, 4], F32)
    sgmT = P.sbuf("sgmapT", [4, 128], F32)
    kb.dma("sp", "ldgm", sgm[:, :], gmap, reads=[R("dram_gmap")], writes=[R(sgm)])
    kb.dma("sp", "ldgmT", sgmT[:, :], gmapT, reads=[R("dram_gmapT")], writes=[R(sgmT)])
    for i in range(NRANK):
        kb.dma("sp", f"ldpt{i}", pt[i][:, :], parts[i],
               reads=[R("dram_parts")], writes=[R(pt[i])])
    for i in range(1, NRANK):
        e_tt(kb, pt[0][:, :], pt[0][:, :], pt[i][:, :], ALU.add,
             reads=[R(pt[0]), R(pt[i])], writes=[R(pt[0])])
    gam = load_cols(P, "gam", gma, D)
    bet = load_cols(P, "bet", bta, D)
    ps_g = P.psum("ps_g", [4, 16], F32)
    ps_h = P.psum("ps_h", [128, 16], F32)
    gv = P.sbuf("gv", [4, 16], F32)
    mr = P.sbuf("mrst", [4, 16], F32)
    gcol = P.sbuf("gcol", [128, 16], F32)
    NTOT = float(32 * S)
    for k in range(NT):
        e_mm(kb, ps_g[:, 2 * k:2 * k + 2], sgm[:, :], pt[0][:, 2 * k:2 * k + 2],
             reads=[R(sgm), R(pt[0])], writes=[R(ps_g)])
    e_copy(kb, gv[:, :], ps_g[:, :], reads=[R(ps_g)],
           writes=[R(gv)], engine="act")
    e_ts(kb, gv[:, :], gv[:, :], 1.0 / NTOT, ALU.mult, reads=[R(gv)],
         writes=[R(gv)])
    e_copy(kb, mr[:, 0:8], gv[:, 0:16:2], reads=[R(gv)], writes=[R(mr, "m")])
    e_tt(kb, mr[:, 8:16], mr[:, 0:8], mr[:, 0:8], ALU.mult,
         reads=[R(mr, "m")], writes=[R(mr, "v")])
    e_tt(kb, mr[:, 8:16], gv[:, 1:16:2], mr[:, 8:16], ALU.subtract,
         reads=[R(gv), R(mr, "v")], writes=[R(mr, "v")])
    e_ts(kb, mr[:, 8:16], mr[:, 8:16], 1e-6, ALU.add,
         reads=[R(mr, "v")], writes=[R(mr, "v")])
    e_act(kb, mr[:, 8:16], mr[:, 8:16], AF.Sqrt, reads=[R(mr, "v")],
          writes=[R(mr, "v")], tag="gnsqrt")
    e_recip(kb, mr[:, 8:16], mr[:, 8:16], reads=[R(mr, "v")], writes=[R(mr, "v")])
    e_mm(kb, ps_h[:, :], sgmT[:, :], mr[:, :],
         reads=[R(sgmT), R(mr, "m"), R(mr, "v")], writes=[R(ps_h)])
    e_copy(kb, gcol[:, :], ps_h[:, :], reads=[R(ps_h)], writes=[R(gcol)],
           engine="act")
    AB = P.sbuf("ABc", [128, 16], F32)
    for k in range(NT):
        load_T(k)
        e_tt(kb, AB[:, 2 * k:2 * k + 1], gam[:, k:k + 1], gcol[:, 8 + k:9 + k],
             ALU.mult, reads=[R(gam), R(gcol)], writes=[R(AB, 2 * k)])
        e_tt(kb, AB[:, 2 * k + 1:2 * k + 2], gcol[:, k:k + 1],
             AB[:, 2 * k:2 * k + 1], ALU.mult,
             reads=[R(gcol), R(AB, 2 * k)], writes=[R(AB, 2 * k + 1)])
        e_tt(kb, AB[:, 2 * k + 1:2 * k + 2], bet[:, k:k + 1],
             AB[:, 2 * k + 1:2 * k + 2], ALU.subtract,
             reads=[R(bet), R(AB, 2 * k + 1)], writes=[R(AB, 2 * k + 1)])
        e_ts(kb, H[k][:, :], Tk(k)[:, :], AB[:, 2 * k:2 * k + 1], ALU.mult,
             scalar2=AB[:, 2 * k + 1:2 * k + 2], op1=ALU.add,
             reads=[R(Tk(k)), R(AB, 2 * k), R(AB, 2 * k + 1)],
             writes=[R(H[k])], tag="hmk")
    wq, wqa = load_weight_fm(P, "wqkv", wqkv, D, 3 * D)
    bq = load_cols(P, "bq", bqkv, 3 * D)
    Q = [P.sbuf(f"q{k}", [128, TOK], BF16) for k in range(NT)]
    Kf = [P.sbuf(f"kf{k}", [128, TOK], BF16) for k in range(NT)]
    emit_linear_fm(P, "q", H, wqa, "wqkv", Q, D, TOK, bias_cols=bq, bias_off=0,
                   w_off=0)
    emit_linear_fm(P, "k", H, wqa, "wqkv", Kf, D, TOK, bias_cols=bq, bias_off=8,
                   w_off=D)
    bvb = P.sbuf("bvb", [128, D], F32)
    ps_bv = P.pshare([128, 512], 0)
    brow = P.sbuf("brow", [1, D], F32)
    kb.dma("sp", "ldbv", brow[:, :], bqkv[2 * D:3 * D],
           reads=[R("dram_bqkv")], writes=[R(brow)])
    for c in range(2):
        e_mm(kb, ps_bv[:, :], cst["ones_row_f32"][:, :],
             brow[:, c * 512:(c + 1) * 512],
             reads=[R(cst["ones_row_f32"]), R(brow)], writes=[R(ps_bv)])
        e_copy(kb, bvb[:, c * 512:(c + 1) * 512], ps_bv[:, :],
               reads=[R(ps_bv)], writes=[R(bvb, c)], engine="act")
    Vt2 = [P.sbuf(f"vt{t}", [128, D], BF16) for t in range(2)]
    Vt = [Vt2[t % 2] for t in range(TOK // 128)]
    emit_linear_tm(P, "v", H, wqa, "wqkv", Vt, D, TOK, w_off=2 * D,
                   bias_row_bc=bvb, out_dram=vout, dram_name="dram_vtm")
    norms = P.sbuf("normsb", [1, 32], F32)
    sqs = P.sbuf("sqs", [128, TOK], F32)
    emit_headnorm_max(P, "nq", Q, 16, TOK, cst, norms, 0, sqs)
    emit_headnorm_max(P, "nk", Kf, 16, TOK, cst, norms, 1, sqs)
    kb.dma("sp", "stn", nout, norms[:, :],
           reads=[R(norms, (r_, h)) for r_ in range(2) for h in range(16)],
           writes=[R("dram_norms")])
    if dbg:
        ptdb = P.out("ptdb", [128, 16], F32)
        gvdb = P.out("gvdb", [4, 16], F32)
        mrdb = P.out("mrdb", [4, 16], F32)
        kb.dma("sp", "stpt", ptdb, pt[0][:, :], reads=[R(pt[0])], writes=[R("d_pt")])
        kb.dma("sp", "stgv", gvdb, gv[:, :], reads=[R(gv)], writes=[R("d_gv")])
        kb.dma("sp", "stmr", mrdb, mr[:, :], reads=[R(mr, "m"), R(mr, "v")], writes=[R("d_mr")])
        hdb = P.out("hdb", [D, TOK], BF16)
        gdb = P.out("gdb", [128, 16], F32)
        adb = P.out("adb", [128, 16], F32)
        for k in range(NT):
            kb.dma("sp", "sthd", hdb[k * 128:(k + 1) * 128, :], H[k][:, :],
                   reads=[R(H[k])], writes=[R("dram_hdb", k)])
        kb.dma("sp", "stgd", gdb, gcol[:, :], reads=[R(gcol)], writes=[R("d_gdb")])
        kb.dma("sp", "stad", adb, AB[:, :],
               reads=[R(AB, j) for j in range(16)], writes=[R("d_adb")])
    for k in range(NT):
        kb.dma("sp", "stq", qout[k * 128:(k + 1) * 128, :], Q[k][:, :],
               reads=[R(Q[k], (o, c)) for o in [k] for c in range(4)],
               writes=[R("dram_qfm", k)])
        kb.dma("sp", "stk", kout[k * 128:(k + 1) * 128, :], Kf[k][:, :],
               reads=[R(Kf[k], (o, c)) for o in [k] for c in range(4)],
               writes=[R("dram_kfm", k)])
    P.finalize()
    return P



def build_attn_mh(name, with_bias):
    QK_SCALE = 1.0 / 8.0
    P = Prog(name)
    kb = P.kb
    xin = P.inp("xfm", [D, TOK], F32)
    trin = P.inp("tres", [D, TOK], F32) if with_bias else None
    qin = P.inp("qfm", [D, TOK], BF16)
    kin = P.inp("kfull", [D, S], BF16)
    vin = P.inp("vfull", [S, D], BF16)
    nin = P.inp("norms4", [1, 4 * 32], F32)
    woin = P.inp("wo", [D, D], F32)
    boin = P.inp("bo", [D], F32) if with_bias else None
    xout = P.out("x1fm", [D, TOK], F32)

    cst = emit_consts(P)
    X = [P.sbuf(f"x{k}", [128, TOK], F32) for k in range(NT)]
    Q = [P.sbuf(f"q{k}", [128, TOK], BF16) for k in range(NT)]
    O = [P.sbuf(f"o{k}", [128, TOK], BF16) for k in range(NT)]
    for k in range(NT):
        kb.dma("sp", "ldx", X[k][:, :], xin[k * 128:(k + 1) * 128, :],
               reads=[R("dram_xfm")], writes=[R(X[k])])
        kb.dma("sp", "ldq", Q[k][:, :], qin[k * 128:(k + 1) * 128, :],
               reads=[R("dram_qfm")], writes=[R(Q[k])])
    nrow = P.sbuf("nrow", [1, 128], F32)
    nmax = P.sbuf("nmax", [1, 32], F32)
    crow = P.sbuf("crow", [1, 16], F32)
    ccol = P.sbuf("ccol", [128, 16], F32)
    kb.dma("sp", "ldn", nrow[:, :], nin, reads=[R("dram_n")], writes=[R(nrow)])
    def fmax(out, a, b, reads, writes):
        def fn(eng, _o=out, _a=a, _b=b):
            return eng.tensor_max(_o, _a, _b)
        return kb.op("dve", fn, reads=reads, writes=writes)
    fmax(nmax[:, :], nrow[:, 0:32], nrow[:, 32:64], [R(nrow)], [R(nmax)])
    fmax(nmax[:, :], nmax[:, :], nrow[:, 64:96], [R(nmax), R(nrow)], [R(nmax)])
    fmax(nmax[:, :], nmax[:, :], nrow[:, 96:128], [R(nmax), R(nrow)], [R(nmax)])
    e_act(kb, nmax[:, :], nmax[:, :], AF.Sqrt, reads=[R(nmax)], writes=[R(nmax)],
          tag="csqrt")
    e_tt(kb, crow[:, :], nmax[:, 0:16], nmax[:, 16:32], ALU.mult,
         reads=[R(nmax)], writes=[R(crow)])
    e_ts(kb, crow[:, :], crow[:, :], -QK_SCALE, ALU.mult,
         reads=[R(crow)], writes=[R(crow)])
    ps_cc = P.pshare([128, 512], 0)
    for h in range(16):
        e_mm(kb, ps_cc[:, 0:1], cst["ones_row_f32"][:, :], crow[:, h:h + 1],
             reads=[R(cst["ones_row_f32"]), R(crow)], writes=[R(ps_cc)],
             tag="ccbc")
        e_copy(kb, ccol[:, h:h + 1], ps_cc[:, 0:1], reads=[R(ps_cc)],
               writes=[R(ccol, h)], engine="act")
    NKT = S // 128
    kpair = [P.sbuf(f"kp{i}", [128, S], BF16) for i in range(1)]
    vpair = [P.sbuf(f"vp{i}", [128, NKT * 130], BF16) for i in range(2)]
    pt_ = [P.sbuf(f"pt{i}", [128, 512], BF16) for i in range(4)]
    ps_st = [P.pshare([128, 512], i) for i in range(2)] + \
            [P.psum(f"psst{i}", [128, 512], F32) for i in range(2)]
    ps_av = [P.psum(f"psav{i}", [65, 512], F32) for i in range(2)]
    rrow = P.sbuf("rrow", [1, 512], F32)
    rbc = P.sbuf("rbc", [64, 512], F32)
    ps_rb = P.psum("ps_rb", [64, 512], F32)
    v4 = vin.rearrange("(kt p) d -> p kt d", p=128)
    for i in range(2):
        e_memset(kb, vpair[i][:, :].rearrange("p (kt c) -> p kt c", c=130)[:, :, 64:65],
                 1.0, writes=[R(vpair[i], "ones")])
        e_memset(kb, vpair[i][:, :].rearrange("p (kt c) -> p kt c", c=130)[:, :, 129:130],
                 1.0, writes=[R(vpair[i], "ones2")])
    for hp in range(8):
        kbuf = kpair[0]
        vbuf = vpair[hp % 2]
        kb.dma("act", "ldk0", kbuf[:, :], kin[hp * 128:(hp + 1) * 128, :],
               reads=[R("dram_k")], writes=[R(kbuf)], tag="ldk", serialize=False)
        vdst = vbuf[:, :].rearrange("p (kt c) -> p kt c", c=130)
        kb.dma("act", f"ldv{hp % 2}", vdst[:, :, 0:64],
               v4[:, :, hp * 128:hp * 128 + 64],
               reads=[R("dram_v")], writes=[R(vbuf, "h0")], tag="ldv", serialize=False)
        kb.dma("act", f"ldv{hp % 2}b", vdst[:, :, 65:129],
               v4[:, :, hp * 128 + 64:hp * 128 + 128],
               reads=[R("dram_v")], writes=[R(vbuf, "h1")], tag="ldv", serialize=False)
        for h in range(2):
            hb64 = h * 64
            for qg in range(2):
                pend = []

                def flush_av(kt_, qb_):
                    pi_ = (kt_ * 2 + qb_) % 4
                    e_mm(kb, ps_av[qb_][:, :],
                         vbuf[:, kt_ * 130 + 65 * h:kt_ * 130 + 65 * h + 65],
                         pt_[pi_][:, :],
                         reads=[R(vbuf), R(vbuf, "h0" if h == 0 else "h1"),
                                R(vbuf, "ones" if h == 0 else "ones2"),
                                R(pt_[pi_])],
                         writes=[R(ps_av[qb_])],
                         start=(kt_ == 0), stop=(kt_ == NKT - 1), tag="av")

                for kt in range(NKT):
                    for qb in range(2):
                        pi = (kt * 2 + qb) % 4
                        pst = ps_st[pi]
                        pt = pt_[pi]
                        qsl = slice(qg * 1024 + qb * 512, qg * 1024 + (qb + 1) * 512)
                        e_mm(kb, pst[:, :],
                             kbuf[hb64:hb64 + 64, kt * 128:(kt + 1) * 128],
                             Q[hp][hb64:hb64 + 64, qsl],
                             reads=[R(kbuf), R(Q[hp])], writes=[R(pst)],
                             tile_position=(hb64, 0), tag="st")
                        e_act(kb, pt[:, :], pst[:, :], AF.Exp,
                              bias=ccol[:, 2 * hp + h:2 * hp + h + 1],
                              scale=QK_SCALE,
                              reads=[R(pst), R(ccol, 2 * hp + h)],
                              writes=[R(pt)], tag="exp")
                        pend.append((kt, qb))
                        if len(pend) > 2:
                            flush_av(*pend.pop(0))
                for it_ in pend:
                    flush_av(*it_)
                for qb in range(2):
                    qsl = slice(qg * 1024 + qb * 512, qg * 1024 + (qb + 1) * 512)
                    e_recip(kb, rrow[:, :], ps_av[qb][64:65, :],
                            reads=[R(ps_av[qb])], writes=[R(rrow)])
                    e_mm(kb, ps_rb[:, :], cst["ones_row_f32"][:, 0:64], rrow[:, :],
                         reads=[R(cst["ones_row_f32"]), R(rrow)],
                         writes=[R(ps_rb)], tag="rbc")
                    e_copy(kb, rbc[:, :], ps_rb[:, :], reads=[R(ps_rb)],
                           writes=[R(rbc)], engine="act")
                    e_tt(kb, O[hp][hb64:hb64 + 64, qsl], ps_av[qb][0:64, :],
                         rbc[:, :], ALU.mult,
                         reads=[R(ps_av[qb]), R(rbc)],
                         writes=[R(O[hp], (h, qsl.start))], tag="oev")
    wo, woa = load_weight_fm(P, "wo", woin, D, D)
    if with_bias:
        bo = load_cols(P, "bo", boin, D)
    else:
        bo = None
    emit_linear_fm(P, "woP", O, woa, "wo", X, D, TOK, bias_cols=bo, bias_off=0,
                   w_off=0, accum_into=True,
                   x_key=None)
    if with_bias:
        TR = P.sbuf("tres_t", [128, 512], F32)
        for k in range(NT):
            for hh in range(4):
                kb.dma("sp", "ldtr", TR[:, :],
                       trin[k * 128:(k + 1) * 128, hh * 512:(hh + 1) * 512],
                       reads=[R("dram_tr")], writes=[R(TR)])
                e_tt(kb, X[k][:, hh * 512:(hh + 1) * 512],
                     X[k][:, hh * 512:(hh + 1) * 512], TR[:, :], ALU.add,
                     reads=[R(X[k]), R(TR)], writes=[R(X[k])], tag="innerres")
    for k in range(NT):
        kb.dma("sp", "stx", xout[k * 128:(k + 1) * 128, :], X[k][:, :],
               reads=[R(X[k])], writes=[R("dram_x1", k)])
    P.finalize()
    return P



def build_ffn0():
    NE, HID = 8, 2816
    NHK = HID // 128
    HCS = [768, 768, 768, 512]
    HCMAX = 768
    P = Prog("FFN0")
    kb = P.kb
    xin = P.inp("xfm", [D, TOK], F32)
    gwTin = P.inp("gwT", [D, NE], F32)
    ebin = P.inp("ebias", [1, NE], F32)
    w1in = P.inp("w1", [NE, D, HID], F32)
    w2in = P.inp("w2", [NE, HID, D], F32)
    w3in = P.inp("w3", [NE, D, HID], F32)
    kronin = P.inp("kron8", [NE, NE * 128], F32)
    xout = P.out("x2fm", [D, TOK], F32)

    cst = emit_consts(P)
    HTOK = 1024
    X = [P.sbuf(f"x{k}", [128, HTOK], F32) for k in range(NT)]
    T = [P.sbuf(f"t{k}", [128, HTOK], BF16) for k in range(NT)]
    Tfc = [P.sbuf(f"tfc{k}", [128, 512], F32) for k in range(NT)]
    lgT = P.sbuf("lgT", [NE, HTOK], F32)
    Ts = [P.sbuf(f"ts{k}", [128, HTOK], BF16) for k in range(NT)]
    w13 = P.sbuf("w13", [128, NT * HCMAX], BF16)
    sg = P.sbuf("sgb", [128, (HCMAX // 128) * HTOK], F32)
    hb = P.sbuf("hb", [128, (HCMAX // 128) * HTOK], BF16)
    w2c = P.sbuf("w2c", [128, (HCMAX // 128) * D], BF16)
    gwT = P.sbuf("s_gwT", [128, NT * NE], F32)
    ebrow = P.sbuf("ebrow", [1, NE], F32)
    ebbc = P.sbuf("ebbc", [128, NE], F32)
    cw = P.sbuf("cw", [NE, HTOK], F32)
    kron = P.sbuf("s_kron", [NE, NE * 128], F32)
    ident = P.sbuf("ident", [128, 128], F32)
    siota = P.sbuf("siota", [128, 128], F32)
    lg = P.sbuf("lg", [128, NE], F32)
    sc2 = P.sbuf("sc2", [128, NE], F32)
    eq = P.sbuf("eqb", [128, NE], F32)
    mcol = P.sbuf("mcol", [128, 1], F32)
    srow = P.sbuf("srow", [128, 1], F32)
    ps_l = P.pshare([128, 512], 0)
    ps_l2 = P.pshare([128, 512], 1)
    ps_t = P.psum("ps_t", [128, 128], F32)

    e_iota(kb, siota[:, :], [[1, 128]], writes=[R(siota)], channel_multiplier=-1)
    e_ts(kb, ident[:, :], siota[:, :], 0.0, ALU.is_equal, reads=[R(siota)],
         writes=[R(ident)], engine="pool")
    kb.dma("sp", "ldeb", ebrow[:, :], ebin, reads=[R("d_eb")], writes=[R(ebrow)])
    kb.dma("sp", "ldkron", kron[:, :], kronin, reads=[R("d_kron")], writes=[R(kron)])
    for k in range(NT):
        kb.dma("pool", "ldgw", gwT[:, k * NE:(k + 1) * NE],
               gwTin[k * 128:(k + 1) * 128, :], reads=[R("d_gwT")],
               writes=[R(gwT, k)])
    e_mm(kb, ps_l[:, 0:NE], cst["ones_row_f32"][:, :], ebrow[:, :],
         reads=[R(cst["ones_row_f32"]), R(ebrow)], writes=[R(ps_l)], tag="ebbc")
    e_copy(kb, ebbc[:, :], ps_l[:, 0:NE], reads=[R(ps_l)], writes=[R(ebbc)],
           engine="act")

    for half in range(2):
        hsl = slice(half * HTOK, (half + 1) * HTOK)
        for k in range(NT):
            kb.dma("sp", f"ldx{k % 2}", X[k][:, :],
                   xin[k * 128:(k + 1) * 128, hsl],
                   reads=[R("dram_xfm")], writes=[R(X[k])])
        def rt_cb(c, rot):
            psr = ps_l2
            for k in range(NT):
                e_mm(kb, psr[0:NE, 0:512], gwT[:, k * NE:(k + 1) * NE],
                     rot[k][:, 0:512],
                     reads=[R(gwT, k), R(rot[k])], writes=[R(psr)],
                     start=(k == 0), stop=(k == NT - 1), tag="rtmm")
            e_copy(kb, lgT[:, c * 512:(c + 1) * 512], psr[0:NE, 0:512],
                   reads=[R(psr)], writes=[R(lgT, c)], engine="act")
        emit_ln(P, X, [Tfc, T], HTOK, cst, f"ln{half}", chunk_cb=rt_cb,
                first_rot=True)
        for ts_ in range(HTOK // 128):
            e_transpose(kb, ps_t[:, 0:NE], lgT[:, ts_ * 128:(ts_ + 1) * 128],
                        ident[0:NE, 0:NE],
                        reads=[R(lgT, ts_ // 4), R(ident)], writes=[R(ps_t)])
            e_copy(kb, lg[:, :], ps_t[:, 0:NE],
                   reads=[R(ps_t)], writes=[R(lg)], engine="act")
            e_red(kb, mcol[:, :], lg[:, :], ALU.max, reads=[R(lg)],
                  writes=[R(mcol)])
            e_ts(kb, mcol[:, :], mcol[:, :], -1.0, ALU.mult, reads=[R(mcol)],
                 writes=[R(mcol)])
            e_act(kb, lg[:, :], lg[:, :], AF.Exp, bias=mcol[:, :],
                  accum_out=srow[:, :], reads=[R(lg), R(mcol)],
                  writes=[R(lg), R(srow)], tag="rtexp")
            e_recip(kb, srow[:, :], srow[:, :], reads=[R(srow)], writes=[R(srow)])
            e_ts(kb, lg[:, :], lg[:, :], srow[:, :], ALU.mult,
                 reads=[R(lg), R(srow)], writes=[R(lg)])
            e_tt(kb, sc2[:, :], lg[:, :], ebbc[:, :], ALU.add,
                 reads=[R(lg), R(ebbc)], writes=[R(sc2)])
            e_red(kb, mcol[:, :], sc2[:, :], ALU.max, reads=[R(sc2)],
                  writes=[R(mcol)])
            e_ts(kb, eq[:, :], sc2[:, :], mcol[:, :], ALU.is_equal,
                 reads=[R(sc2), R(mcol)], writes=[R(eq)])
            e_stt(kb, sc2[:, :], eq[:, :], -1e30, sc2[:, :], ALU.mult, ALU.add,
                  reads=[R(eq), R(sc2)], writes=[R(sc2)])
            e_red(kb, mcol[:, :], sc2[:, :], ALU.max, reads=[R(sc2)],
                  writes=[R(mcol)])
            e_ts(kb, sc2[:, :], sc2[:, :], mcol[:, :], ALU.is_equal,
                 reads=[R(sc2), R(mcol)], writes=[R(sc2)])
            e_tt(kb, eq[:, :], eq[:, :], sc2[:, :], ALU.add,
                 reads=[R(eq), R(sc2)], writes=[R(eq)])
            e_tt(kb, eq[:, :], eq[:, :], lg[:, :], ALU.mult,
                 reads=[R(eq), R(lg)], writes=[R(eq)])
            e_transpose(kb, ps_t[0:NE, 0:128], eq[:, :], ident[:, :],
                        reads=[R(eq), R(ident)], writes=[R(ps_t)])
            e_copy(kb, cw[:, ts_ * 128:(ts_ + 1) * 128], ps_t[0:NE, 0:128],
                   reads=[R(ps_t)], writes=[R(cw, ts_)], engine="act")
        for e in range(NE):
            for c2 in range(2):
                e_mm(kb, ps_l2[:, :], kron[:, e * 128:(e + 1) * 128],
                     cw[:, c2 * 512:(c2 + 1) * 512],
                     reads=[R(kron)] + [R(cw, t2) for t2 in range(4 * c2, 4 * c2 + 4)],
                     writes=[R(ps_l2)], tag="cwbc")
                for k in range(NT):
                    e_tt(kb, Ts[k][:, c2 * 512:(c2 + 1) * 512], T[k][:, c2 * 512:(c2 + 1) * 512],
                         ps_l2[:, :], ALU.mult,
                         reads=[R(T[k]), R(ps_l2)], writes=[R(Ts[k], c2)],
                         tag="tscale")
            hc_off = 0
            for hc, HC in enumerate(HCS):
                kb.dma("pool", "w13",
                       w13[:, :].rearrange("p (k h) -> p k h", h=HCMAX)[:, :, 0:HC],
                       w1in[e, :, hc_off:hc_off + HC].rearrange("(k p) h -> p k h", p=128),
                       reads=[R("d_w1")], writes=[R("w13")], tag="ldw1",
                       serialize=False)
                for ho in range(HC // 128):
                    for c2 in range(2):
                        p = ps_l if (ho + c2) % 2 == 0 else ps_l2
                        for k in range(NT):
                            e_mm(kb, p[:, :], w13[:, k * HCMAX + ho * 128:k * HCMAX + (ho + 1) * 128],
                                 Ts[k][:, c2 * 512:(c2 + 1) * 512],
                                 reads=[R("w13", k), R(Ts[k], c2)], writes=[R(p)],
                                 start=(k == 0), stop=(k == NT - 1), tag="gmm")
                        e_act(kb, sg[:, ho * HTOK + c2 * 512:ho * HTOK + (c2 + 1) * 512],
                              p[:, :], AF.Silu, reads=[R(p)],
                              writes=[R("sgb", (ho, c2))], tag="silu")
                kb.dma("pool", "w13",
                       w13[:, :].rearrange("p (k h) -> p k h", h=HCMAX)[:, :, 0:HC],
                       w3in[e, :, hc_off:hc_off + HC].rearrange("(k p) h -> p k h", p=128),
                       reads=[R("d_w3")], writes=[R("w13")], tag="ldw3",
                       serialize=False)
                for ho in range(HC // 128):
                    for c2 in range(2):
                        p = ps_l if (ho + c2) % 2 == 0 else ps_l2
                        for k in range(NT):
                            e_mm(kb, p[:, :], w13[:, k * HCMAX + ho * 128:k * HCMAX + (ho + 1) * 128],
                                 Ts[k][:, c2 * 512:(c2 + 1) * 512],
                                 reads=[R("w13", k), R(Ts[k], c2)], writes=[R(p)],
                                 start=(k == 0), stop=(k == NT - 1), tag="umm")
                        e_tt(kb, hb[:, ho * HTOK + c2 * 512:ho * HTOK + (c2 + 1) * 512],
                             sg[:, ho * HTOK + c2 * 512:ho * HTOK + (c2 + 1) * 512],
                             p[:, :], ALU.mult,
                             reads=[R(p), R("sgb", (ho, c2))],
                             writes=[R("hb", (ho, c2))], tag="hmul")
                kb.dma("pool", "w2c",
                       w2c[:, :].rearrange("p (k h) -> p k h", h=D)[:, 0:HC // 128, :],
                       w2in[e, hc_off:hc_off + HC, :].rearrange("(k p) h -> p k h", p=128),
                       reads=[R("d_w2")], writes=[R("w2c")], tag="ldw2",
                       serialize=False)
                for o in range(NT):
                    for c2 in range(2):
                        p = ps_l if (o + c2) % 2 == 0 else ps_l2
                        for hk in range(HC // 128):
                            e_mm(kb, p[:, :], w2c[:, hk * D + o * 128:hk * D + (o + 1) * 128],
                                 hb[:, hk * HTOK + c2 * 512:hk * HTOK + (c2 + 1) * 512],
                                 reads=[R("w2c", hk), R("hb", (hk, c2))],
                                 writes=[R(p)],
                                 start=(hk == 0), stop=(hk == HC // 128 - 1),
                                 tag="dmm")
                        e_tt(kb, X[o][:, c2 * 512:(c2 + 1) * 512],
                             X[o][:, c2 * 512:(c2 + 1) * 512], p[:, :], ALU.add,
                             reads=[R(X[o]), R(p)], writes=[R(X[o])], tag="xacc")
                hc_off += HC
        for k in range(NT):
            kb.dma("sp", f"stx{k % 2}", xout[k * 128:(k + 1) * 128, hsl],
                   X[k][:, :], reads=[R(X[k])], writes=[R("dram_xo", (half, k))])
    P.finalize()
    return P



def build_qkv_plain(name, wide):
    P = Prog(name)
    kb = P.kb
    xin = P.inp("xfm", [D, TOK], F32)
    wqkv = P.inp("wqkv", [D, 3 * D], F32)
    qout = P.out("qfm", [D, TOK], BF16)
    kout = P.out("kfm", [D, TOK], BF16)
    vout = P.out("vtm", [TOK, D], BF16)
    nout = P.out("norms", [1, 32], F32)
    cst = emit_consts(P)
    H = [P.sbuf(f"h{k}", [128, TOK], BF16) for k in range(NT)]
    emit_ln_dram(P, xin, [H], TOK, cst, "ln")
    wq, wqa = load_weight_fm(P, "wqkv", wqkv, D, 3 * D)
    Q = [P.sbuf(f"q{k}", [128, TOK], BF16) for k in range(NT)]
    Kf = [P.sbuf(f"kf{k}", [128, TOK], BF16) for k in range(NT)]
    emit_linear_fm(P, "q", H, wqa, "wqkv", Q, D, TOK, w_off=0)
    emit_linear_fm(P, "k", H, wqa, "wqkv", Kf, D, TOK, w_off=D)
    Vt2 = [P.sbuf(f"vt{t}", [128, D], BF16) for t in range(2)]
    Vt = [Vt2[t % 2] for t in range(TOK // 128)]
    emit_linear_tm(P, "v", H, wqa, "wqkv", Vt, D, TOK, w_off=2 * D,
                   out_dram=vout, dram_name="dram_vtm")
    norms = P.sbuf("normsb", [1, 32], F32)
    sqs = P.sbuf("sqs", [128, TOK], F32)
    if not wide:
        emit_headnorm_max(P, "nq", Q, 16, TOK, cst, norms, 0, sqs)
        emit_headnorm_max(P, "nk", Kf, 16, TOK, cst, norms, 1, sqs)
        nreads = [R(norms, (r_, h)) for r_ in range(2) for h in range(16)]
    else:
        ps_n = P.pshare([1, 512], 0)
        nr = P.sbuf("nr_w", [1, TOK], F32)
        ones = cst["ones_col_f32"]
        for row, QQ in ((0, Q), (1, Kf)):
            for c in range(TOK // 512):
                for k in range(NT):
                    e_act(kb, sqs[:, 0:512], QQ[k][:, c * 512:(c + 1) * 512],
                          AF.Square, reads=[R(QQ[k])], writes=[R(sqs)],
                          tag="wsq")
                    e_mm(kb, ps_n[:, :], ones[:, :], sqs[:, 0:512],
                         reads=[R(ones), R(sqs)], writes=[R(ps_n)],
                         start=(k == 0), stop=(k == NT - 1), tag="wnmm")
                e_copy(kb, nr[:, c * 512:(c + 1) * 512], ps_n[:, :],
                       reads=[R(ps_n)], writes=[R(nr, c)], engine="act")
            col = row * 16
            e_red(kb, norms[0:1, col:col + 1], nr[:, :], ALU.max,
                  reads=[R(nr, c) for c in range(TOK // 512)],
                  writes=[R(norms, (row, 0))], tag="wmax")
        nreads = [R(norms, (r_, 0)) for r_ in range(2)]
    kb.dma("sp", "stn", nout, norms[:, :], reads=nreads, writes=[R("d_no")])
    for k in range(NT):
        kb.dma("sp", "stq", qout[k * 128:(k + 1) * 128, :], Q[k][:, :],
               reads=[R(Q[k], (kk2, c)) for kk2 in [k] for c in range(4)],
               writes=[R("dram_qfm", k)])
        kb.dma("sp", "stk", kout[k * 128:(k + 1) * 128, :], Kf[k][:, :],
               reads=[R(Kf[k], (kk2, c)) for kk2 in [k] for c in range(4)],
               writes=[R("dram_kfm", k)])
    P.finalize()
    return P



def build_attn_causal():
    QK_SCALE = 1.0 / 32.0
    P = Prog("ATTC")
    kb = P.kb
    xin = P.inp("xfm", [D, TOK], F32)
    qin = P.inp("qfm", [D, TOK], BF16)
    kin = P.inp("kfull", [D, S], BF16)
    vin = P.inp("vfull", [S, D], BF16)
    nin = P.inp("norms4", [1, 4 * 32], F32)
    mkin = P.inp("cmask", [128, 4 * 128], F32)
    xout = P.out("x5fm", [D, TOK], F32)
    cst = emit_consts(P)
    X = [P.sbuf(f"x{k}", [128, TOK], F32) for k in range(NT)]
    Q = [P.sbuf(f"q{k}", [128, TOK], BF16) for k in range(NT)]
    for k in range(NT):
        kb.dma("sp", f"ldx{k % 2}", X[k][:, :], xin[k * 128:(k + 1) * 128, :],
               reads=[R("dram_xfm")], writes=[R(X[k])])
        kb.dma("sp", "ldq", Q[k][:, :], qin[k * 128:(k + 1) * 128, :],
               reads=[R("dram_qfm")], writes=[R(Q[k])])
    cmask = P.sbuf("s_cmask", [128, 4 * 128], F32)
    kb.dma("sp", "ldmk", cmask[:, :], mkin, reads=[R("d_mk")], writes=[R(cmask)])
    nrow = P.sbuf("nrow", [1, 128], F32)
    nmax = P.sbuf("nmax", [1, 32], F32)
    ccol = P.sbuf("ccol", [128, 1], F32)
    kb.dma("sp", "ldn", nrow[:, :], nin, reads=[R("d_n")], writes=[R(nrow)])
    def fmax(out, a, b, reads, writes):
        def fn(eng, _o=out, _a=a, _b=b):
            return eng.tensor_max(_o, _a, _b)
        return kb.op("dve", fn, reads=reads, writes=writes)
    fmax(nmax[:, :], nrow[:, 0:32], nrow[:, 32:64], [R(nrow)], [R(nmax)])
    fmax(nmax[:, :], nmax[:, :], nrow[:, 64:96], [R(nmax), R(nrow)], [R(nmax)])
    fmax(nmax[:, :], nmax[:, :], nrow[:, 96:128], [R(nmax), R(nrow)], [R(nmax)])
    e_act(kb, nmax[:, :], nmax[:, :], AF.Sqrt, reads=[R(nmax)], writes=[R(nmax)],
          tag="csq")
    e_tt(kb, nmax[:, 0:1], nmax[:, 0:1], nmax[:, 16:17], ALU.mult,
         reads=[R(nmax)], writes=[R(nmax, "c")])
    e_ts(kb, nmax[:, 0:1], nmax[:, 0:1], -QK_SCALE, ALU.mult,
         reads=[R(nmax, "c")], writes=[R(nmax, "c")])
    ps_cc = P.pshare([128, 512], 0)
    e_mm(kb, ps_cc[:, 0:1], cst["ones_row_f32"][:, :], nmax[:, 0:1],
         reads=[R(cst["ones_row_f32"]), R(nmax, "c")], writes=[R(ps_cc)],
         tag="cbc")
    e_copy(kb, ccol[:, :], ps_cc[:, 0:1], reads=[R(ps_cc)], writes=[R(ccol)],
           engine="act")
    kblk = [P.sbuf(f"kb{i}", [128, NT * 128], BF16) for i in range(2)]
    vblk = [P.sbuf(f"vb{i}", [128, D], BF16) for i in range(2)]
    pt_ = [P.sbuf(f"ptc{i}", [128, 128], BF16) for i in range(2)]
    ps_st = [P.pshare([128, 512], i) for i in range(2)]
    ps_av = [P.psum(f"psavc{i}", [128, 512], F32) for i in range(2)]
    ps_sum = P.psum("ps_sumc", [1, 128], F32)
    sumr = P.sbuf("sumr", [1, 128], F32)
    rbc = P.sbuf("rbcc", [128, 128], F32)
    O = [P.sbuf(f"oc{k}", [128, TOK], BF16) for k in range(NT)]
    k4 = kin.rearrange("(dt p) s -> p dt s", p=128)
    r_ = None
    for qi in range(NBLK):
        nkb = 4 * (qi + 1)
        kjs = [rr * NBLK + j for rr in range(4) for j in range(qi + 1)]
        for ji, kj in enumerate(kjs):
            kbuf = kblk[ji % 2]
            vbuf = vblk[ji % 2]
            kb.dma("act", f"ldkb{ji % 2}", kbuf[:, :],
                   k4[:, :, kj * 128:(kj + 1) * 128],
                   reads=[R("dram_k")], writes=[R(kbuf)], tag="ldkb", serialize=False)
            kb.dma("act", f"ldvb{ji % 2}", vbuf[:, :],
                   vin[kj * 128:(kj + 1) * 128, :],
                   reads=[R("dram_v")], writes=[R(vbuf)], tag="ldvb", serialize=False)
            pst = ps_st[ji % 2]
            for dt in range(NT):
                e_mm(kb, pst[:, 0:128], kbuf[:, dt * 128:(dt + 1) * 128],
                     Q[dt][:, qi * 128:(qi + 1) * 128],
                     reads=[R(kbuf), R(Q[dt])], writes=[R(pst)],
                     start=(dt == 0), stop=(dt == NT - 1), tag="stc")
            pt = pt_[ji % 2]
            e_act(kb, pt[:, :], pst[:, 0:128], AF.Exp, bias=ccol[:, :],
                  scale=QK_SCALE, reads=[R(pst), R(ccol)], writes=[R(pt)],
                  tag="expc")
            if kj % NBLK == qi:
                rr = kj // NBLK
                e_tt(kb, pt[:, :], pt[:, :], cmask[:, rr * 128:(rr + 1) * 128],
                     ALU.mult, reads=[R(pt), R(cmask)], writes=[R(pt)],
                     tag="maskc")
            e_mm(kb, ps_sum[:, :], cst["ones_col_bf16"][:, :], pt[:, :],
                 reads=[R(cst["ones_col_bf16"]), R(pt)], writes=[R(ps_sum)],
                 start=(ji == 0), stop=(ji == nkb - 1), tag="sumc")
            for dh in range(2):
                for j2 in range(4):
                    e_mm(kb, ps_av[dh][:, j2 * 128:(j2 + 1) * 128],
                         vbuf[:, dh * 512 + j2 * 128:dh * 512 + (j2 + 1) * 128],
                         pt[:, :], reads=[R(vbuf), R(pt)],
                         writes=[R(ps_av[dh])],
                         start=(ji == 0 and j2 == 0),
                         stop=False, tag="avc")
        e_recip(kb, sumr[:, :], ps_sum[:, :], reads=[R(ps_sum)], writes=[R(sumr)])
        e_mm(kb, ps_cc[:, 0:128], cst["ones_row_f32"][:, :], sumr[:, :],
             reads=[R(cst["ones_row_f32"]), R(sumr)], writes=[R(ps_cc)],
             tag="rbcc")
        e_copy(kb, rbc[:, :], ps_cc[:, 0:128], reads=[R(ps_cc)], writes=[R(rbc)],
               engine="act")
        for dh in range(2):
            for j2 in range(4):
                o = dh * 4 + j2
                e_tt(kb, O[o][:, qi * 128:(qi + 1) * 128],
                     ps_av[dh][:, j2 * 128:(j2 + 1) * 128], rbc[:, :], ALU.mult,
                     reads=[R(ps_av[dh]), R(rbc)], writes=[R(O[o], qi)],
                     tag="oevc")
    for k in range(NT):
        e_tt(kb, X[k][:, :], X[k][:, :], O[k][:, :], ALU.add,
             reads=[R(X[k])] + [R(O[k], qi) for qi in range(NBLK)],
             writes=[R(X[k])], tag="resc")
        kb.dma("sp", f"stx{k % 2}", xout[k * 128:(k + 1) * 128, :], X[k][:, :],
               reads=[R(X[k])], writes=[R("dram_xo", k)])
    P.finalize()
    return P


def emit_ln_dram(P, xdram, outs, ntok, cst, pref, eps=1e-5):
    kb = P.kb
    nc2 = 512
    nchunk = ntok // nc2
    XC = [P.sbuf(f"{pref}_xc{k}", [128, nc2], F32) for k in range(NT)]
    sq = P.sbuf(f"{pref}_sq", [128, nc2], F32)
    mrow = P.sbuf(f"{pref}_m", [1, nc2], F32)
    ssrow = P.sbuf(f"{pref}_ss", [1, nc2], F32)
    rr = P.sbuf(f"{pref}_rr", [1, nc2], F32)
    mr2 = P.sbuf(f"{pref}_mr", [1, nc2], F32)
    ps_red = P.pshare([1, nc2], 0)
    ps_bc = P.pshare([128, nc2], 0)
    ps_bc2 = P.pshare([128, nc2], 1)
    ones = cst["ones_col_f32"]
    ones_row = cst["ones_row_f32"]
    for c in range(nchunk):
        sl = slice(c * nc2, (c + 1) * nc2)
        for k in range(NT):
            kb.dma("sp", f"{pref}ldx{k % 4}", XC[k][:, :],
                   xdram[k * 128:(k + 1) * 128, sl],
                   reads=[R("dram_x")], writes=[R(XC[k])])
        for k in range(NT):
            e_mm(kb, ps_red[:, :], ones[:, :], XC[k][:, :],
                 reads=[R(ones), R(XC[k])], writes=[R(ps_red)],
                 start=(k == 0), stop=(k == NT - 1), tag=f"{pref}mean")
        e_copy(kb, mrow[:, :], ps_red[:, :], reads=[R(ps_red)],
               writes=[R(mrow)], engine="act")
        for k in range(NT):
            e_act(kb, sq[:, :], XC[k][:, :], AF.Square,
                  reads=[R(XC[k])], writes=[R(sq)], tag=f"{pref}sq")
            e_mm(kb, ps_red[:, :], ones[:, :], sq[:, :],
                 reads=[R(ones), R(sq)], writes=[R(ps_red)],
                 start=(k == 0), stop=(k == NT - 1), tag=f"{pref}ss")
        e_copy(kb, ssrow[:, :], ps_red[:, :], reads=[R(ps_red)],
               writes=[R(ssrow)], engine="act")
        e_ts(kb, mrow[:, :], mrow[:, :], 1.0 / D, ALU.mult, reads=[R(mrow)],
             writes=[R(mrow)])
        e_ts(kb, ssrow[:, :], ssrow[:, :], 1.0 / D, ALU.mult, reads=[R(ssrow)],
             writes=[R(ssrow)])
        e_tt(kb, rr[:, :], mrow[:, :], mrow[:, :], ALU.mult,
             reads=[R(mrow)], writes=[R(rr)])
        e_tt(kb, rr[:, :], ssrow[:, :], rr[:, :], ALU.subtract,
             reads=[R(ssrow), R(rr)], writes=[R(rr)])
        e_ts(kb, rr[:, :], rr[:, :], float(eps), ALU.add, reads=[R(rr)],
             writes=[R(rr)])
        e_act(kb, rr[:, :], rr[:, :], AF.Sqrt, reads=[R(rr)], writes=[R(rr)],
              tag=f"{pref}sqrt")
        e_recip(kb, rr[:, :], rr[:, :], reads=[R(rr)], writes=[R(rr)])
        e_tt(kb, mr2[:, :], mrow[:, :], rr[:, :], ALU.mult,
             reads=[R(mrow), R(rr)], writes=[R(mr2)])
        e_mm(kb, ps_bc[:, :], ones_row[:, :], rr[:, :],
             reads=[R(ones_row), R(rr)], writes=[R(ps_bc)], tag=f"{pref}bc")
        e_mm(kb, ps_bc2[:, :], ones_row[:, :], mr2[:, :],
             reads=[R(ones_row), R(mr2)], writes=[R(ps_bc2)], tag=f"{pref}bc2")
        for k in range(NT):
            first = outs[0][k]
            e_tt(kb, first[:, sl], XC[k][:, :], ps_bc[:, :], ALU.mult,
                 reads=[R(XC[k]), R(ps_bc)], writes=[R(first, c)])
            e_tt(kb, first[:, sl], first[:, sl], ps_bc2[:, :], ALU.subtract,
                 reads=[R(first, c), R(ps_bc2)], writes=[R(first, c)])
            for extra in outs[1:]:
                e_copy(kb, extra[k][:, sl], first[:, sl],
                       reads=[R(first, c)], writes=[R(extra[k], c)])



def build_ffn1(dbg=False):
    NE, MOE_I, NGRP, NLOC, ANC_I, SH_I = 32, 512, 4, 16, 256, 512
    P = Prog("FFN1")
    kb = P.kb
    xin = P.inp("xfm", [D, TOK], F32)
    gwTin = P.inp("gwT", [D, NE], F32)
    gbin = P.inp("gbias", [1, NE], F32)
    wgin = P.inp("w_gate", [NE, D, MOE_I], F32)
    wuin = P.inp("w_up", [NE, D, MOE_I], F32)
    wdin = P.inp("w_down", [NE, MOE_I, D], F32)
    swgin = P.inp("sh_gate", [D, SH_I], F32)
    swuin = P.inp("sh_up", [D, SH_I], F32)
    swdin = P.inp("sh_down", [SH_I, D], F32)
    awgin = P.inp("an_gate", [NLOC, D, ANC_I], F32)
    awuin = P.inp("an_up", [NLOC, D, ANC_I], F32)
    awdin = P.inp("an_down", [NLOC, ANC_I, D], F32)
    kronin = P.inp("kron32", [NE, NE * 128], F32)
    xout = P.out("xofm", [D, TOK], F32)

    cst = emit_consts(P)
    HTOK = 1024
    X = [P.sbuf(f"x{k}", [128, HTOK], F32) for k in range(NT)]
    T = [P.sbuf(f"t{k}", [128, HTOK], BF16) for k in range(NT)]
    Tfc = [P.sbuf(f"tfc{k}", [128, 512], F32) for k in range(NT)]
    lgT = P.sbuf("lgT", [NE, HTOK], F32)
    wgb = P.sbuf("wgb", [128, NT * MOE_I], BF16)
    hb = P.sbuf("hb", [128, (MOE_I // 128) * HTOK], BF16)
    sg = P.sbuf("sgb", [128, (MOE_I // 128) * HTOK], F32)
    wdb = P.sbuf("wdb", [128, (MOE_I // 128) * D], BF16)
    gwT = P.sbuf("s_gwT", [128, NT * NE], F32)
    gbrow = P.sbuf("gbrow", [1, NE], F32)
    gbbc = P.sbuf("gbbc", [128, NE], F32)
    cw = P.sbuf("cw", [NE, HTOK], F32)
    acw = P.sbuf("acw", [16, HTOK], F32)
    kron = P.sbuf("s_kron", [NE, NE * 128], F32)
    ident = P.sbuf("ident", [128, 128], F32)
    siota = P.sbuf("siota", [128, 128], F32)
    cbc = P.sbuf("cbc", [128, HTOK], F32)
    lg = P.sbuf("lg", [128, NE], F32)
    sc = P.sbuf("scb", [128, NE], F32)
    scm = P.sbuf("scm", [128, NE], F32)
    eq = P.sbuf("eqb", [128, NE], F32)
    eq0 = P.sbuf("eq0b", [128, NE], F32)
    sel = P.sbuf("selb", [128, NE], F32)
    gsc = P.sbuf("gsc", [128, NGRP], F32)
    gtmp = P.sbuf("gtmp", [128, NGRP], F32)
    mcol = P.sbuf("mcol", [128, 1], F32)
    m2col = P.sbuf("m2col", [128, 1], F32)
    ps_l = P.pshare([128, 512], 0)
    ps_l2 = P.pshare([128, 512], 1)
    ps_t = P.psum("ps_t", [NE, 128], F32)
    ps_t2 = P.psum("ps_t2", [128, 128], F32)

    e_iota(kb, siota[:, :], [[1, 128]], writes=[R(siota)], channel_multiplier=-1)
    e_ts(kb, ident[:, :], siota[:, :], 0.0, ALU.is_equal, reads=[R(siota)],
         writes=[R(ident)], engine="pool")
    kb.dma("sp", "ldgb", gbrow[:, :], gbin, reads=[R("d_gb")], writes=[R(gbrow)])
    kb.dma("sp", "ldkron", kron[:, :], kronin, reads=[R("d_kron")], writes=[R(kron)])
    for k in range(NT):
        kb.dma("pool", "ldgw", gwT[:, k * NE:(k + 1) * NE],
               gwTin[k * 128:(k + 1) * 128, :], reads=[R("d_gwT")],
               writes=[R(gwT, k)])
    e_mm(kb, ps_l[:, 0:NE], cst["ones_row_f32"][:, :], gbrow[:, :],
         reads=[R(cst["ones_row_f32"]), R(gbrow)], writes=[R(ps_l)], tag="gbbc")
    e_copy(kb, gbbc[:, :], ps_l[:, 0:NE], reads=[R(ps_l)], writes=[R(gbbc)],
           engine="act")

    def expert_block(pref, e, wg_ap, wu_ap, wd_ap, moei, coef_rows, coef_n,
                     kron_base, alpha):
        nmo = moei // 128
        kb.dma("pool", "wgb",
               wgb[:, 0:NT * moei].rearrange("p (k h) -> p k h", h=moei),
               wg_ap[:, :].rearrange("(k p) h -> p k h", p=128),
               reads=[R("d_wg")], writes=[R(wgb)], tag=f"{pref}lg",
               serialize=False)
        for ho in range(nmo):
            for c2 in range(2):
                p = ps_l if (ho + c2) % 2 == 0 else ps_l2
                for k in range(NT):
                    e_mm(kb, p[:, :], wgb[:, k * moei + ho * 128:k * moei + (ho + 1) * 128],
                         T[k][:, c2 * 512:(c2 + 1) * 512],
                         reads=[R(wgb, k), R(T[k])], writes=[R(p)],
                         start=(k == 0), stop=(k == NT - 1), tag=f"{pref}g")
                e_act(kb, sg[:, ho * HTOK + c2 * 512:ho * HTOK + (c2 + 1) * 512],
                      p[:, :], AF.Silu, reads=[R(p)],
                      writes=[R("sgb", (ho, c2))], tag=f"{pref}si")
        kb.dma("pool", "wgb",
               wgb[:, 0:NT * moei].rearrange("p (k h) -> p k h", h=moei),
               wu_ap[:, :].rearrange("(k p) h -> p k h", p=128),
               reads=[R("d_wu")], writes=[R(wgb)], tag=f"{pref}lu",
               serialize=False)
        if coef_rows is not None:
            for c2 in range(2):
                e_mm(kb, ps_l2[:, :] if c2 == 0 else ps_l[:, :],
                     kron[0:coef_n, kron_base * 128:(kron_base + 1) * 128],
                     coef_rows[:, c2 * 512:(c2 + 1) * 512],
                     reads=[R(kron), R(coef_rows)],
                     writes=[R(ps_l2) if c2 == 0 else R(ps_l)], tag=f"{pref}cb")
                e_copy(kb, cbc[:, c2 * 512:(c2 + 1) * 512],
                       ps_l2[:, :] if c2 == 0 else ps_l[:, :],
                       reads=[R(ps_l2) if c2 == 0 else R(ps_l)],
                       writes=[R(cbc, c2)], engine="act")
        for ho in range(nmo):
            for c2 in range(2):
                p = ps_l if (ho + c2) % 2 == 0 else ps_l2
                for k in range(NT):
                    e_mm(kb, p[:, :], wgb[:, k * moei + ho * 128:k * moei + (ho + 1) * 128],
                         T[k][:, c2 * 512:(c2 + 1) * 512],
                         reads=[R(wgb, k), R(T[k])], writes=[R(p)],
                         start=(k == 0), stop=(k == NT - 1), tag=f"{pref}u")
                sl = slice(ho * HTOK + c2 * 512, ho * HTOK + (c2 + 1) * 512)
                if coef_rows is not None:
                    e_tt(kb, sg[:, sl], sg[:, sl], p[:, :], ALU.mult,
                         reads=[R(p), R("sgb", (ho, c2))],
                         writes=[R("sgb", (ho, c2))], tag=f"{pref}h")
                    e_tt(kb, hb[:, sl], sg[:, sl], cbc[:, c2 * 512:(c2 + 1) * 512],
                         ALU.mult,
                         reads=[R("sgb", (ho, c2)), R(cbc, c2)],
                         writes=[R("hb", (ho, c2))], tag=f"{pref}hc")
                else:
                    e_tt(kb, hb[:, sl], sg[:, sl], p[:, :], ALU.mult,
                         reads=[R(p), R("sgb", (ho, c2))], writes=[R("hb", (ho, c2))],
                         tag=f"{pref}h")
        kb.dma("pool", "wdb",
               wdb[:, :].rearrange("p (k h) -> p k h", h=D)[:, 0:nmo, :],
               wd_ap[:, :].rearrange("(k p) h -> p k h", p=128),
               reads=[R("d_wd")], writes=[R(wdb)], tag=f"{pref}ld",
               serialize=False)
        for o in range(NT):
            for c2 in range(2):
                p = ps_l if (o + c2) % 2 == 0 else ps_l2
                for hk in range(nmo):
                    e_mm(kb, p[:, :], wdb[:, hk * D + o * 128:hk * D + (o + 1) * 128],
                         hb[:, hk * HTOK + c2 * 512:hk * HTOK + (c2 + 1) * 512],
                         reads=[R(wdb, hk), R("hb", (hk, c2))], writes=[R(p)],
                         start=(hk == 0), stop=(hk == nmo - 1), tag=f"{pref}d")
                if alpha == 1.0:
                    e_tt(kb, X[o][:, c2 * 512:(c2 + 1) * 512],
                         X[o][:, c2 * 512:(c2 + 1) * 512], p[:, :], ALU.add,
                         reads=[R(X[o]), R(p)], writes=[R(X[o])], tag=f"{pref}x")
                else:
                    e_stt(kb, X[o][:, c2 * 512:(c2 + 1) * 512], p[:, :],
                          float(alpha), X[o][:, c2 * 512:(c2 + 1) * 512],
                          ALU.mult, ALU.add,
                          reads=[R(X[o]), R(p)], writes=[R(X[o])], tag=f"{pref}xa")

    for half in range(2):
        hsl = slice(half * HTOK, (half + 1) * HTOK)
        for k in range(NT):
            kb.dma("sp", f"ldx{k % 2}", X[k][:, :],
                   xin[k * 128:(k + 1) * 128, hsl],
                   reads=[R("dram_xfm")], writes=[R(X[k])])
        def rt_cb(c, rot):
            psr = ps_l2
            for k in range(NT):
                e_mm(kb, psr[0:NE, 0:512], gwT[:, k * NE:(k + 1) * NE],
                     rot[k][:, 0:512],
                     reads=[R(gwT, k), R(rot[k])], writes=[R(psr)],
                     start=(k == 0), stop=(k == NT - 1), tag="rt1")
            e_copy(kb, lgT[:, c * 512:(c + 1) * 512], psr[0:NE, 0:512],
                   reads=[R(psr)], writes=[R(lgT, c)], engine="act")
        emit_ln(P, X, [Tfc, T], HTOK, cst, f"ln{half}", chunk_cb=rt_cb,
                first_rot=True)
        for ts_ in range(HTOK // 128):
            tsl = slice(ts_ * 128, (ts_ + 1) * 128)
            e_transpose(kb, ps_t2[:, 0:NE], lgT[:, ts_ * 128:(ts_ + 1) * 128],
                        ident[0:NE, 0:NE],
                        reads=[R(lgT, ts_ // 4), R(ident)], writes=[R(ps_t2)])
            e_act(kb, lg[:, :], ps_t2[:, 0:NE], AF.Sigmoid, reads=[R(ps_t2)],
                  writes=[R(lg)], tag="rtsig")
            e_tt(kb, sc[:, :], lg[:, :], gbbc[:, :], ALU.add,
                 reads=[R(lg), R(gbbc)], writes=[R(sc)])
            for g in range(NGRP):
                gs = slice(g * 8, (g + 1) * 8)
                e_red(kb, mcol[:, :], sc[:, gs], ALU.max, reads=[R(sc)],
                      writes=[R(mcol)])
                e_ts(kb, eq[:, gs], sc[:, gs], mcol[:, :], ALU.is_equal,
                     reads=[R(sc), R(mcol)], writes=[R(eq, g)])
                e_stt(kb, scm[:, gs], eq[:, gs], -1e30, sc[:, gs],
                      ALU.mult, ALU.add, reads=[R(eq, g), R(sc)],
                      writes=[R(scm, g)])
                e_red(kb, m2col[:, :], scm[:, gs], ALU.max, reads=[R(scm, g)],
                      writes=[R(m2col)])
                e_tt(kb, gsc[:, g:g + 1], mcol[:, :], m2col[:, :], ALU.add,
                     reads=[R(mcol), R(m2col)], writes=[R(gsc, g)])
            e_red(kb, mcol[:, :], gsc[:, :], ALU.max,
                  reads=[R(gsc, g) for g in range(NGRP)], writes=[R(mcol)])
            e_ts(kb, gtmp[:, :], gsc[:, :], mcol[:, :], ALU.is_equal,
                 reads=[R(gsc, g) for g in range(NGRP)] + [R(mcol)],
                 writes=[R(gtmp)])
            e_stt(kb, gtmp[:, :], gtmp[:, :], -1e30,
                  gsc[:, :], ALU.mult, ALU.add,
                  reads=[R(gtmp)] + [R(gsc, g) for g in range(NGRP)],
                  writes=[R(gtmp)])
            e_red(kb, m2col[:, :], gtmp[:, :], ALU.max, reads=[R(gtmp)],
                  writes=[R(m2col)])
            e_ts(kb, gtmp[:, :], gsc[:, :], m2col[:, :], ALU.is_ge,
                 reads=[R(gsc, g) for g in range(NGRP)] + [R(m2col)],
                 writes=[R(gtmp)])
            e_ts(kb, gtmp[:, :], gtmp[:, :], 1.0, ALU.subtract, scalar2=1e30,
                 op1=ALU.mult, reads=[R(gtmp)], writes=[R(gtmp)])
            for g in range(NGRP):
                gs = slice(g * 8, (g + 1) * 8)
                e_ts(kb, scm[:, gs], sc[:, gs], gtmp[:, g:g + 1], ALU.add,
                     reads=[R(sc), R(gtmp)], writes=[R(scm, f"m{g}")])
            e_memset(kb, sel[:, :], 0.0, writes=[R(sel)], engine="dve")
            scm_r = [R(scm, f"m{g}") for g in range(NGRP)]
            for it in range(6):
                e_red(kb, mcol[:, :], scm[:, :], ALU.max, reads=scm_r,
                      writes=[R(mcol)])
                dst = eq0 if it == 0 else eq
                e_ts(kb, dst[:, :], scm[:, :], mcol[:, :], ALU.is_equal,
                     reads=scm_r + [R(mcol)], writes=[R(dst)])
                e_tt(kb, sel[:, :], sel[:, :], dst[:, :], ALU.add,
                     reads=[R(sel), R(dst)], writes=[R(sel)])
                e_stt(kb, scm[:, :], dst[:, :], -1e30, scm[:, :],
                      ALU.mult, ALU.add, reads=[R(dst)] + scm_r, writes=scm_r)
            e_tt(kb, sel[:, :], sel[:, :], lg[:, :], ALU.mult,
                 reads=[R(sel), R(lg)], writes=[R(sel)])
            e_red(kb, mcol[:, :], sel[:, :], ALU.add, reads=[R(sel)],
                  writes=[R(mcol)])
            e_ts(kb, mcol[:, :], mcol[:, :], 1e-20, ALU.add, reads=[R(mcol)],
                 writes=[R(mcol)])
            e_recip(kb, mcol[:, :], mcol[:, :], reads=[R(mcol)], writes=[R(mcol)])
            e_ts(kb, sel[:, :], sel[:, :], mcol[:, :], ALU.mult, scalar2=0.2,
                 op1=ALU.mult, reads=[R(sel), R(mcol)], writes=[R(sel)])
            e_transpose(kb, ps_t[:, :], sel[:, :], ident[:, :],
                        reads=[R(sel), R(ident)], writes=[R(ps_t)])
            e_copy(kb, cw[:, tsl], ps_t[:, :], reads=[R(ps_t)],
                   writes=[R(cw, ts_)], engine="act")
            e_tt(kb, eq[:, 0:16], eq0[:, 0:NE:2], eq0[:, 1:NE:2], ALU.add,
                 reads=[R(eq0)], writes=[R(eq, "a")])
            e_transpose(kb, ps_t[0:16, :], eq[:, 0:16], ident[:, :],
                        reads=[R(eq, "a"), R(ident)], writes=[R(ps_t)])
            e_copy(kb, acw[:, tsl], ps_t[0:16, :], reads=[R(ps_t)],
                   writes=[R(acw, ts_)], engine="act")
        cwreads = [R(cw, t2) for t2 in range(8)]
        acreads = [R(acw, t2) for t2 in range(8)]
        if dbg and half == 0:
            cwdb = P.out("cwdb", [NE, HTOK], F32)
            acdb = P.out("acdb", [16, HTOK], F32)
            kb.dma("sp", "stcw", cwdb, cw[:, :], reads=cwreads, writes=[R("d_cw")])
            kb.dma("sp", "stac", acdb, acw[:, :], reads=acreads, writes=[R("d_ac")])
        for e in range(NE):
            expert_block(f"r{e}", e, wgin[e], wuin[e], wdin[e], MOE_I,
                         cw, NE, e, 1.0)
        expert_block("sh", 0, swgin, swuin, swdin, SH_I, None, 0, 0, 1.0)
        for e in range(NLOC):
            expert_block(f"a{e}", e, awgin[e], awuin[e], awdin[e], ANC_I,
                         acw, 16, e, 0.1)
        for k in range(NT):
            kb.dma("sp", f"stx{k % 2}", xout[k * 128:(k + 1) * 128, hsl],
                   X[k][:, :], reads=[R(X[k])], writes=[R("dram_xo", (half, k))])
    P.finalize()
    return P



def build_scanprep():
    P = Prog("SPREP")
    kb = P.kb
    xin = P.inp("xfm", [D, TOK], F32)
    ewin = P.inp("eta_w", [D], F32)
    ebin = P.inp("eta_b", [1], F32)
    awin = P.inp("alpha_w", [D], F32)
    abin = P.inp("alpha_b", [1], F32)
    gwin = P.inp("gate_w", [D], F32)
    gbin = P.inp("gate_b", [1], F32)
    tout = P.out("t6fm", [D, TOK], BF16)
    kmout = P.out("kmean", [32, D], F32)
    scout = P.out("sca", [1, 5 * 32], F32)

    cst = emit_consts(P)
    Tf = [P.sbuf(f"tf{k}", [128, TOK], F32) for k in range(NT)]
    Tb = [P.sbuf(f"tb{k}", [128, TOK], BF16) for k in range(NT)]
    emit_ln_dram(P, xin, [Tf, Tb], TOK, cst, "ln")
    for k in range(NT):
        kb.dma("sp", "stt", tout[k * 128:(k + 1) * 128, :], Tb[k][:, :],
               reads=[R(Tb[k], c) for c in range(4)], writes=[R("d_t", k)])
    sqb = P.sbuf("sqb", [128, 512], F32)
    nr = P.sbuf("nr", [1, TOK], F32)
    ps_r = P.pshare([1, 512], 0)
    ps_b = P.pshare([128, 512], 0)
    ones = cst["ones_col_f32"]
    ones_row = cst["ones_row_f32"]
    for c in range(4):
        sl = slice(c * 512, (c + 1) * 512)
        for k in range(NT):
            e_act(kb, sqb[:, :], Tf[k][:, sl], AF.Square,
                  reads=[R(Tf[k], c)], writes=[R(sqb)], tag="nsq")
            e_mm(kb, ps_r[:, :], ones[:, :], sqb[:, :],
                 reads=[R(ones), R(sqb)], writes=[R(ps_r)],
                 start=(k == 0), stop=(k == NT - 1), tag="nmm")
        e_copy(kb, nr[:, sl], ps_r[:, :], reads=[R(ps_r)], writes=[R(nr, c)],
               engine="act")
    e_act(kb, nr[:, :], nr[:, :], AF.Sqrt,
          reads=[R(nr, c) for c in range(4)], writes=[R(nr)], tag="nsqrt")
    e_ts(kb, nr[:, :], nr[:, :], 1e-5, ALU.max, reads=[R(nr)], writes=[R(nr)])
    e_recip(kb, nr[:, :], nr[:, :], reads=[R(nr)], writes=[R(nr)])
    kmT = [P.sbuf(f"kmT{k}", [128, 32], F32) for k in range(NT)]
    s6b = P.sbuf("s6b", [128, 512], F32)
    for c in range(4):
        sl = slice(c * 512, (c + 1) * 512)
        e_mm(kb, ps_b[:, :], ones_row[:, :], nr[:, sl],
             reads=[R(ones_row), R(nr)], writes=[R(ps_b)], tag="rnbc")
        for k in range(NT):
            e_tt(kb, s6b[:, :], Tf[k][:, sl], ps_b[:, :], ALU.mult,
                 reads=[R(Tf[k], c), R(ps_b)], writes=[R(s6b)], tag="s6")
            for j in range(8):
                e_red(kb, kmT[k][:, c * 8 + j:c * 8 + j + 1],
                      s6b[:, j * 64:(j + 1) * 64], ALU.add,
                      reads=[R(s6b)], writes=[R(kmT[k], c * 8 + j)], tag="kmred")
    for k in range(NT):
        e_ts(kb, kmT[k][:, :], kmT[k][:, :], 1.0 / 64, ALU.mult,
             reads=[R(kmT[k], j) for j in range(32)], writes=[R(kmT[k])])
    ident = P.sbuf("identS", [128, 128], F32)
    siota = P.sbuf("siotaS", [128, 128], F32)
    e_iota(kb, siota[:, :], [[1, 128]], writes=[R(siota)], channel_multiplier=-1)
    e_ts(kb, ident[:, :], siota[:, :], 0.0, ALU.is_equal, reads=[R(siota)],
         writes=[R(ident)], engine="pool")
    ps_t = P.psum("ps_tS", [32, 128], F32)
    kmr = P.sbuf("kmr", [32, D], F32)
    for k in range(NT):
        e_transpose(kb, ps_t[:, :], kmT[k][:, :], ident[:, :],
                    reads=[R(kmT[k]), R(ident)], writes=[R(ps_t)])
        e_copy(kb, kmr[:, k * 128:(k + 1) * 128], ps_t[:, :],
               reads=[R(ps_t)], writes=[R(kmr, k)], engine="act")
    kb.dma("sp", "stkm", kmout, kmr[:, :],
           reads=[R(kmr, k) for k in range(NT)], writes=[R("d_km")])
    sqk = P.sbuf("sqk", [128, 32], F32)
    ps_k = P.psum("ps_kS", [1, 32], F32)
    kkr = P.sbuf("kkr", [1, 32], F32)
    for k in range(NT):
        e_act(kb, sqk[:, :], kmT[k][:, :], AF.Square,
              reads=[R(kmT[k])], writes=[R(sqk)], tag="kksq")
        e_mm(kb, ps_k[:, :], ones[:, :], sqk[:, :],
             reads=[R(ones), R(sqk)], writes=[R(ps_k)],
             start=(k == 0), stop=(k == NT - 1), tag="kkmm")
    e_copy(kb, kkr[:, :], ps_k[:, :], reads=[R(ps_k)], writes=[R(kkr)],
           engine="act")
    ecol = load_cols(P, "etaw", ewin, D)
    acol = load_cols(P, "alphaw", awin, D)
    gcol = load_cols(P, "gatew", gwin, D)
    bias3 = P.sbuf("bias3", [1, 3], F32)
    kb.dma("sp", "ldb3", bias3[:, 0:1], ebin, reads=[R("d_b3")], writes=[R(bias3, 0)])
    kb.dma("sp", "ldb3b", bias3[:, 1:2], abin, reads=[R("d_b3")], writes=[R(bias3, 1)])
    kb.dma("sp", "ldb3c", bias3[:, 2:3], gbin, reads=[R("d_b3")], writes=[R(bias3, 2)])
    sca = P.sbuf("scab", [1, 5 * 32], F32)
    tmp32 = P.sbuf("tmp32", [1, 32], F32)
    g32 = P.sbuf("g32", [1, 32], F32)
    srow = P.sbuf("srowS", [1, TOK], F32)
    for name, col, brow, dst in (("eta", ecol, 0, 0), ("alpha", acol, 1, 1)):
        for c in range(4):
            sl = slice(c * 512, (c + 1) * 512)
            for k in range(NT):
                e_mm(kb, ps_r[:, :], col[:, k:k + 1], Tf[k][:, sl],
                     reads=[R(col), R(Tf[k], c)], writes=[R(ps_r)],
                     start=(k == 0), stop=(k == NT - 1), tag=f"{name}mm")
            e_act(kb, srow[:, sl], ps_r[:, :], AF.Sigmoid,
                  bias=bias3[:, brow:brow + 1],
                  reads=[R(ps_r), R(bias3, brow)], writes=[R(srow, c)],
                  tag=f"{name}sig")
        for ch in range(32):
            e_red(kb, sca[:, dst * 32 + ch:dst * 32 + ch + 1],
                  srow[:, ch * 64:(ch + 1) * 64], ALU.add,
                  reads=[R(srow, ch // 8)], writes=[R(sca, (dst, ch))],
                  tag=f"{name}red")
    for k in range(NT):
        e_mm(kb, ps_k[:, :], gcol[:, k:k + 1], kmT[k][:, :],
             reads=[R(gcol), R(kmT[k])], writes=[R(ps_k)],
             start=(k == 0), stop=(k == NT - 1), tag="gmm")
    e_act(kb, g32[:, :], ps_k[:, :], AF.Sigmoid, bias=bias3[:, 2:3],
          reads=[R(ps_k), R(bias3, 2)], writes=[R(g32)], tag="gsig")
    sca_r0 = [R(sca, (0, ch)) for ch in range(32)]
    sca_r1 = [R(sca, (1, ch)) for ch in range(32)]
    e_ts(kb, sca[:, 0:32], sca[:, 0:32], 0.2 / 64, ALU.mult,
         reads=sca_r0, writes=[R(sca, "e")])
    e_ts(kb, sca[:, 32:64], sca[:, 32:64], 0.5 / 64, ALU.mult, scalar2=0.5,
         op1=ALU.add, reads=sca_r1, writes=[R(sca, "al")])
    e_ts(kb, tmp32[:, :], sca[:, 32:64], 1.0, ALU.subtract,
         reads=[R(sca, "al")], writes=[R(tmp32)])
    e_tt(kb, tmp32[:, :], g32[:, :], tmp32[:, :], ALU.mult,
         reads=[R(g32), R(tmp32)], writes=[R(tmp32)])
    e_ts(kb, tmp32[:, :], tmp32[:, :], 1.0, ALU.add,
         reads=[R(tmp32)], writes=[R(tmp32)])
    e_tt(kb, sca[:, 32:64], g32[:, :], sca[:, 0:32], ALU.mult,
         reads=[R(g32), R(sca, "e")], writes=[R(sca, "u")])
    e_copy(kb, sca[:, 0:32], tmp32[:, :], reads=[R(tmp32)], writes=[R(sca, "a")])
    e_tt(kb, sca[:, 64:96], sca[:, 0:32], sca[:, 0:32], ALU.mult,
         reads=[R(sca, "a")], writes=[R(sca, "a2")])
    e_tt(kb, sca[:, 96:128], sca[:, 0:32], sca[:, 32:64], ALU.mult,
         reads=[R(sca, "a"), R(sca, "u")], writes=[R(sca, "au")])
    e_ts(kb, sca[:, 96:128], sca[:, 96:128], 2.0, ALU.mult,
         reads=[R(sca, "au")], writes=[R(sca, "au")])
    e_tt(kb, sca[:, 128:160], sca[:, 32:64], sca[:, 32:64], ALU.mult,
         reads=[R(sca, "u")], writes=[R(sca, "uu")])
    e_tt(kb, sca[:, 128:160], sca[:, 128:160], kkr[:, :], ALU.mult,
         reads=[R(sca, "uu"), R(kkr)], writes=[R(sca, "uu")])
    kb.dma("sp", "stsc", scout, sca[:, :],
           reads=[R(sca, "a"), R(sca, "u"), R(sca, "a2"), R(sca, "au"),
                  R(sca, "uu")], writes=[R("d_sc")])
    P.finalize()
    return P



def build_scan(NCH=128, stage=9):
    P = Prog("SCAN")
    kb = P.kb
    NCHF = 128
    t6in = P.inp("t6g", [D, S], BF16)
    kmrin = P.inp("kmr", [NCHF, D], F32)
    kmcin = P.inp("kmcT", [D, NCHF], F32)
    scin = P.inp("sca", [1, 5 * NCHF], F32)
    m0in = P.inp("m0T", [D, D], F32)
    w1in = P.inp("w1T", [D, D], F32)
    w2in = P.inp("w2T", [D, D], F32)
    oout = P.out("outsg", [D, S], BF16)

    cst = emit_consts(P)
    kb_ = kb
    Mst = [P.sbuf(f"mst{k}", [128, D], F32) for k in range(NT)]
    Mbf = [P.sbuf(f"mbf{k}", [128, D], BF16) for k in range(NT)]
    for k in range(NT):
        kb.dma("sp", "ldm0", Mst[k][:, :], m0in[k * 128:(k + 1) * 128, :],
               reads=[R("d_m0")], writes=[R(Mst[k])])
        e_copy(kb, Mbf[k][:, :], Mst[k][:, :], reads=[R(Mst[k])],
               writes=[R(Mbf[k])], engine="act")
    w1b, w1a = load_weight_fm(P, "w1T", w1in, D, D)
    w2b, w2a = load_weight_fm(P, "w2T", w2in, D, D)
    kmc = P.sbuf("kmc", [128, NT * NCHF], F32)
    kb.dma("sp", "ldkmc",
           kmc[:, :].rearrange("p (k c) -> p k c", c=NCHF),
           kmcin.rearrange("(k p) c -> p k c", p=128),
           reads=[R("d_kmc")], writes=[R(kmc)], serialize=False)
    scar = P.sbuf("scar", [1, 5 * NCHF], F32)
    kb.dma("sp", "ldsc", scar[:, :], scin, reads=[R("d_sc")], writes=[R(scar)])
    sqm = P.sbuf("sqmS", [128, 512], F32)
    rcol = P.sbuf("rcolS", [128, 8], F32)
    ps_f = P.psum("pdS", [1, 8], F32)
    Ft = P.sbuf("Ft", [1, 8], F32)
    for k in range(NT):
        for c in range(2):
            e_act(kb, sqm[:, :], Mst[k][:, c * 512:(c + 1) * 512], AF.Square,
                  reads=[R(Mst[k])], writes=[R(sqm)], tag="f0sq")
            e_red(kb, rcol[:, k:k + 1] if c == 0 else rcol[:, k:k + 1],
                  sqm[:, :], ALU.add, reads=[R(sqm)],
                  writes=[R(rcol, (k, c))], tag="f0red")
            if c == 0:
                e_copy(kb, sqm[:, 0:1], rcol[:, k:k + 1],
                       reads=[R(rcol, (k, 0))], writes=[R(sqm, "t")])
            else:
                e_tt(kb, rcol[:, k:k + 1], rcol[:, k:k + 1], sqm[:, 0:1],
                     ALU.add, reads=[R(rcol, (k, 1)), R(sqm, "t")],
                     writes=[R(rcol, (k, "s"))])
    e_mm(kb, ps_f[:, :], cst["ones_col_f32"][:, :], rcol[:, :],
         reads=[R(cst["ones_col_f32"])] + [R(rcol, (k, "s")) for k in range(NT)],
         writes=[R(ps_f)], tag="f0mm")
    e_copy(kb, Ft[:, :], ps_f[:, :], reads=[R(ps_f)], writes=[R(Ft, "i")],
           engine="act")
    e_red(kb, Ft[:, 0:1], Ft[:, :], ALU.add, reads=[R(Ft, "i")],
          writes=[R(Ft, 0)], tag="f0sum")

    errd = P.dram("errscratch", [8, 128], F32)
    cb = [P.sbuf(f"cbS{i}", [128, NT * 64], BF16) for i in range(2)]
    krow = [P.sbuf(f"krowS{i}", [1, D], F32) for i in range(2)]
    po = P.psum("poS", [128, 512], F32)
    ph = P.psum("phS", [128, 512], F32)
    pvk = P.psum("pvkS", [128, 16], F32)
    pv = pvk[:, 8:16]
    pk = pvk[:, 0:8]
    pd = P.bufs["pdS"][0:1, 0:2]
    pout = [P.psum(f"poutS{i}", [128, 512], F32) for i in range(2)]
    ps_bc1 = P.pshare([128, 512], 0)
    ob = P.sbuf("obS", [128, 512], BF16)
    hb = P.sbuf("hbS", [128, 512], BF16)
    omc = P.sbuf("omcS", [128, 8], F32)
    hmcf = P.sbuf("hmcfS", [128, 8], F32)
    hmc = P.sbuf("hmcS", [128, 8], BF16)
    kcb = P.sbuf("kcbS", [128, 8], BF16)
    Mkc = P.sbuf("MkcS", [128, 16], F32)
    se8 = P.sbuf("se8S", [8, 128], F32)
    erow = P.sbuf("erowS", [1, D], F32)
    erb = P.sbuf("erbS", [1, D], BF16)
    krb = P.sbuf("krbS", [1, D], BF16)
    sac = P.sbuf("sacS", [128, 1], F32)
    w9 = P.sbuf("w9S", [1, 12], F32)
    identS = P.sbuf("identS", [128, 128], F32)
    siotaS = P.sbuf("siotaS", [128, 128], F32)
    pse = P.psum("pseS", [8, 128], F32)
    e_iota(kb, siotaS[:, :], [[1, 128]], writes=[R(siotaS)], channel_multiplier=-1)
    e_ts(kb, identS[:, :], siotaS[:, :], 0.0, ALU.is_equal, reads=[R(siotaS)],
         writes=[R(identS)], engine="pool")

    t6r = t6in.rearrange("(k p) s -> p k s", p=128)
    oor = oout.rearrange("(k p) s -> p k s", p=128)

    def loadstep(t):
        b = cb[t % 2]
        kb.dma("act", f"ldcb{t % 2}",
               b[:, :].rearrange("p (k c) -> p k c", c=64),
               t6r[:, :, t * 64:(t + 1) * 64],
               reads=[R("d_t6")], writes=[R(b)], serialize=False, tag="ldcb")
        kb.dma("act", f"ldkr{t % 2}", krow[t % 2][:, :],
               kmrin[t:t + 1, :], reads=[R("d_kmr")],
               writes=[R(krow[t % 2])], serialize=False, tag="ldkr")

    loadstep(0)
    for t in range(NCH):
        b = cb[t % 2]
        kr = krow[t % 2]
        if t + 1 < NCH:
            loadstep(t + 1)
        for o in range(NT):
            for k in range(NT):
                e_mm(kb, po[:, o * 64:(o + 1) * 64],
                     Mbf[k][:, o * 128:(o + 1) * 128],
                     b[:, k * 64:(k + 1) * 64],
                     reads=[R(Mbf[k]), R(b)], writes=[R(po)],
                     start=(o == 0 and k == 0), stop=False, tag="outmm")
        e_copy(kb, ob[:, :], po[:, :], reads=[R(po)], writes=[R(ob)],
               engine="act")
        kb.dma("sp", "stout", oor[:, :, t * 64:(t + 1) * 64],
               ob[:, :].rearrange("p (k c) -> p k c", c=64),
               reads=[R(ob)], writes=[R("d_og", t)], serialize=False,
               tag="stout")
        if stage < 2:
            continue
        for o in range(NT):
            e_red(kb, omc[:, o:o + 1], po[:, o * 64:(o + 1) * 64], ALU.add,
                  reads=[R(po)], writes=[R(omc, o)], tag="omred")
        e_ts(kb, omc[:, :], omc[:, :], 1.0 / 64, ALU.mult,
             reads=[R(omc, o) for o in range(NT)], writes=[R(omc)])
        if stage < 2.7:
            continue
        for o2 in range(NT):
            for k in range(NT):
                e_mm(kb, ph[:, o2 * 64:(o2 + 1) * 64],
                     w1a(k)[:, o2 * 128:(o2 + 1) * 128],
                     ob[:, k * 64:(k + 1) * 64],
                     reads=[R("s_w1T", k), R(ob)], writes=[R(ph)],
                     start=(o2 == 0 and k == 0), stop=False, tag="vgmm")
        e_act(kb, hb[:, :], ph[:, :], AF.Silu, reads=[R(ph)], writes=[R(hb)],
              tag="vgsilu")
        for o2 in range(NT):
            e_red(kb, hmcf[:, o2:o2 + 1], hb[:, o2 * 64:(o2 + 1) * 64],
                  ALU.add, reads=[R(hb)], writes=[R(hmcf, o2)], tag="hmred")
        e_ts(kb, hmc[:, :], hmcf[:, :], 1.0 / 64, ALU.mult,
             reads=[R(hmcf, o2) for o2 in range(NT)], writes=[R(hmc)])
        if stage < 3:
            continue
        for o2 in range(NT):
            for j in range(NT):
                e_mm(kb, pvk[:, 8 + o2:9 + o2],
                     w2a(j)[:, o2 * 128:(o2 + 1) * 128], hmc[:, j:j + 1],
                     reads=[R("s_w2T", j), R(hmc)], writes=[R("pvkS")],
                     start=(o2 == 0 and j == 0), stop=False, tag="v2mm")
        e_copy(kb, kcb[:, :], kmc[:, :].rearrange("p (k c) -> p k c", c=NCHF)[:, :, t],
               reads=[R(kmc)], writes=[R(kcb)], tag="kcb")
        for o in range(NT):
            for i in range(NT):
                e_mm(kb, pvk[:, o:o + 1],
                     Mbf[i][:, o * 128:(o + 1) * 128], kcb[:, i:i + 1],
                     reads=[R(Mbf[i]), R(kcb)], writes=[R("pvkS")],
                     start=(o == 0 and i == 0), stop=False, tag="mkmm")
        e_copy(kb, Mkc[:, 0:8], pk, reads=[R("pvkS")], writes=[R(Mkc, "mk")],
               engine="act")
        e_tt(kb, Mkc[:, 8:16], pv, omc[:, :], ALU.add,
             reads=[R("pvkS"), R(omc)], writes=[R(Mkc, "er")])
        e_tt(kb, Mkc[:, 8:16], Mkc[:, 8:16], Mkc[:, 0:8], ALU.subtract,
             reads=[R(Mkc, "er"), R(Mkc, "mk")], writes=[R(Mkc, "er")])
        for i in range(NT):
            e_mm(kb, pd, Mkc[:, 8 + i:9 + i],
                 Mkc[:, i:17 + i:8][:, 0:2] if False else Mkc[:, i:i + 9:8],
                 reads=[R(Mkc, "mk"), R(Mkc, "er")], writes=[R("pdS")],
                 start=(i == 0), stop=(i == NT - 1), tag="dotmm")
        e_copy(kb, Ft[:, 1:3], pd, reads=[R("pdS")], writes=[R(Ft, "d")],
               engine="act")
        if stage < 4:
            continue
        e_tt(kb, w9[:, 0:1], scar[:, 2 * NCHF + t:2 * NCHF + t + 1], Ft[:, 0:1],
             ALU.mult, reads=[R(scar), R(Ft, 0)], writes=[R(w9, 0)])
        e_tt(kb, w9[:, 1:2], scar[:, 3 * NCHF + t:3 * NCHF + t + 1], Ft[:, 1:2],
             ALU.mult, reads=[R(scar), R(Ft, "d")], writes=[R(w9, 1)])
        e_tt(kb, w9[:, 2:3], scar[:, 4 * NCHF + t:4 * NCHF + t + 1], Ft[:, 2:3],
             ALU.mult, reads=[R(scar), R(Ft, "d")], writes=[R(w9, 2)])
        e_tt(kb, w9[:, 0:1], w9[:, 0:1], w9[:, 1:2], ALU.add,
             reads=[R(w9, 0), R(w9, 1)], writes=[R(w9, 0)])
        e_tt(kb, w9[:, 0:1], w9[:, 0:1], w9[:, 2:3], ALU.add,
             reads=[R(w9, 0), R(w9, 2)], writes=[R(w9, 0)])
        e_act(kb, w9[:, 3:4], w9[:, 0:1], AF.Sqrt, reads=[R(w9, 0)],
              writes=[R(w9, 3)], tag="ssqrt")
        e_ts(kb, w9[:, 3:4], w9[:, 3:4], 1e-6, ALU.add, reads=[R(w9, 3)],
             writes=[R(w9, 3)])
        e_recip(kb, w9[:, 3:4], w9[:, 3:4], reads=[R(w9, 3)], writes=[R(w9, 3)])
        e_ts(kb, w9[:, 3:4], w9[:, 3:4], 30.0, ALU.mult, scalar2=1.0,
             op1=ALU.min, reads=[R(w9, 3)], writes=[R(w9, 3)])
        e_tt(kb, w9[:, 4:5], w9[:, 3:4], w9[:, 3:4], ALU.mult,
             reads=[R(w9, 3)], writes=[R(w9, 4)])
        e_tt(kb, Ft[:, 0:1], w9[:, 4:5], w9[:, 0:1], ALU.mult,
             reads=[R(w9, 4), R(w9, 0)], writes=[R(Ft, 0)])
        e_tt(kb, w9[:, 5:6], w9[:, 3:4], scar[:, t:t + 1], ALU.mult,
             reads=[R(w9, 3), R(scar)], writes=[R(w9, 5)])
        e_tt(kb, w9[:, 6:7], w9[:, 3:4], scar[:, NCHF + t:NCHF + t + 1],
             ALU.mult, reads=[R(w9, 3), R(scar)], writes=[R(w9, 6)])
        e_mm(kb, ps_bc1[:, 0:1], cst["ones_row_f32"][:, :], w9[:, 5:6],
             reads=[R(cst["ones_row_f32"]), R(w9, 5)], writes=[R(ps_bc1)],
             tag="sabc")
        e_copy(kb, sac[:, :], ps_bc1[:, 0:1], reads=[R(ps_bc1)],
               writes=[R(sac)], engine="act")
        if stage < 5:
            continue
        e_transpose(kb, pse[:, :], Mkc[:, 8:16], identS[:, :],
                    reads=[R(Mkc, "er"), R(identS)], writes=[R(pse)])
        e_copy(kb, se8[:, :], pse[:, :], reads=[R(pse)], writes=[R(se8)],
               engine="dve")
        kb.dma("sp", "errio", errd[:, :], se8[:, :], reads=[R(se8)],
               writes=[R("d_err")], tag="errw")
        kb.dma("sp", "errio", erow[:, :],
               errd[:, :].rearrange("a b -> (a b)"),
               reads=[R("d_err")], writes=[R(erow)], tag="errr")
        e_ts(kb, erb[:, :], erow[:, :], w9[:, 6:7], ALU.mult,
             reads=[R(erow), R(w9, 6)], writes=[R(erb)])
        e_copy(kb, krb[:, :], kr[:, :], reads=[R(kr)], writes=[R(krb)])
        if stage < 6:
            continue
        for i in range(NT):
            for ch in range(2):
                pp = pout[(i * 2 + ch) % 2]
                e_mm(kb, pp[:, :], krb[0:1, i * 128:(i + 1) * 128],
                     erb[0:1, ch * 512:(ch + 1) * 512],
                     reads=[R(krb), R(erb)], writes=[R(pp)], tag="outer")
                e_stt(kb, Mst[i][:, ch * 512:(ch + 1) * 512],
                      Mst[i][:, ch * 512:(ch + 1) * 512], sac[:, :], pp[:, :],
                      ALU.mult, ALU.add,
                      reads=[R(Mst[i]), R(sac), R(pp)], writes=[R(Mst[i])],
                      tag="mupd")
                e_copy(kb, Mbf[i][:, ch * 512:(ch + 1) * 512],
                       Mst[i][:, ch * 512:(ch + 1) * 512],
                       reads=[R(Mst[i])], writes=[R(Mbf[i])],
                       engine="pool" if ch == 0 else "dve", tag="mcast")
    P.finalize()
    return P



def build_vg():
    P = Prog("VG")
    kb = P.kb
    xin = P.inp("xfm", [D, TOK], F32)
    oin = P.inp("outso", [D, TOK], BF16)
    w1in = P.inp("w1T", [D, D], F32)
    w2in = P.inp("w2T", [D, D], F32)
    xout = P.out("x7fm", [D, TOK], F32)
    cst = emit_consts(P)
    X = [P.sbuf(f"x{k}", [128, TOK], F32) for k in range(NT)]
    O = [P.sbuf(f"ob{k}", [128, TOK], BF16) for k in range(NT)]
    Hh = [P.sbuf(f"hh{k}", [128, TOK], BF16) for k in range(NT)]
    for k in range(NT):
        kb.dma("sp", f"ldx{k % 2}", X[k][:, :], xin[k * 128:(k + 1) * 128, :],
               reads=[R("dram_x")], writes=[R(X[k])])
        kb.dma("sp", f"ldo{k % 2}", O[k][:, :], oin[k * 128:(k + 1) * 128, :],
               reads=[R("dram_o")], writes=[R(O[k])])
    w1b, w1a = load_weight_fm(P, "w1T", w1in, D, D)
    w2b, w2a = load_weight_fm(P, "w2T", w2in, D, D)
    ps = [P.pshare([128, 512], i) for i in range(2)]
    for o in range(NT):
        for c in range(4):
            p = ps[(o * 4 + c) % 2]
            sl = slice(c * 512, (c + 1) * 512)
            for k in range(NT):
                e_mm(kb, p[:, :], w1a(k)[:, o * 128:(o + 1) * 128], O[k][:, sl],
                     reads=[R("s_w1T", k), R(O[k])], writes=[R(p)],
                     start=(k == 0), stop=(k == NT - 1), tag="vg1")
            e_act(kb, Hh[o][:, sl], p[:, :], AF.Silu, reads=[R(p)],
                  writes=[R(Hh[o], c)], tag="vgsi")
    for o in range(NT):
        for c in range(4):
            p = ps[(o * 4 + c) % 2]
            sl = slice(c * 512, (c + 1) * 512)
            for k in range(NT):
                e_mm(kb, p[:, :], w2a(k)[:, o * 128:(o + 1) * 128],
                     Hh[k][:, sl],
                     reads=[R("s_w2T", k), R(Hh[k], c)], writes=[R(p)],
                     start=(k == 0), stop=(k == NT - 1), tag="vg2")
            e_tt(kb, X[o][:, sl], X[o][:, sl], p[:, :], ALU.add,
                 reads=[R(X[o]), R(p)], writes=[R(X[o])], tag="vgacc")
        e_tt(kb, X[o][:, :], X[o][:, :], O[o][:, :], ALU.add,
             reads=[R(X[o]), R(O[o])], writes=[R(X[o])], tag="vgres")
        kb.dma("sp", f"stx{o % 2}", xout[o * 128:(o + 1) * 128, :], X[o][:, :],
               reads=[R(X[o])], writes=[R("d_xo", o)])
    P.finalize()
    return P
